# version 03
# baseline (speedup 1.0000x reference)
import numpy as np
import ml_dtypes
from contextlib import ExitStack

import concourse.bass as bass
import concourse.mybir as mybir
from concourse.bass_utils import run_bass_kernel_spmd

F32 = mybir.dt.float32
BF16 = mybir.dt.bfloat16
I32 = mybir.dt.int32
AF = mybir.ActivationFunctionType
ALU = mybir.AluOpType

NCORES = 8
D = 1024
TT = 16384
TL = TT // NCORES
KC = 8
F4 = 4096
MC = 32
NQ = 4
QT = TL // NQ
NTG = TL // 128
NP = 8
EPS = 1e-5
ROPE_BASE = 500000.0
QSCALE = 64.0 ** -0.5

ENGS = ("sync", "gpsimd", "tensor", "vector", "scalar")


class Plan:

    def __init__(self):
        self.streams = {e: [] for e in ENGS}
        self.cnt = {}
        self.handles = {}

    def new_sem(self, name):
        self.cnt[name] = 0

    def emit(self, eng, fn):
        self.streams[eng].append(fn)

    def wait(self, eng, sem, val=None):
        v = self.cnt[sem] if val is None else val
        if v is None or v <= 0:
            return
        self.emit(eng, lambda E, s=sem, v=v: E.wait_ge(self.handles[s], v))

    def op(self, eng, fn, sem=None, k=1):
        if sem is None:
            self.emit(eng, fn)
            return None
        self.cnt[sem] += k
        self.emit(eng, lambda E, fn=fn, s=sem, k=k: fn(E).then_inc(self.handles[s], k))
        return self.cnt[sem]

    def dma(self, eng, out, in_, sem):
        return self.op(eng, lambda E, o=out, i=in_: E.dma_start(out=o, in_=i), sem, 16)


def build_program():
    nc = bass.Bass("TRN2", target_bir_lowering=False)
    P = Plan()
    for s in ("sd", "gd", "t", "v", "s", "cc", "c0", "x0",
              "w1s0", "w1s1", "w2s0", "w2s1", "qa0", "qa1", "wb0", "wb1",
              "p0", "p1", "wos0", "wos1", "cc2"):
        P.new_sem(s)

    x_fm = nc.dram_tensor("x_fm", [D, TL], F32, kind="ExternalInput")
    wqkv = nc.dram_tensor("wqkv", [D, 3 * D], F32, kind="ExternalInput")
    wout = nc.dram_tensor("wout", [D, D], F32, kind="ExternalInput")
    gamT = nc.dram_tensor("gamT", [128, KC], F32, kind="ExternalInput")
    betT = nc.dram_tensor("betT", [128, KC], F32, kind="ExternalInput")
    b1T = nc.dram_tensor("b1T", [128, MC], F32, kind="ExternalInput")
    b2T = nc.dram_tensor("b2T", [128, KC], F32, kind="ExternalInput")
    q1 = nc.dram_tensor("q1", [D, F4], I32, kind="ExternalInput")
    s1 = nc.dram_tensor("s1", [KC, F4], F32, kind="ExternalInput")
    q2 = nc.dram_tensor("q2", [F4, D], I32, kind="ExternalInput")
    s2 = nc.dram_tensor("s2", [MC, D], F32, kind="ExternalInput")
    ropc = nc.dram_tensor("ropc", [128, NTG * 64], BF16, kind="ExternalInput")
    rops = nc.dram_tensor("rops", [128, NTG * 64], BF16, kind="ExternalInput")
    identin = nc.dram_tensor("identin", [128, 128], BF16, kind="ExternalInput")
    selin = nc.dram_tensor("selin", [16, 1024], BF16, kind="ExternalInput")
    out_fm = nc.dram_tensor("out_fm", [D, TL], F32, kind="ExternalOutput")

    w1c = nc.dram_tensor("w1c", [F4, D], BF16, kind="Internal")
    w2c = nc.dram_tensor("w2c", [D, F4], BF16, kind="Internal")
    cc_in = nc.dram_tensor("cc_in", [128, 1032], BF16, kind="Internal")
    cc_out = nc.dram_tensor("cc_out", [128, 1032], BF16, kind="Internal")

    es = ExitStack()
    sb = lambda name, shape, dt: es.enter_context(nc.sbuf_tensor(name, shape, dt))
    ps = lambda name, shape, dt: es.enter_context(nc.psum_tensor(name, shape, dt))

    X = sb("X", [128, KC * TL], F32)
    arena = sb("arena", [128, 32768], BF16)
    wbuf = sb("wbuf", [128, 18432], BF16)
    kft = [sb("kft0", [128, 1024], BF16)]
    vt = [sb("vt0", [128, 1024], BF16)]
    rtile = sb("rtile", [128, 1024], BF16)
    rcos = sb("rcos", [128, 1024], BF16)
    rsin = sb("rsin", [128, 1024], BF16)
    denr = sb("denr", [16, QT], BF16)
    dmax = sb("dmax", [16, QT], F32)
    kvx = sb("kvx", [128, 1032], BF16)
    ksb = sb("ksb", [128, 128], BF16)
    sel = sb("sel", [16, 1024], BF16)
    mir = [sb("mir0", [128, QT], BF16)]
    sqt = [sb("sqt0", [128, QT], BF16)]
    tb1 = sb("tb1", [128, QT], BF16)
    tb2 = sb("tb2", [128, QT], BF16)
    z4t = sb("z4t", [128, QT], F32)
    rows = sb("rows", [1, 5 * QT], F32)
    sgam = sb("sgam", [128, KC], F32)
    sbet = sb("sbet", [128, KC], F32)
    sb1t = sb("sb1t", [128, MC], F32)
    sqb2 = sb("sqb2", [128, KC], F32)
    tabc = sb("tabc", [128, NTG * 64], BF16)
    tabs = sb("tabs", [128, NTG * 64], BF16)
    ident = sb("ident", [128, 128], BF16)
    ones_c = sb("ones_c", [128, 1], BF16)
    ones_r = sb("ones_r", [1, 128], F32)

    R_M, R_C, R_A1, R_P, R_Q = (rows[0:1, i * QT:(i + 1) * QT] for i in range(5))
    R_TS = R_C

    pB0 = ps("pB0", [128, 1024], F32)
    pB1 = ps("pB1", [128, 1024], F32)
    pSS = ps("pSS", [128, 512], F32)
    pM = ps("pM", [128, 512], F32)
    pC = ps("pC", [128, 512], F32)
    pTR = ps("pTR", [128, 1024], BF16)

    def Xv(k, c0, n):
        return X[:, k * TL + c0: k * TL + c0 + n]

    def Av(m):
        return arena[:, m * QT:(m + 1) * QT]

    def Hq(k):
        return arena[:, 16384 + k * QT: 16384 + (k + 1) * QT]

    def QF(p, c0, n):
        return arena[:, p * TL + c0: p * TL + c0 + n]

    def Y1(k, c0, n):
        return arena[:, 16384 + k * TL + c0: 16384 + k * TL + c0 + n]

    def W1S(buf):
        return wbuf[:, buf * 1024:(buf + 1) * 1024]

    def W2S(buf):
        return wbuf[:, 2048 + buf * 4096: 2048 + (buf + 1) * 4096]

    DQB = 10240
    def QA(buf):
        return wbuf[:, DQB + buf * 2048: DQB + (buf + 1) * 2048]

    def SC(buf):
        return wbuf[:, DQB + 4096 + buf * 2048: DQB + 4096 + (buf + 1) * 2048]

    def PART(slot):
        return wbuf[:, 2048 + slot * 8192: 2048 + (slot + 1) * 8192]

    G = {}

    def pb_wait(buf):
        P.wait("tensor", "v", G.get(("pbv", buf), 0))
        P.wait("tensor", "s", G.get(("pbs", buf), 0))

    def pb_used(buf, v=None, s=None):
        if v is not None:
            G[("pbv", buf)] = v
        if s is not None:
            G[("pbs", buf)] = s

    P.dma("gpsimd", sgam[:, :], gamT.ap(), "c0")
    P.dma("gpsimd", sbet[:, :], betT.ap(), "c0")
    P.dma("gpsimd", sb1t[:, :], b1T.ap(), "c0")
    P.dma("gpsimd", sqb2[:, :], b2T.ap(), "c0")
    P.dma("gpsimd", tabc[:, :], ropc.ap(), "c0")
    P.dma("gpsimd", tabs[:, :], rops.ap(), "c0")
    P.dma("gpsimd", ident[:, :], identin.ap(), "c0")
    P.dma("gpsimd", sel[:, :], selin.ap(), "c0")
    const_ld = P.cnt["c0"]

    xld = [P.dma("gpsimd", Xv(k, 0, TL), x_fm.ap()[k * 128:(k + 1) * 128, :], "x0")
           for k in range(KC)]

    P.op("vector", lambda E: E.memset(ones_c[:, :], 1.0), "v")
    P.op("vector", lambda E: E.memset(ones_r[:, :], 1.0), "v")
    P.wait("scalar", "c0", const_ld)
    P.op("scalar", lambda E: E.mul(sqb2[:, :], sqb2[:, :], 0.25), "s")

    dq_prev = {}

    def emit_dequant(qsrc, ssrc, cache, nk, ncols):
        last = None
        cstride = cache.shape[1]
        for g in range(nk):
            for h in range(2):
                buf = h
                cw = ncols // 2
                c0 = h * cw
                mw = cw // 128
                qs = "qa0" if buf == 0 else "qa1"
                ws = "wb0" if buf == 0 else "wb1"
                pv = dq_prev.get(buf)
                if pv is not None:
                    P.wait("gpsimd", ws, pv)
                sv = dq_prev.get(("v", buf))
                if sv is not None:
                    P.wait("gpsimd", "v", sv)
                P.dma("gpsimd", QA(buf)[:, 0:cw],
                      qsrc.ap()[g * 128:(g + 1) * 128, c0:c0 + cw], qs)
                bc = bass.AP(ssrc, g * ncols + c0, [[0, 128], [1, cw]])
                lsc = P.dma("gpsimd", SC(buf)[:, 0:cw], bc, qs)
                P.wait("vector", qs, lsc)
                wv = P.op("vector", lambda E, b=buf, w=cw: E.scalar_tensor_tensor(
                    QA(b)[:, 0:w], QA(b)[:, 0:w], -128.0, SC(b)[:, 0:w],
                    ALU.add, ALU.mult), "v")
                P.wait("sync", "v", wv)
                outap = bass.AP(cache, g * 128 + (c0 // 128) * 128 * cstride,
                                [[cstride, 128], [128 * cstride, mw], [1, 128]])
                inap = bass.AP(wbuf, DQB + buf * 2048, [[18432, 128], [128, mw], [1, 128]])
                last = P.dma("sync", outap, inap, ws)
                dq_prev[buf] = last
                dq_prev[("v", buf)] = wv
        return last

    emit_dequant(q1, s1, w1c, KC, F4)
    dq_wb_final = (P.cnt["wb0"], P.cnt["wb1"])

    def emit_ln_quarter(t0, mode, dst):
        for k in range(KC):
            b = 0
            P.wait("vector", "x0", xld[-1])
            pv = G.get(("mir", b))
            if pv is not None:
                P.wait("vector", "t", pv)
            P.op("vector", lambda E, k=k, b=b, t0=t0: E.tensor_copy(
                mir[b][:, :], Xv(k, t0, QT)), "v")
            mv = P.op("vector", lambda E, b=b: E.tensor_mul(
                sqt[b][:, :], mir[b][:, :], mir[b][:, :]), "v")
            P.wait("tensor", "v", mv)
            if k == 0:
                P.wait("tensor", "v", G.get(("pss_rd_v"), 0))
                P.wait("tensor", "s", G.get(("pss_rd_s"), 0))
            P.op("tensor", lambda E, b=b, k=k: E.matmul(
                pSS[0:1, :], ones_c[:, :], mir[b][:, :],
                start=(k == 0), stop=(k == KC - 1)), "t")
            tm = P.op("tensor", lambda E, b=b, k=k: E.matmul(
                pSS[32:33, :], ones_c[:, :], sqt[b][:, :],
                start=(k == 0), stop=(k == KC - 1), tile_position=(0, 32)), "t")
            G[("mir", b)] = tm
        P.wait("vector", "t", None)
        P.op("vector", lambda E: E.tensor_scalar_mul(R_M, pSS[0:1, :], -1.0 / D), "v")
        P.op("vector", lambda E: E.tensor_mul(R_TS, R_M, R_M), "v")
        P.op("vector", lambda E: E.scalar_tensor_tensor(
            R_TS, pSS[32:33, :], 1.0 / D, R_TS, ALU.mult, ALU.subtract), "v")
        G[("pss_rd_v")] = P.cnt["v"]
        cv = P.op("vector", lambda E: E.tensor_scalar(
            R_A1, R_TS, EPS, None, ALU.add), "v")
        P.wait("scalar", "v", cv)
        P.op("scalar", lambda E: E.activation(out=R_A1, in_=R_A1, func=AF.Ln), "s")
        ssn = P.op("scalar", lambda E: E.activation(out=R_A1, in_=R_A1, func=AF.Exp, scale=-0.5), "s")
        if mode == "ffn":
            cv2 = P.op("vector", lambda E: E.tensor_scalar(
                R_C, R_TS, 1.0 + EPS, EPS * EPS, ALU.mult, ALU.add), "v")
            P.wait("scalar", "v", cv2)
            P.op("scalar", lambda E: E.activation(out=R_C, in_=R_C, func=AF.Ln), "s")
            ssn = P.op("scalar", lambda E: E.activation(out=R_C, in_=R_C, func=AF.Exp, scale=-0.5), "s")
            P.wait("vector", "s", ssn)
            P.op("vector", lambda E: E.tensor_scalar(R_P, R_A1, 0.5, 1.0, ALU.mult, ALU.add), "v")
            P.op("vector", lambda E: E.scalar_tensor_tensor(R_Q, R_M, 0.5, R_A1, ALU.mult, ALU.mult), "v")
        rows_v, rows_s = P.cnt["v"], P.cnt["s"]
        P.wait("tensor", "v", rows_v)
        P.wait("tensor", "s", rows_s)
        P.wait("tensor", "v", G.get(("pmc_rd_v"), 0))
        P.wait("tensor", "s", G.get(("pmc_rd_s"), 0))
        P.op("tensor", lambda E: E.matmul(pM[:, :], ones_r[:, :], R_M, start=True, stop=True), "t")
        src2 = R_C if mode == "ffn" else R_A1
        bt = P.op("tensor", lambda E, s2=src2: E.matmul(pC[:, :], ones_r[:, :], s2, start=True, stop=True), "t")
        P.wait("vector", "t", bt)
        P.wait("vector", "t", G.get(("dst_t",), 0))
        P.wait("scalar", "t", G.get(("dst_t",), 0))
        for k in range(KC):
            pv = G.get(("tb_s"))
            if pv is not None and mode == "ffn":
                P.wait("vector", "s", pv)
            P.op("vector", lambda E, k=k, t0=t0: E.tensor_add(
                tb1[:, :], Xv(k, t0, QT), pM[:, :]), "v")
            if mode == "ffn":
                tv = P.op("vector", lambda E: E.tensor_mul(tb2[:, :], tb1[:, :], pC[:, :]), "v")
                P.wait("scalar", "v", tv)
                smk = P.op("scalar", lambda E, k=k: E.activation(
                    out=dst(k), in_=tb2[:, :], func=AF.Identity,
                    bias=sbet[:, k:k + 1], scale=sgam[:, k:k + 1]), "s")
                G[("tb_s")] = smk
            else:
                P.op("vector", lambda E, k=k: E.tensor_mul(dst(k), tb1[:, :], pC[:, :]), "v")
        G[("pmc_rd_v")] = P.cnt["v"]
        G[("pmc_rd_s")] = P.cnt["s"]
        return P.cnt["v"], P.cnt["s"]

    def emit_ffn(first):
        for q in range(NQ):
            t0 = q * QT
            h_v, h_s = emit_ln_quarter(t0, "ffn", Hq)
            if first and q == 0:
                P.wait("sync", "wb0", dq_wb_final[0])
                P.wait("sync", "wb1", dq_wb_final[1])
            for m in range(MC):
                buf = m % 2
                g = G.get(("w1s", buf))
                if g is not None:
                    P.wait("sync", "t", g)
                ls = "w1s0" if buf == 0 else "w1s1"
                lw = P.dma("sync", W1S(buf)[:, :],
                           w1c.ap()[m * 128:(m + 1) * 128, :], ls)
                pb = pB0 if buf == 0 else pB1
                pb_wait(buf)
                P.wait("tensor", ls, lw)
                P.wait("tensor", "v", h_v)
                P.wait("tensor", "s", h_s)
                tmk = None
                for k in range(KC):
                    tmk = P.op("tensor", lambda E, pb=pb, buf=buf, k=k: E.matmul(
                        pb[:, 0:512], W1S(buf)[:, k * 128:(k + 1) * 128], Hq(k),
                        start=(k == 0), stop=(k == KC - 1)), "t")
                G[("w1s", buf)] = tmk
                P.wait("scalar", "t", tmk)
                smk = P.op("scalar", lambda E, pb=pb, m=m: E.activation(
                    out=Av(m), in_=pb[:, 0:512], func=AF.Silu,
                    bias=sb1t[:, m:m + 1], scale=1.0), "s")
                pb_used(buf, s=smk)
            G[("dst_t",)] = P.cnt["t"]
            a_ready_s = P.cnt["s"]
            if first and q == 0:
                emit_dequant(q2, s2, w2c, MC, D)
            pq_v = P.cnt["v"]
            P.wait("tensor", "v", pq_v)
            P.op("tensor", lambda E: E.matmul(pM[:, :], ones_r[:, :], R_P, start=True, stop=True), "t")
            pqb = P.op("tensor", lambda E: E.matmul(pC[:, :], ones_r[:, :], R_Q, start=True, stop=True), "t")
            G[("pmc_rd_v")] = 0
            G[("pmc_rd_s")] = 0
            for o in range(KC):
                buf = o % 2
                g = G.get(("w2s", buf))
                if g is not None:
                    P.wait("sync", "t", g)
                ls = "w2s0" if buf == 0 else "w2s1"
                lw = P.dma("sync", W2S(buf)[:, :],
                           w2c.ap()[o * 128:(o + 1) * 128, :], ls)
                pb = pB0 if buf == 0 else pB1
                pb_wait(buf)
                P.wait("tensor", ls, lw)
                P.wait("tensor", "s", a_ready_s)
                tmk = None
                for k in range(MC):
                    tmk = P.op("tensor", lambda E, pb=pb, buf=buf, k=k: E.matmul(
                        pb[:, 0:512], W2S(buf)[:, k * 128:(k + 1) * 128], Av(k),
                        start=(k == 0), stop=(k == MC - 1)), "t")
                G[("w2s", buf)] = tmk
                zg = G.get(("z4_v"))
                if zg is not None:
                    P.wait("scalar", "v", zg)
                P.wait("scalar", "t", tmk)
                zs = P.op("scalar", lambda E, pb=pb, o=o: E.activation(
                    out=z4t[:, :], in_=pb[:, 0:512], func=AF.Identity,
                    bias=sqb2[:, o:o + 1], scale=0.25), "s")
                pb_used(buf, s=zs)
                P.wait("vector", "t", pqb)
                P.wait("vector", "s", zs)
                P.op("vector", lambda E, o=o, t0=t0: E.tensor_mul(
                    Xv(o, t0, QT), Xv(o, t0, QT), pM[:, :]), "v")
                zv = P.op("vector", lambda E, o=o, t0=t0: E.tensor_add(
                    Xv(o, t0, QT), Xv(o, t0, QT), z4t[:, :]), "v")
                G[("z4_v")] = zv
                vmk = P.op("vector", lambda E, o=o, t0=t0: E.tensor_add(
                    Xv(o, t0, QT), Xv(o, t0, QT), pC[:, :]), "v")
                G[("pmc_rd_v")] = vmk
                if not first:
                    P.wait("gpsimd", "v", vmk)
                    P.dma("gpsimd", out_fm.ap()[o * 128:(o + 1) * 128, t0:t0 + QT],
                          Xv(o, t0, QT), "gd")

    def emit_attn():
        ffn1_t = P.cnt["t"]

        y1_marks = []
        for tc in range(NQ):
            y1_marks.append(emit_ln_quarter(
                tc * QT, "attn", lambda k, tc=tc: Y1(k, tc * QT, QT)))

        P.wait("gpsimd", "t", ffn1_t)
        P.wait("gpsimd", "wb0", None)
        P.wait("gpsimd", "wb1", None)
        lq = P.dma("gpsimd", PART(0)[:, :].rearrange("p (k n) -> p k n", k=KC),
                   wqkv.ap()[:, 0:1024].rearrange("(k p) n -> p k n", p=128), "p0")

        def qkv_mms(part_slot, tg, pbank, pmark):
            mvv, mss = y1_marks[tg // 4]
            P.wait("tensor", "v", mvv)
            P.wait("tensor", pmark[0], pmark[1])
            tmk = None
            for h in range(2):
                for k in range(KC):
                    tmk = P.op("tensor", lambda E, pbank=pbank, k=k, h=h, tg=tg,
                               ps=part_slot: E.matmul(
                        pbank[:, h * 512:(h + 1) * 512],
                        Y1(k, (tg // 4) * QT + (tg % 4) * 128, 128),
                        PART(ps)[:, k * 1024 + h * 512: k * 1024 + (h + 1) * 512],
                        start=(k == 0), stop=(k == KC - 1)), "t")
            return tmk

        def rope_elu(tg, pbank, scale_q):
            sc = P.op("scalar", lambda E, pb=pbank: E.activation(
                out=rtile[:, :], in_=pb[:, :], func=AF.Identity), "s")
            P.wait("vector", "s", sc)
            P.op("vector", lambda E, tg=tg: E.tensor_tensor(
                rcos[:, :].rearrange("p (g j) -> p g j", g=16),
                rtile[:, :].rearrange("p (g j) -> p g j", g=16),
                bass.AP(tabc, tg * 64, [[NTG * 64, 128], [0, 16], [1, 64]]),
                ALU.mult), "v")
            P.op("vector", lambda E, tg=tg: E.tensor_tensor(
                bass.AP(rsin, 0, [[1024, 128], [64, 16], [1, 32]]),
                bass.AP(rtile, 32, [[1024, 128], [64, 16], [1, 32]]),
                bass.AP(tabs, tg * 64, [[NTG * 64, 128], [0, 16], [1, 32]]),
                ALU.mult), "v")
            P.op("vector", lambda E, tg=tg: E.tensor_tensor(
                bass.AP(rsin, 32, [[1024, 128], [64, 16], [1, 32]]),
                bass.AP(rtile, 0, [[1024, 128], [64, 16], [1, 32]]),
                bass.AP(tabs, tg * 64 + 32, [[NTG * 64, 128], [0, 16], [1, 32]]),
                ALU.mult), "v")
            P.op("vector", lambda E: E.tensor_add(rtile[:, :], rcos[:, :], rsin[:, :]), "v")
            s = QSCALE if scale_q else 1.0
            mv = P.op("vector", lambda E, s=s: E.tensor_scalar(
                rcos[:, :], rtile[:, :], s, 0.0, ALU.mult, ALU.min), "v")
            P.wait("scalar", "v", mv)
            ev = P.op("scalar", lambda E: E.activation(out=rsin[:, :], in_=rcos[:, :], func=AF.Exp), "s")
            P.wait("vector", "s", ev)
            if scale_q:
                P.op("vector", lambda E, s=s: E.tensor_scalar(
                    rcos[:, :], rtile[:, :], s, 0.0, ALU.mult, ALU.max), "v")
                fv = P.op("vector", lambda E: E.tensor_add(rtile[:, :], rcos[:, :], rsin[:, :]), "v")
            else:
                fv = P.op("vector", lambda E: E.scalar_tensor_tensor(
                    rtile[:, :], rtile[:, :], 0.0, rsin[:, :], ALU.max, ALU.add), "v")
            return fv

        tr_guard = None
        for tg in range(NTG):
            pb = pB0 if tg % 2 == 0 else pB1
            pb_wait(tg % 2)
            tmk = qkv_mms(0, tg, pb, ("p0", lq))
            P.wait("scalar", "t", tmk)
            fv = rope_elu(tg, pb, True)
            pb_used(tg % 2, s=P.cnt["s"])
            P.wait("tensor", "v", fv)
            if tr_guard is not None:
                P.wait("tensor", "s", tr_guard)
            ttk = None
            for p in range(NP):
                ttk = P.op("tensor", lambda E, p=p: E.transpose(
                    pTR[:, p * 128:(p + 1) * 128], rtile[:, p * 128:(p + 1) * 128],
                    ident[:, :]), "t")
            P.wait("scalar", "t", ttk)
            tr_guard = P.op("scalar", lambda E, tg=tg: E.activation(
                out=bass.AP(arena, tg * 128, [[32768, 128], [TL, NP], [1, 128]]),
                in_=pTR[:, :].rearrange("p (g j) -> p g j", g=NP),
                func=AF.Identity), "s")
        q_pass_t = P.cnt["t"]

        lk = P.dma("gpsimd", PART(1)[:, :].rearrange("p (k n) -> p k n", k=KC),
                   wqkv.ap()[:, 1024:2048].rearrange("(k p) n -> p k n", p=128), "p1")
        P.wait("gpsimd", "t", q_pass_t)
        lv = P.dma("gpsimd", PART(0)[:, :].rearrange("p (k n) -> p k n", k=KC),
                   wqkv.ap()[:, 2048:3072].rearrange("(k p) n -> p k n", p=128), "p0")

        kv_mm = None
        for tg in range(NTG):
            b = 0
            pb_wait(0)
            tmk = qkv_mms(1, tg, pB0, ("p1", lk))
            P.wait("scalar", "t", tmk)
            P.wait("scalar", "v", G.get(("kcp",), 0))
            fv = rope_elu(tg, pB0, False)
            pb_used(0, s=P.cnt["s"])
            kg = G.get(("kft", b))
            if kg is not None:
                P.wait("vector", "t", kg)
            kcp = P.op("vector", lambda E, b=b: E.tensor_copy(kft[b][:, :], rtile[:, :]), "v")
            G[("kcp",)] = kcp
            pb_wait(1)
            tmv = qkv_mms(0, tg, pB1, ("p0", lv))
            P.wait("scalar", "t", tmv)
            vv = P.op("scalar", lambda E, b=b: E.activation(
                out=vt[b][:, :], in_=pB1[:, :], func=AF.Identity), "s")
            pb_used(1, s=vv)
            P.wait("tensor", "v", kcp)
            P.wait("tensor", "s", vv)
            for p in range(NP):
                pdst = pM if p < 4 else pC
                pcol = (p % 4) * 128
                P.op("tensor", lambda E, b=b, p=p, pdst=pdst, pcol=pcol, tg=tg: E.matmul(
                    pdst[:, pcol:pcol + 128], kft[b][:, p * 128:(p + 1) * 128],
                    vt[b][:, p * 128:(p + 1) * 128],
                    start=(tg == 0 and p % 4 == 0), stop=(tg == NTG - 1),
                    skip_group_check=True), "t")
                kv_mm = P.op("tensor", lambda E, b=b, p=p, tg=tg: E.matmul(
                    pSS[:, p:p + 1], kft[b][:, p * 128:(p + 1) * 128], ones_c[:, :],
                    start=(tg == 0 and p == 0), stop=(tg == NTG - 1),
                    skip_group_check=True), "t")
            G[("kft", b)] = kv_mm

        P.wait("vector", "t", None)
        P.op("vector", lambda E: E.tensor_copy(kvx[:, 0:512], pM[:, :]), "v")
        P.op("vector", lambda E: E.tensor_copy(kvx[:, 512:1024], pC[:, :]), "v")
        pk = P.op("vector", lambda E: E.tensor_copy(kvx[:, 1024:1032], pSS[:, 0:8]), "v")
        P.wait("gpsimd", "v", pk)
        P.dma("gpsimd", cc_in.ap(), kvx[:, :], "gd")
        P.wait("gpsimd", "gd", None)
        P.op("gpsimd", lambda E: E.collective_compute(
            "AllReduce", ALU.add,
            replica_groups=[list(range(NCORES))],
            ins=[cc_in.ap().opt()], outs=[cc_out.ap().opt()]), "cc")
        P.wait("gpsimd", "cc", None)
        arb = P.dma("gpsimd", kvx[:, :], cc_out.ap(), "cc2")
        P.wait("vector", "cc2", arb)
        P.op("vector", lambda E: E.memset(ksb[:, :], 0.0), "v")
        for p in range(NP):
            P.op("vector", lambda E, p=p: E.tensor_copy(
                ksb[0:64, p * 16 + 2 * p: p * 16 + 2 * p + 1],
                kvx[0:64, 1024 + p:1025 + p]), "v")
            P.op("vector", lambda E, p=p: E.tensor_copy(
                ksb[64:128, p * 16 + 2 * p + 1: p * 16 + 2 * p + 2],
                kvx[64:128, 1024 + p:1025 + p]), "v")
        kv_ready_v = P.cnt["v"]

        G[("qsc_v",)] = kv_ready_v
        den_s = None
        for tc in range(NQ):
            P.wait("tensor", "v", kv_ready_v)
            if den_s is not None:
                P.wait("tensor", "v", G[("dmax_v",)])
            dmk = None
            for p in range(NP):
                dmk = P.op("tensor", lambda E, p=p, tc=tc: E.matmul(
                    pSS[0:16, :], ksb[:, p * 16:(p + 1) * 16], QF(p, tc * QT, QT),
                    start=(p == 0), stop=(p == NP - 1)), "t")
            P.wait("vector", "t", dmk)
            P.wait("vector", "s", G.get(("dens",), 0))
            dvv = P.op("vector", lambda E: E.tensor_scalar_max(dmax[:, :], pSS[0:16, :], 1e-6), "v")
            G[("dmax_v",)] = dvv
            P.wait("scalar", "v", dvv)
            P.op("scalar", lambda E: E.activation(out=dmax[:, :], in_=dmax[:, :], func=AF.Ln), "s")
            den_s = P.op("scalar", lambda E: E.activation(
                out=denr[:, :], in_=dmax[:, :], func=AF.Exp, scale=-1.0), "s")
            G[("dens",)] = den_s
            for p in range(NP):
                i = (tc * NP + p) % 2
                P.wait("tensor", "s", den_s)
                P.wait("tensor", "v", G.get(("qsc_v",), 0))
                bmk = P.op("tensor", lambda E, p=p: E.matmul(
                    pM[:, :], sel[:, p * 128:(p + 1) * 128],
                    denr[:, :], start=True, stop=True), "t")
                P.wait("vector", "t", bmk)
                qv = P.op("vector", lambda E, p=p, tc=tc: E.tensor_mul(
                    QF(p, tc * QT, QT), QF(p, tc * QT, QT), pM[:, :]), "v")
                G[("qsc_v",)] = qv
                pb = pB0 if i == 0 else pB1
                pb_wait(i)
                P.wait("tensor", "v", qv)
                P.op("tensor", lambda E, p=p, tc=tc, pb=pb: E.matmul(
                    pb[0:64, 0:512], kvx[0:64, p * 128:p * 128 + 64],
                    bass.AP(arena, p * TL + tc * QT, [[32768, 64], [1, QT]]),
                    start=True, stop=True, tile_position=(0, 0)), "t")
                amk = P.op("tensor", lambda E, p=p, tc=tc, pb=pb: E.matmul(
                    pb[64:128, 0:512], kvx[64:128, p * 128 + 64:(p + 1) * 128],
                    bass.AP(arena, 64 * 32768 + p * TL + tc * QT, [[32768, 64], [1, QT]]),
                    start=True, stop=True, tile_position=(64, 64)), "t")
                P.wait("scalar", "t", amk)
                smk = P.op("scalar", lambda E, p=p, tc=tc, pb=pb: E.activation(
                    out=Y1(p, tc * QT, QT), in_=pb[:, 0:512], func=AF.Identity), "s")
                pb_used(i, s=smk)
        at_done_s = P.cnt["s"]

        wo_t = None
        for o in range(KC):
            buf = o % 2
            g = G.get(("wos", buf))
            if g is not None:
                P.wait("gpsimd", "t", g)
            if o < 2:
                P.wait("gpsimd", "t", q_pass_t)
            ls = "wos0" if buf == 0 else "wos1"
            lw = P.dma("gpsimd", W1S(buf)[:, :].rearrange("p (k j) -> p k j", k=KC),
                       wout.ap()[:, o * 128:(o + 1) * 128].rearrange("(k p) j -> p k j", p=128),
                       ls)
            for tc in range(NQ):
                i = (o * NQ + tc) % 2
                pb = pB0 if i == 0 else pB1
                pb_wait(i)
                P.wait("tensor", ls, lw)
                P.wait("tensor", "s", at_done_s)
                tmk = None
                for k in range(KC):
                    tmk = P.op("tensor", lambda E, o=o, tc=tc, k=k, pb=pb, buf=buf: E.matmul(
                        pb[:, 0:512], W1S(buf)[:, k * 128:(k + 1) * 128],
                        Y1(k, tc * QT, QT),
                        start=(k == 0), stop=(k == KC - 1)), "t")
                G[("wos", buf)] = tmk
                wo_t = tmk
                P.wait("vector", "t", tmk)
                vmk = P.op("vector", lambda E, o=o, tc=tc, pb=pb: E.tensor_add(
                    Xv(o, tc * QT, QT), Xv(o, tc * QT, QT), pb[:, 0:512]), "v")
                pb_used(i, v=vmk)

        G[("dst_t",)] = P.cnt["t"]
        G[("pmc_rd_v")] = P.cnt["v"]
        G[("pmc_rd_s")] = P.cnt["s"]
        G[("pss_rd_v")] = P.cnt["v"]
        G[("w1s", 0)] = wo_t
        G[("w1s", 1)] = wo_t
        G[("w2s", 0)] = kv_mm
        G[("w2s", 1)] = kv_mm
        G[("hq_gate",)] = wo_t

    import os as _os
    _phases = int(_os.environ.get("MK_PHASES", "3"))

    def dump_X():
        vnow = P.cnt["v"]
        P.wait("gpsimd", "v", vnow)
        for o in range(KC):
            P.dma("gpsimd", out_fm.ap()[o * 128:(o + 1) * 128, :], Xv(o, 0, TL), "gd")

    emit_ffn(first=True)
    if _phases == 1:
        dump_X()
    if _phases >= 2:
        emit_attn()
        if _phases == 2:
            dump_X()
    P.wait("vector", "t", None)
    P.wait("scalar", "t", None)
    emit_ffn(first=False)


    P.wait("gpsimd", "gd", None)
    P.wait("sync", "gd", P.cnt["gd"])
    for s in ("c0", "x0", "w1s0", "w1s1", "w2s0", "w2s1", "qa0", "qa1",
              "wb0", "wb1", "p0", "p1", "wos0", "wos1", "cc2"):
        P.wait("sync", s, None)

    with es:
        for s in list(P.cnt):
            P.handles[s] = es.enter_context(nc.semaphore(s))
        blk = es.enter_context(nc.Block())
        for eng in ENGS:
            stream = P.streams[eng]
            def body(E, stream=stream):
                for fn in stream:
                    fn(E)
            getattr(blk, eng)(body)
    return nc


def _rope_tables(core_id):
    pos = np.arange(core_id * TL, (core_id + 1) * TL, dtype=np.float64)
    inv = ROPE_BASE ** (-np.arange(0, 64, 2, dtype=np.float64) / 64.0)
    fr = pos[:, None] * inv[None, :]
    emb = np.concatenate([fr, fr], axis=1)
    cos = np.cos(emb).astype(np.float32)
    sin = np.sin(emb).astype(np.float32)
    ss = sin.copy()
    ss[:, 0:32] *= -1.0
    to_sb = lambda a: np.ascontiguousarray(
        a.reshape(NTG, 128, 64).transpose(1, 0, 2).reshape(128, NTG * 64)
    ).astype(ml_dtypes.bfloat16)
    return to_sb(cos), to_sb(ss)


def kernel(L0_attn_x, L0_attn_W_qkv, L0_attn_W_out, L0_ffn_ln_gamma,
           L0_ffn_ln_beta, L0_ffn_fc1_q, L0_ffn_fc1_scales, L0_ffn_fc1_bias,
           L0_ffn_fc2_q, L0_ffn_fc2_scales, L0_ffn_fc2_bias):
    x = np.ascontiguousarray(np.asarray(L0_attn_x, dtype=np.float32)[0].T)
    wqkv = np.ascontiguousarray(np.asarray(L0_attn_W_qkv, dtype=np.float32))
    wo = np.ascontiguousarray(np.asarray(L0_attn_W_out, dtype=np.float32))
    gamT = np.ascontiguousarray(np.asarray(L0_ffn_ln_gamma, np.float32).reshape(KC, 128).T)
    betT = np.ascontiguousarray(np.asarray(L0_ffn_ln_beta, np.float32).reshape(KC, 128).T)
    b1T = np.ascontiguousarray(np.asarray(L0_ffn_fc1_bias, np.float32).reshape(MC, 128).T)
    b2T = np.ascontiguousarray(np.asarray(L0_ffn_fc2_bias, np.float32).reshape(KC, 128).T)
    q1 = np.ascontiguousarray(np.asarray(L0_ffn_fc1_q, np.int32))
    s1 = np.ascontiguousarray(np.asarray(L0_ffn_fc1_scales, np.float32))
    q2 = np.ascontiguousarray(np.asarray(L0_ffn_fc2_q, np.int32))
    s2 = np.ascontiguousarray(np.asarray(L0_ffn_fc2_scales, np.float32))
    iden = np.eye(128, dtype=ml_dtypes.bfloat16)
    selm = np.zeros((16, 1024), dtype=ml_dtypes.bfloat16)
    for p in range(NP):
        selm[2 * p, p * 128:p * 128 + 64] = 1
        selm[2 * p + 1, p * 128 + 64:(p + 1) * 128] = 1

    nc = build_program()
    in_maps = []
    for c in range(NCORES):
        tc, ts = _rope_tables(c)
        in_maps.append(dict(
            x_fm=np.ascontiguousarray(x[:, c * TL:(c + 1) * TL]),
            wqkv=wqkv, wout=wo, gamT=gamT, betT=betT, b1T=b1T, b2T=b2T,
            q1=q1, s1=s1, q2=q2, s2=s2, ropc=tc, rops=ts, identin=iden, selin=selm,
        ))
    res = run_bass_kernel_spmd(nc, in_maps, core_ids=list(range(NCORES)))
    full = np.concatenate([r["out_fm"] for r in res.results], axis=1)
    return np.ascontiguousarray(full.T)[None, :, :]


# version 05
# speedup vs baseline: 1.2044x; 1.2044x over previous
import numpy as np
import ml_dtypes
from contextlib import ExitStack

import concourse.bass as bass
import concourse.mybir as mybir
from concourse.bass_utils import run_bass_kernel_spmd

F32 = mybir.dt.float32
BF16 = mybir.dt.bfloat16
I32 = mybir.dt.int32
AF = mybir.ActivationFunctionType
ALU = mybir.AluOpType

NCORES = 8
D = 1024
TT = 16384
TL = TT // NCORES
KC = 8
F4 = 4096
MC = 32
NQ = 4
QT = TL // NQ
NTG = TL // 128
NP = 8
EPS = 1e-5
ROPE_BASE = 500000.0
QSCALE = 64.0 ** -0.5

ENGS = ("sync", "gpsimd", "tensor", "vector", "scalar")


class Plan:

    def __init__(self):
        self.streams = {e: [] for e in ENGS}
        self.cnt = {}
        self.handles = {}

    def new_sem(self, name):
        self.cnt[name] = 0

    def emit(self, eng, fn):
        self.streams[eng].append(fn)

    def wait(self, eng, sem, val=None):
        v = self.cnt[sem] if val is None else val
        if v is None or v <= 0:
            return
        self.emit(eng, lambda E, s=sem, v=v: E.wait_ge(self.handles[s], v))

    def op(self, eng, fn, sem=None, k=1):
        if sem is None:
            self.emit(eng, fn)
            return None
        self.cnt[sem] += k
        self.emit(eng, lambda E, fn=fn, s=sem, k=k: fn(E).then_inc(self.handles[s], k))
        return self.cnt[sem]

    def dma(self, eng, out, in_, sem):
        return self.op(eng, lambda E, o=out, i=in_: E.dma_start(out=o, in_=i), sem, 16)


def build_program():
    nc = bass.Bass("TRN2", target_bir_lowering=False)
    P = Plan()
    for s in ("sd", "gd", "t", "v", "s", "cc", "c0", "x0",
              "w1s0", "w1s1", "w1s2", "w1s3", "w2s0", "w2s1",
              "wbq0", "wbq1", "wbq2", "wbq3", "wbo0", "wbo1",
              "p0", "p1", "wos0", "wos1", "cc2"):
        P.new_sem(s)

    x_fm = nc.dram_tensor("x_fm", [D, TL], F32, kind="ExternalInput")
    wqkv = nc.dram_tensor("wqkv", [D, 3 * D], F32, kind="ExternalInput")
    wout = nc.dram_tensor("wout", [D, D], F32, kind="ExternalInput")
    gamT = nc.dram_tensor("gamT", [128, KC], F32, kind="ExternalInput")
    betT = nc.dram_tensor("betT", [128, KC], F32, kind="ExternalInput")
    b1T = nc.dram_tensor("b1T", [128, MC], F32, kind="ExternalInput")
    b2T = nc.dram_tensor("b2T", [128, KC], F32, kind="ExternalInput")
    q1 = nc.dram_tensor("q1", [F4, D], I32, kind="ExternalInput")
    s1 = nc.dram_tensor("s1", [MC, D], F32, kind="ExternalInput")
    q2 = nc.dram_tensor("q2", [D, F4], I32, kind="ExternalInput")
    s2 = nc.dram_tensor("s2", [KC, F4], F32, kind="ExternalInput")
    ropc = nc.dram_tensor("ropc", [128, NTG * 64], BF16, kind="ExternalInput")
    rops = nc.dram_tensor("rops", [128, NTG * 64], BF16, kind="ExternalInput")
    identin = nc.dram_tensor("identin", [128, 128], BF16, kind="ExternalInput")
    selin = nc.dram_tensor("selin", [16, 1024], BF16, kind="ExternalInput")
    out_fm = nc.dram_tensor("out_fm", [D, TL], F32, kind="ExternalOutput")

    w1c = nc.dram_tensor("w1c", [F4, D], BF16, kind="Internal")
    w2c = nc.dram_tensor("w2c", [D, F4], BF16, kind="Internal")
    cc_in = nc.dram_tensor("cc_in", [128, 1032], BF16, kind="Internal")
    cc_out = nc.dram_tensor("cc_out", [128, 1032], BF16, kind="Internal")

    es = ExitStack()
    sb = lambda name, shape, dt: es.enter_context(nc.sbuf_tensor(name, shape, dt))
    ps = lambda name, shape, dt: es.enter_context(nc.psum_tensor(name, shape, dt))

    X = sb("X", [128, KC * TL], F32)
    arena = sb("arena", [128, 32768], BF16)
    wbuf = sb("wbuf", [128, 18432], BF16)
    kft = [sb("kft0", [128, 1024], BF16)]
    vt = [sb("vt0", [128, 1024], BF16)]
    rtile = sb("rtile", [128, 1024], BF16)
    rcos = sb("rcos", [128, 1024], BF16)
    rsin = sb("rsin", [128, 1024], BF16)
    denr = sb("denr", [16, QT], BF16)
    dmax = sb("dmax", [16, QT], F32)
    kvx = sb("kvx", [128, 1032], BF16)
    ksb = sb("ksb", [128, 128], BF16)
    sel = sb("sel", [16, 1024], BF16)
    mir = [sb("mir0", [128, QT], BF16)]
    sqt = [sb("sqt0", [128, QT], BF16)]
    tb1 = sb("tb1", [128, QT], BF16)
    tb2 = sb("tb2", [128, QT], BF16)
    z4t = sb("z4t", [128, QT], F32)
    rows = sb("rows", [1, 5 * QT], F32)
    sgam = sb("sgam", [128, KC], F32)
    sbet = sb("sbet", [128, KC], F32)
    sb1t = sb("sb1t", [128, MC], F32)
    sqb2 = sb("sqb2", [128, KC], F32)
    tabc = sb("tabc", [128, NTG * 64], BF16)
    tabs = sb("tabs", [128, NTG * 64], BF16)
    ident = sb("ident", [128, 128], BF16)
    ones_c = sb("ones_c", [128, 1], BF16)
    ones_r = sb("ones_r", [1, 128], F32)

    R_M, R_C, R_A1, R_P, R_Q = (rows[0:1, i * QT:(i + 1) * QT] for i in range(5))
    R_TS = R_C

    pB0 = ps("pB0", [128, 1024], F32)
    pB1 = ps("pB1", [128, 1024], F32)
    pSS = ps("pSS", [128, 512], F32)
    pM = ps("pM", [128, 512], F32)
    pC = ps("pC", [128, 512], F32)
    pTR = ps("pTR", [128, 1024], BF16)

    def Xv(k, c0, n):
        return X[:, k * TL + c0: k * TL + c0 + n]

    def Av(m):
        return arena[:, m * QT:(m + 1) * QT]

    def Hq(k):
        return arena[:, 16384 + k * QT: 16384 + (k + 1) * QT]

    def QF(p, c0, n):
        return arena[:, p * TL + c0: p * TL + c0 + n]

    def Y1(k, c0, n):
        return arena[:, 16384 + k * TL + c0: 16384 + k * TL + c0 + n]

    def W1S(buf):
        return wbuf[:, buf * 1024:(buf + 1) * 1024]

    def SCB(buf):
        return wbuf[:, 4096 + buf * 1024: 4096 + (buf + 1) * 1024]

    def W2S(buf):
        return wbuf[:, 6144 + buf * 4096: 6144 + (buf + 1) * 4096]

    SC2B = wbuf[:, 14336:18432]

    def PART(slot):
        return wbuf[:, slot * 8192:(slot + 1) * 8192]

    def WOS(buf):
        return wbuf[:, 16384 + buf * 1024: 16384 + (buf + 1) * 1024]

    G = {}

    def pb_wait(buf):
        P.wait("tensor", "v", G.get(("pbv", buf), 0))
        P.wait("tensor", "s", G.get(("pbs", buf), 0))

    def pb_used(buf, v=None, s=None):
        if v is not None:
            G[("pbv", buf)] = v
        if s is not None:
            G[("pbs", buf)] = s

    P.dma("gpsimd", sgam[:, :], gamT.ap(), "c0")
    P.dma("gpsimd", sbet[:, :], betT.ap(), "c0")
    P.dma("gpsimd", sb1t[:, :], b1T.ap(), "c0")
    P.dma("gpsimd", sqb2[:, :], b2T.ap(), "c0")
    P.dma("gpsimd", tabc[:, :], ropc.ap(), "c0")
    P.dma("gpsimd", tabs[:, :], rops.ap(), "c0")
    P.dma("gpsimd", ident[:, :], identin.ap(), "c0")
    P.dma("gpsimd", sel[:, :], selin.ap(), "c0")
    const_ld = P.cnt["c0"]

    xld = [P.dma("gpsimd", Xv(k, 0, TL), x_fm.ap()[k * 128:(k + 1) * 128, :], "x0")
           for k in range(KC)]

    P.op("vector", lambda E: E.memset(ones_c[:, :], 1.0), "v")
    P.op("vector", lambda E: E.memset(ones_r[:, :], 1.0), "v")
    P.wait("scalar", "c0", const_ld)
    P.op("scalar", lambda E: E.mul(sqb2[:, :], sqb2[:, :], 0.25), "s")

    def emit_ln_quarter(t0, mode, dst):
        for k in range(KC):
            b = 0
            P.wait("vector", "x0", xld[-1])
            pv = G.get(("mir", b))
            if pv is not None:
                P.wait("vector", "t", pv)
            P.op("vector", lambda E, k=k, b=b, t0=t0: E.tensor_copy(
                mir[b][:, :], Xv(k, t0, QT)), "v")
            mv = P.op("vector", lambda E, b=b: E.tensor_mul(
                sqt[b][:, :], mir[b][:, :], mir[b][:, :]), "v")
            P.wait("tensor", "v", mv)
            if k == 0:
                P.wait("tensor", "v", G.get(("pss_rd_v"), 0))
                P.wait("tensor", "s", G.get(("pss_rd_s"), 0))
            P.op("tensor", lambda E, b=b, k=k: E.matmul(
                pSS[0:1, :], ones_c[:, :], mir[b][:, :],
                start=(k == 0), stop=(k == KC - 1)), "t")
            tm = P.op("tensor", lambda E, b=b, k=k: E.matmul(
                pSS[32:33, :], ones_c[:, :], sqt[b][:, :],
                start=(k == 0), stop=(k == KC - 1), tile_position=(0, 32)), "t")
            G[("mir", b)] = tm
        P.wait("vector", "t", None)
        P.op("vector", lambda E: E.tensor_scalar_mul(R_M, pSS[0:1, :], -1.0 / D), "v")
        P.op("vector", lambda E: E.tensor_mul(R_TS, R_M, R_M), "v")
        P.op("vector", lambda E: E.scalar_tensor_tensor(
            R_TS, pSS[32:33, :], 1.0 / D, R_TS, ALU.mult, ALU.subtract), "v")
        G[("pss_rd_v")] = P.cnt["v"]
        cv = P.op("vector", lambda E: E.tensor_scalar(
            R_A1, R_TS, EPS, None, ALU.add), "v")
        P.wait("scalar", "v", cv)
        P.op("scalar", lambda E: E.activation(out=R_A1, in_=R_A1, func=AF.Ln), "s")
        ssn = P.op("scalar", lambda E: E.activation(out=R_A1, in_=R_A1, func=AF.Exp, scale=-0.5), "s")
        if mode == "ffn":
            cv2 = P.op("vector", lambda E: E.tensor_scalar(
                R_C, R_TS, 1.0 + EPS, EPS * EPS, ALU.mult, ALU.add), "v")
            P.wait("scalar", "v", cv2)
            P.op("scalar", lambda E: E.activation(out=R_C, in_=R_C, func=AF.Ln), "s")
            ssn = P.op("scalar", lambda E: E.activation(out=R_C, in_=R_C, func=AF.Exp, scale=-0.5), "s")
            P.wait("vector", "s", ssn)
            P.op("vector", lambda E: E.tensor_scalar(R_P, R_A1, 0.5, 1.0, ALU.mult, ALU.add), "v")
            P.op("vector", lambda E: E.scalar_tensor_tensor(R_Q, R_M, 0.5, R_A1, ALU.mult, ALU.mult), "v")
        rows_v, rows_s = P.cnt["v"], P.cnt["s"]
        P.wait("tensor", "v", rows_v)
        P.wait("tensor", "s", rows_s)
        P.wait("tensor", "v", G.get(("pmc_rd_v"), 0))
        P.wait("tensor", "s", G.get(("pmc_rd_s"), 0))
        P.op("tensor", lambda E: E.matmul(pM[:, :], ones_r[:, :], R_M, start=True, stop=True), "t")
        src2 = R_C if mode == "ffn" else R_A1
        bt = P.op("tensor", lambda E, s2=src2: E.matmul(pC[:, :], ones_r[:, :], s2, start=True, stop=True), "t")
        P.wait("vector", "t", bt)
        P.wait("vector", "t", G.get(("dst_t",), 0))
        P.wait("scalar", "t", G.get(("dst_t",), 0))
        for k in range(KC):
            pv = G.get(("tb_s"))
            if pv is not None and mode == "ffn":
                P.wait("vector", "s", pv)
            P.op("vector", lambda E, k=k, t0=t0: E.tensor_add(
                tb1[:, :], Xv(k, t0, QT), pM[:, :]), "v")
            if mode == "ffn":
                tv = P.op("vector", lambda E: E.tensor_mul(tb2[:, :], tb1[:, :], pC[:, :]), "v")
                P.wait("scalar", "v", tv)
                smk = P.op("scalar", lambda E, k=k: E.activation(
                    out=dst(k), in_=tb2[:, :], func=AF.Identity,
                    bias=sbet[:, k:k + 1], scale=sgam[:, k:k + 1]), "s")
                G[("tb_s")] = smk
            else:
                P.op("vector", lambda E, k=k: E.tensor_mul(dst(k), tb1[:, :], pC[:, :]), "v")
        G[("pmc_rd_v")] = P.cnt["v"]
        G[("pmc_rd_s")] = P.cnt["s"]
        return P.cnt["v"], P.cnt["s"]

    def emit_ffn(first):
        for q in range(NQ):
            t0 = q * QT
            h_v, h_s = emit_ln_quarter(t0, "ffn", Hq)
            for m in range(MC):
                buf = m % 4
                ls = f"w1s{buf}"
                if first and q == 0:
                    scb = m % 2
                    P.wait("gpsimd", "t", G.get(("w1s", buf), 0))
                    P.wait("gpsimd", f"wbq{buf}", G.get(("wbq", buf), 0))
                    P.wait("gpsimd", "v", G.get(("scb", scb), 0))
                    P.dma("gpsimd", W1S(buf)[:, :],
                          q1.ap()[m * 128:(m + 1) * 128, :], ls)
                    lsc = P.dma("gpsimd", SCB(scb)[:, :],
                                bass.AP(s1, m * D, [[0, 128], [1, D]]), ls)
                    P.wait("vector", ls, lsc)
                    stt = P.op("vector", lambda E, buf=buf, scb=scb: E.scalar_tensor_tensor(
                        W1S(buf)[:, :], W1S(buf)[:, :], -128.0, SCB(scb)[:, :],
                        ALU.add, ALU.mult), "v")
                    G[("scb", scb)] = stt
                    P.wait("sync", "v", stt)
                    wbk = P.dma("sync", w1c.ap()[m * 128:(m + 1) * 128, :],
                                W1S(buf)[:, :], f"wbq{buf}")
                    G[("wbq", buf)] = wbk
                    w_ready = ("v", stt)
                else:
                    g = G.get(("w1s", buf))
                    if g is not None:
                        P.wait("sync", "t", g)
                    if first and q == 1 and m < 4:
                        for wb in range(4):
                            P.wait("sync", f"wbq{wb}", G.get(("wbq", wb), 0))
                    lw = P.dma("sync", W1S(buf)[:, :],
                               w1c.ap()[m * 128:(m + 1) * 128, :], ls)
                    w_ready = (ls, lw)
                pb = pB0 if m % 2 == 0 else pB1
                pb_wait(m % 2)
                P.wait("tensor", w_ready[0], w_ready[1])
                P.wait("tensor", "v", h_v)
                P.wait("tensor", "s", h_s)
                tmk = None
                for k in range(KC):
                    tmk = P.op("tensor", lambda E, pb=pb, buf=buf, k=k: E.matmul(
                        pb[:, 0:512], W1S(buf)[:, k * 128:(k + 1) * 128], Hq(k),
                        start=(k == 0), stop=(k == KC - 1)), "t")
                G[("w1s", buf)] = tmk
                P.wait("scalar", "t", tmk)
                smk = P.op("scalar", lambda E, pb=pb, m=m: E.activation(
                    out=Av(m), in_=pb[:, 0:512], func=AF.Silu,
                    bias=sb1t[:, m:m + 1], scale=1.0), "s")
                pb_used(m % 2, s=smk)
            G[("dst_t",)] = P.cnt["t"]
            a_ready_s = P.cnt["s"]
            pq_v = P.cnt["v"]
            P.wait("tensor", "v", pq_v)
            P.op("tensor", lambda E: E.matmul(pM[:, :], ones_r[:, :], R_P, start=True, stop=True), "t")
            pqb = P.op("tensor", lambda E: E.matmul(pC[:, :], ones_r[:, :], R_Q, start=True, stop=True), "t")
            G[("pmc_rd_v")] = 0
            G[("pmc_rd_s")] = 0
            for o in range(KC):
                buf = o % 2
                ls = "w2s0" if buf == 0 else "w2s1"
                if first and q == 0:
                    P.wait("gpsimd", "t", G.get(("w2s", buf), 0))
                    P.wait("gpsimd", f"wbo{buf}", G.get(("wbo", buf), 0))
                    P.wait("gpsimd", "v", G.get(("sc2",), 0))
                    P.dma("gpsimd", W2S(buf)[:, :],
                          q2.ap()[o * 128:(o + 1) * 128, :], ls)
                    lsc = P.dma("gpsimd", SC2B[:, :],
                                bass.AP(s2, o * F4, [[0, 128], [1, F4]]), ls)
                    P.wait("vector", ls, lsc)
                    stt = P.op("vector", lambda E, buf=buf: E.scalar_tensor_tensor(
                        W2S(buf)[:, :], W2S(buf)[:, :], -128.0, SC2B[:, :],
                        ALU.add, ALU.mult), "v")
                    G[("sc2",)] = stt
                    P.wait("sync", "v", stt)
                    wbk = P.dma("sync", w2c.ap()[o * 128:(o + 1) * 128, :],
                                W2S(buf)[:, :], f"wbo{buf}")
                    G[("wbo", buf)] = wbk
                    w_ready = ("v", stt)
                else:
                    g = G.get(("w2s", buf))
                    if g is not None:
                        P.wait("sync", "t", g)
                    if first and q == 1 and o < 2:
                        for wb in range(2):
                            P.wait("sync", f"wbo{wb}", G.get(("wbo", wb), 0))
                    lw = P.dma("sync", W2S(buf)[:, :],
                               w2c.ap()[o * 128:(o + 1) * 128, :], ls)
                    w_ready = (ls, lw)
                pb = pB0 if buf == 0 else pB1
                pb_wait(buf)
                P.wait("tensor", w_ready[0], w_ready[1])
                P.wait("tensor", "s", a_ready_s)
                tmk = None
                for k in range(MC):
                    tmk = P.op("tensor", lambda E, pb=pb, buf=buf, k=k: E.matmul(
                        pb[:, 0:512], W2S(buf)[:, k * 128:(k + 1) * 128], Av(k),
                        start=(k == 0), stop=(k == MC - 1)), "t")
                G[("w2s", buf)] = tmk
                zg = G.get(("z4_v"))
                if zg is not None:
                    P.wait("scalar", "v", zg)
                P.wait("scalar", "t", tmk)
                zs = P.op("scalar", lambda E, pb=pb, o=o: E.activation(
                    out=z4t[:, :], in_=pb[:, 0:512], func=AF.Identity,
                    bias=sqb2[:, o:o + 1], scale=0.25), "s")
                pb_used(buf, s=zs)
                P.wait("vector", "t", pqb)
                P.wait("vector", "s", zs)
                P.op("vector", lambda E, o=o, t0=t0: E.tensor_mul(
                    Xv(o, t0, QT), Xv(o, t0, QT), pM[:, :]), "v")
                zv = P.op("vector", lambda E, o=o, t0=t0: E.tensor_add(
                    Xv(o, t0, QT), Xv(o, t0, QT), z4t[:, :]), "v")
                G[("z4_v")] = zv
                vmk = P.op("vector", lambda E, o=o, t0=t0: E.tensor_add(
                    Xv(o, t0, QT), Xv(o, t0, QT), pC[:, :]), "v")
                G[("pmc_rd_v")] = vmk
                if not first:
                    P.wait("gpsimd", "v", vmk)
                    P.dma("gpsimd", out_fm.ap()[o * 128:(o + 1) * 128, t0:t0 + QT],
                          Xv(o, t0, QT), "gd")

    def emit_attn():
        ffn1_t = P.cnt["t"]

        y1_marks = []
        for tc in range(NQ):
            y1_marks.append(emit_ln_quarter(
                tc * QT, "attn", lambda k, tc=tc: Y1(k, tc * QT, QT)))

        P.wait("gpsimd", "t", ffn1_t)
        for wb in range(4):
            P.wait("gpsimd", f"wbq{wb}", None)
        for wb in range(2):
            P.wait("gpsimd", f"wbo{wb}", None)
        lq = P.dma("gpsimd", PART(0)[:, :].rearrange("p (k n) -> p k n", k=KC),
                   wqkv.ap()[:, 0:1024].rearrange("(k p) n -> p k n", p=128), "p0")

        def qkv_mms(part_slot, tg, pbank, pmark):
            mvv, mss = y1_marks[tg // 4]
            P.wait("tensor", "v", mvv)
            P.wait("tensor", pmark[0], pmark[1])
            tmk = None
            for h in range(2):
                for k in range(KC):
                    tmk = P.op("tensor", lambda E, pbank=pbank, k=k, h=h, tg=tg,
                               ps=part_slot: E.matmul(
                        pbank[:, h * 512:(h + 1) * 512],
                        Y1(k, (tg // 4) * QT + (tg % 4) * 128, 128),
                        PART(ps)[:, k * 1024 + h * 512: k * 1024 + (h + 1) * 512],
                        start=(k == 0), stop=(k == KC - 1)), "t")
            return tmk

        def rope_elu(tg, pbank, scale_q):
            sc = P.op("scalar", lambda E, pb=pbank: E.activation(
                out=rtile[:, :], in_=pb[:, :], func=AF.Identity), "s")
            P.wait("vector", "s", sc)
            P.op("vector", lambda E, tg=tg: E.tensor_tensor(
                rcos[:, :].rearrange("p (g j) -> p g j", g=16),
                rtile[:, :].rearrange("p (g j) -> p g j", g=16),
                bass.AP(tabc, tg * 64, [[NTG * 64, 128], [0, 16], [1, 64]]),
                ALU.mult), "v")
            P.op("vector", lambda E, tg=tg: E.tensor_tensor(
                bass.AP(rsin, 0, [[1024, 128], [64, 16], [1, 32]]),
                bass.AP(rtile, 32, [[1024, 128], [64, 16], [1, 32]]),
                bass.AP(tabs, tg * 64, [[NTG * 64, 128], [0, 16], [1, 32]]),
                ALU.mult), "v")
            P.op("vector", lambda E, tg=tg: E.tensor_tensor(
                bass.AP(rsin, 32, [[1024, 128], [64, 16], [1, 32]]),
                bass.AP(rtile, 0, [[1024, 128], [64, 16], [1, 32]]),
                bass.AP(tabs, tg * 64 + 32, [[NTG * 64, 128], [0, 16], [1, 32]]),
                ALU.mult), "v")
            P.op("vector", lambda E: E.tensor_add(rtile[:, :], rcos[:, :], rsin[:, :]), "v")
            s = QSCALE if scale_q else 1.0
            mv = P.op("vector", lambda E, s=s: E.tensor_scalar(
                rcos[:, :], rtile[:, :], s, 0.0, ALU.mult, ALU.min), "v")
            P.wait("scalar", "v", mv)
            ev = P.op("scalar", lambda E: E.activation(out=rsin[:, :], in_=rcos[:, :], func=AF.Exp), "s")
            P.wait("vector", "s", ev)
            if scale_q:
                P.op("vector", lambda E, s=s: E.tensor_scalar(
                    rcos[:, :], rtile[:, :], s, 0.0, ALU.mult, ALU.max), "v")
                fv = P.op("vector", lambda E: E.tensor_add(rtile[:, :], rcos[:, :], rsin[:, :]), "v")
            else:
                fv = P.op("vector", lambda E: E.scalar_tensor_tensor(
                    rtile[:, :], rtile[:, :], 0.0, rsin[:, :], ALU.max, ALU.add), "v")
            return fv

        tr_guard = None
        for tg in range(NTG):
            pb = pB0 if tg % 2 == 0 else pB1
            pb_wait(tg % 2)
            tmk = qkv_mms(0, tg, pb, ("p0", lq))
            P.wait("scalar", "t", tmk)
            fv = rope_elu(tg, pb, True)
            pb_used(tg % 2, s=P.cnt["s"])
            P.wait("tensor", "v", fv)
            if tr_guard is not None:
                P.wait("tensor", "s", tr_guard)
            ttk = None
            for p in range(NP):
                ttk = P.op("tensor", lambda E, p=p: E.transpose(
                    pTR[:, p * 128:(p + 1) * 128], rtile[:, p * 128:(p + 1) * 128],
                    ident[:, :]), "t")
            P.wait("scalar", "t", ttk)
            tr_guard = P.op("scalar", lambda E, tg=tg: E.activation(
                out=bass.AP(arena, tg * 128, [[32768, 128], [TL, NP], [1, 128]]),
                in_=pTR[:, :].rearrange("p (g j) -> p g j", g=NP),
                func=AF.Identity), "s")
        q_pass_t = P.cnt["t"]

        lk = P.dma("gpsimd", PART(1)[:, :].rearrange("p (k n) -> p k n", k=KC),
                   wqkv.ap()[:, 1024:2048].rearrange("(k p) n -> p k n", p=128), "p1")
        P.wait("gpsimd", "t", q_pass_t)
        lv = P.dma("gpsimd", PART(0)[:, :].rearrange("p (k n) -> p k n", k=KC),
                   wqkv.ap()[:, 2048:3072].rearrange("(k p) n -> p k n", p=128), "p0")

        kv_mm = None
        for tg in range(NTG):
            b = 0
            pb_wait(0)
            tmk = qkv_mms(1, tg, pB0, ("p1", lk))
            P.wait("scalar", "t", tmk)
            P.wait("scalar", "v", G.get(("kcp",), 0))
            fv = rope_elu(tg, pB0, False)
            pb_used(0, s=P.cnt["s"])
            kg = G.get(("kft", b))
            if kg is not None:
                P.wait("vector", "t", kg)
            kcp = P.op("vector", lambda E, b=b: E.tensor_copy(kft[b][:, :], rtile[:, :]), "v")
            G[("kcp",)] = kcp
            pb_wait(1)
            tmv = qkv_mms(0, tg, pB1, ("p0", lv))
            P.wait("scalar", "t", tmv)
            vv = P.op("scalar", lambda E, b=b: E.activation(
                out=vt[b][:, :], in_=pB1[:, :], func=AF.Identity), "s")
            pb_used(1, s=vv)
            P.wait("tensor", "v", kcp)
            P.wait("tensor", "s", vv)
            for p in range(NP):
                pdst = pM if p < 4 else pC
                pcol = (p % 4) * 128
                P.op("tensor", lambda E, b=b, p=p, pdst=pdst, pcol=pcol, tg=tg: E.matmul(
                    pdst[:, pcol:pcol + 128], kft[b][:, p * 128:(p + 1) * 128],
                    vt[b][:, p * 128:(p + 1) * 128],
                    start=(tg == 0 and p % 4 == 0), stop=(tg == NTG - 1),
                    skip_group_check=True), "t")
                kv_mm = P.op("tensor", lambda E, b=b, p=p, tg=tg: E.matmul(
                    pSS[:, p:p + 1], kft[b][:, p * 128:(p + 1) * 128], ones_c[:, :],
                    start=(tg == 0 and p == 0), stop=(tg == NTG - 1),
                    skip_group_check=True), "t")
            G[("kft", b)] = kv_mm

        P.wait("vector", "t", None)
        P.op("vector", lambda E: E.tensor_copy(kvx[:, 0:512], pM[:, :]), "v")
        P.op("vector", lambda E: E.tensor_copy(kvx[:, 512:1024], pC[:, :]), "v")
        pk = P.op("vector", lambda E: E.tensor_copy(kvx[:, 1024:1032], pSS[:, 0:8]), "v")
        P.wait("gpsimd", "v", pk)
        P.dma("gpsimd", cc_in.ap(), kvx[:, :], "gd")
        P.wait("gpsimd", "gd", None)
        P.op("gpsimd", lambda E: E.collective_compute(
            "AllReduce", ALU.add,
            replica_groups=[list(range(NCORES))],
            ins=[cc_in.ap().opt()], outs=[cc_out.ap().opt()]), "cc")
        P.wait("gpsimd", "cc", None)
        arb = P.dma("gpsimd", kvx[:, :], cc_out.ap(), "cc2")
        P.wait("vector", "cc2", arb)
        P.op("vector", lambda E: E.memset(ksb[:, :], 0.0), "v")
        for p in range(NP):
            P.op("vector", lambda E, p=p: E.tensor_copy(
                ksb[0:64, p * 16 + 2 * p: p * 16 + 2 * p + 1],
                kvx[0:64, 1024 + p:1025 + p]), "v")
            P.op("vector", lambda E, p=p: E.tensor_copy(
                ksb[64:128, p * 16 + 2 * p + 1: p * 16 + 2 * p + 2],
                kvx[64:128, 1024 + p:1025 + p]), "v")
        kv_ready_v = P.cnt["v"]

        G[("qsc_v",)] = kv_ready_v
        den_s = None
        for tc in range(NQ):
            P.wait("tensor", "v", kv_ready_v)
            if den_s is not None:
                P.wait("tensor", "v", G[("dmax_v",)])
            dmk = None
            for p in range(NP):
                dmk = P.op("tensor", lambda E, p=p, tc=tc: E.matmul(
                    pSS[0:16, :], ksb[:, p * 16:(p + 1) * 16], QF(p, tc * QT, QT),
                    start=(p == 0), stop=(p == NP - 1)), "t")
            P.wait("vector", "t", dmk)
            P.wait("vector", "s", G.get(("dens",), 0))
            dvv = P.op("vector", lambda E: E.tensor_scalar_max(dmax[:, :], pSS[0:16, :], 1e-6), "v")
            G[("dmax_v",)] = dvv
            P.wait("scalar", "v", dvv)
            P.op("scalar", lambda E: E.activation(out=dmax[:, :], in_=dmax[:, :], func=AF.Ln), "s")
            den_s = P.op("scalar", lambda E: E.activation(
                out=denr[:, :], in_=dmax[:, :], func=AF.Exp, scale=-1.0), "s")
            G[("dens",)] = den_s
            for p in range(NP):
                i = (tc * NP + p) % 2
                P.wait("tensor", "s", den_s)
                P.wait("tensor", "v", G.get(("qsc_v",), 0))
                bmk = P.op("tensor", lambda E, p=p: E.matmul(
                    pM[:, :], sel[:, p * 128:(p + 1) * 128],
                    denr[:, :], start=True, stop=True), "t")
                P.wait("vector", "t", bmk)
                qv = P.op("vector", lambda E, p=p, tc=tc: E.tensor_mul(
                    QF(p, tc * QT, QT), QF(p, tc * QT, QT), pM[:, :]), "v")
                G[("qsc_v",)] = qv
                pb = pB0 if i == 0 else pB1
                pb_wait(i)
                P.wait("tensor", "v", qv)
                P.op("tensor", lambda E, p=p, tc=tc, pb=pb: E.matmul(
                    pb[0:64, 0:512], kvx[0:64, p * 128:p * 128 + 64],
                    bass.AP(arena, p * TL + tc * QT, [[32768, 64], [1, QT]]),
                    start=True, stop=True, tile_position=(0, 0)), "t")
                amk = P.op("tensor", lambda E, p=p, tc=tc, pb=pb: E.matmul(
                    pb[64:128, 0:512], kvx[64:128, p * 128 + 64:(p + 1) * 128],
                    bass.AP(arena, 64 * 32768 + p * TL + tc * QT, [[32768, 64], [1, QT]]),
                    start=True, stop=True, tile_position=(64, 64)), "t")
                P.wait("scalar", "t", amk)
                smk = P.op("scalar", lambda E, p=p, tc=tc, pb=pb: E.activation(
                    out=Y1(p, tc * QT, QT), in_=pb[:, 0:512], func=AF.Identity), "s")
                pb_used(i, s=smk)
        at_done_s = P.cnt["s"]

        wo_t = None
        for o in range(KC):
            buf = o % 2
            g = G.get(("wos", buf))
            if g is not None:
                P.wait("gpsimd", "t", g)
            if o < 2:
                P.wait("gpsimd", "t", q_pass_t)
            ls = "wos0" if buf == 0 else "wos1"
            lw = P.dma("gpsimd", WOS(buf)[:, :].rearrange("p (k j) -> p k j", k=KC),
                       wout.ap()[:, o * 128:(o + 1) * 128].rearrange("(k p) j -> p k j", p=128),
                       ls)
            for tc in range(NQ):
                i = (o * NQ + tc) % 2
                pb = pB0 if i == 0 else pB1
                pb_wait(i)
                P.wait("tensor", ls, lw)
                P.wait("tensor", "s", at_done_s)
                tmk = None
                for k in range(KC):
                    tmk = P.op("tensor", lambda E, o=o, tc=tc, k=k, pb=pb, buf=buf: E.matmul(
                        pb[:, 0:512], WOS(buf)[:, k * 128:(k + 1) * 128],
                        Y1(k, tc * QT, QT),
                        start=(k == 0), stop=(k == KC - 1)), "t")
                G[("wos", buf)] = tmk
                wo_t = tmk
                P.wait("vector", "t", tmk)
                vmk = P.op("vector", lambda E, o=o, tc=tc, pb=pb: E.tensor_add(
                    Xv(o, tc * QT, QT), Xv(o, tc * QT, QT), pb[:, 0:512]), "v")
                pb_used(i, v=vmk)

        G[("dst_t",)] = P.cnt["t"]
        G[("pmc_rd_v")] = P.cnt["v"]
        G[("pmc_rd_s")] = P.cnt["s"]
        G[("pss_rd_v")] = P.cnt["v"]
        for bb in range(4):
            G[("w1s", bb)] = wo_t
        G[("w2s", 0)] = wo_t
        G[("w2s", 1)] = wo_t
        G[("hq_gate",)] = wo_t

    import os as _os
    _phases = int(_os.environ.get("MK_PHASES", "3"))

    def dump_X():
        vnow = P.cnt["v"]
        P.wait("gpsimd", "v", vnow)
        for o in range(KC):
            P.dma("gpsimd", out_fm.ap()[o * 128:(o + 1) * 128, :], Xv(o, 0, TL), "gd")

    emit_ffn(first=True)
    if _phases == 1:
        dump_X()
    if _phases >= 2:
        emit_attn()
        if _phases == 2:
            dump_X()
    P.wait("vector", "t", None)
    P.wait("scalar", "t", None)
    emit_ffn(first=False)


    P.wait("gpsimd", "gd", None)
    P.wait("sync", "gd", P.cnt["gd"])
    for s in ("c0", "x0", "w1s0", "w1s1", "w1s2", "w1s3", "w2s0", "w2s1",
              "wbq0", "wbq1", "wbq2", "wbq3", "wbo0", "wbo1",
              "p0", "p1", "wos0", "wos1", "cc2"):
        P.wait("sync", s, None)

    with es:
        for s in list(P.cnt):
            P.handles[s] = es.enter_context(nc.semaphore(s))
        blk = es.enter_context(nc.Block())
        for eng in ENGS:
            stream = P.streams[eng]
            def body(E, stream=stream):
                for fn in stream:
                    fn(E)
            getattr(blk, eng)(body)
    return nc


def _rope_tables(core_id):
    pos = np.arange(core_id * TL, (core_id + 1) * TL, dtype=np.float64)
    inv = ROPE_BASE ** (-np.arange(0, 64, 2, dtype=np.float64) / 64.0)
    fr = pos[:, None] * inv[None, :]
    emb = np.concatenate([fr, fr], axis=1)
    cos = np.cos(emb).astype(np.float32)
    sin = np.sin(emb).astype(np.float32)
    ss = sin.copy()
    ss[:, 0:32] *= -1.0
    to_sb = lambda a: np.ascontiguousarray(
        a.reshape(NTG, 128, 64).transpose(1, 0, 2).reshape(128, NTG * 64)
    ).astype(ml_dtypes.bfloat16)
    return to_sb(cos), to_sb(ss)


def kernel(L0_attn_x, L0_attn_W_qkv, L0_attn_W_out, L0_ffn_ln_gamma,
           L0_ffn_ln_beta, L0_ffn_fc1_q, L0_ffn_fc1_scales, L0_ffn_fc1_bias,
           L0_ffn_fc2_q, L0_ffn_fc2_scales, L0_ffn_fc2_bias):
    x = np.ascontiguousarray(np.asarray(L0_attn_x, dtype=np.float32)[0].T)
    wqkv = np.ascontiguousarray(np.asarray(L0_attn_W_qkv, dtype=np.float32))
    wo = np.ascontiguousarray(np.asarray(L0_attn_W_out, dtype=np.float32))
    gamT = np.ascontiguousarray(np.asarray(L0_ffn_ln_gamma, np.float32).reshape(KC, 128).T)
    betT = np.ascontiguousarray(np.asarray(L0_ffn_ln_beta, np.float32).reshape(KC, 128).T)
    b1T = np.ascontiguousarray(np.asarray(L0_ffn_fc1_bias, np.float32).reshape(MC, 128).T)
    b2T = np.ascontiguousarray(np.asarray(L0_ffn_fc2_bias, np.float32).reshape(KC, 128).T)
    q1 = np.ascontiguousarray(np.asarray(L0_ffn_fc1_q, np.int32)
                              .reshape(KC, 128, MC, 128).transpose(2, 1, 0, 3)
                              .reshape(F4, D))
    s1 = np.ascontiguousarray(np.asarray(L0_ffn_fc1_scales, np.float32)
                              .reshape(KC, MC, 128).transpose(1, 0, 2).reshape(MC, D))
    q2 = np.ascontiguousarray(np.asarray(L0_ffn_fc2_q, np.int32)
                              .reshape(MC, 128, KC, 128).transpose(2, 1, 0, 3)
                              .reshape(D, F4))
    s2 = np.ascontiguousarray(np.asarray(L0_ffn_fc2_scales, np.float32)
                              .reshape(MC, KC, 128).transpose(1, 0, 2).reshape(KC, F4))
    iden = np.eye(128, dtype=ml_dtypes.bfloat16)
    selm = np.zeros((16, 1024), dtype=ml_dtypes.bfloat16)
    for p in range(NP):
        selm[2 * p, p * 128:p * 128 + 64] = 1
        selm[2 * p + 1, p * 128 + 64:(p + 1) * 128] = 1

    nc = build_program()
    in_maps = []
    for c in range(NCORES):
        tc, ts = _rope_tables(c)
        in_maps.append(dict(
            x_fm=np.ascontiguousarray(x[:, c * TL:(c + 1) * TL]),
            wqkv=wqkv, wout=wo, gamT=gamT, betT=betT, b1T=b1T, b2T=b2T,
            q1=q1, s1=s1, q2=q2, s2=s2, ropc=tc, rops=ts, identin=iden, selin=selm,
        ))
    res = run_bass_kernel_spmd(nc, in_maps, core_ids=list(range(NCORES)))
    full = np.concatenate([r["out_fm"] for r in res.results], axis=1)
    return np.ascontiguousarray(full.T)[None, :, :]


# version 06
# speedup vs baseline: 1.2205x; 1.0134x over previous
import numpy as np
import ml_dtypes
from contextlib import ExitStack

import concourse.bass as bass
import concourse.mybir as mybir
from concourse.bass_utils import run_bass_kernel_spmd

F32 = mybir.dt.float32
BF16 = mybir.dt.bfloat16
I32 = mybir.dt.int32
AF = mybir.ActivationFunctionType
ALU = mybir.AluOpType

NCORES = 8
D = 1024
TT = 16384
TL = TT // NCORES
KC = 8
F4 = 4096
MC = 32
NQ = 4
QT = TL // NQ
NTG = TL // 128
NP = 8
EPS = 1e-5
ROPE_BASE = 500000.0
QSCALE = 64.0 ** -0.5

ENGS = ("sync", "gpsimd", "tensor", "vector", "scalar")


class Plan:

    def __init__(self):
        self.streams = {e: [] for e in ENGS}
        self.cnt = {}
        self.handles = {}

    def new_sem(self, name):
        self.cnt[name] = 0

    def emit(self, eng, fn):
        self.streams[eng].append(fn)

    def wait(self, eng, sem, val=None):
        v = self.cnt[sem] if val is None else val
        if v is None or v <= 0:
            return
        self.emit(eng, lambda E, s=sem, v=v: E.wait_ge(self.handles[s], v))

    def op(self, eng, fn, sem=None, k=1):
        if sem is None:
            self.emit(eng, fn)
            return None
        self.cnt[sem] += k
        self.emit(eng, lambda E, fn=fn, s=sem, k=k: fn(E).then_inc(self.handles[s], k))
        return self.cnt[sem]

    def dma(self, eng, out, in_, sem):
        return self.op(eng, lambda E, o=out, i=in_: E.dma_start(out=o, in_=i), sem, 16)


def build_program():
    nc = bass.Bass("TRN2", target_bir_lowering=False)
    P = Plan()
    for s in ("sd", "gd", "t", "v", "s", "cc", "c0", "x0",
              "w1s0", "w1s1", "w1s2", "w1s3", "w2s0", "w2s1",
              "wbq0", "wbq1", "wbq2", "wbq3", "wbo0", "wbo1",
              "p0", "p1", "wos0", "wos1", "cc2"):
        P.new_sem(s)

    x_fm = nc.dram_tensor("x_fm", [D, TL], F32, kind="ExternalInput")
    wqkv = nc.dram_tensor("wqkv", [D, 3 * D], F32, kind="ExternalInput")
    wout = nc.dram_tensor("wout", [D, D], F32, kind="ExternalInput")
    gamT = nc.dram_tensor("gamT", [128, KC], F32, kind="ExternalInput")
    betT = nc.dram_tensor("betT", [128, KC], F32, kind="ExternalInput")
    b1T = nc.dram_tensor("b1T", [128, MC], F32, kind="ExternalInput")
    b2T = nc.dram_tensor("b2T", [128, KC], F32, kind="ExternalInput")
    q1 = nc.dram_tensor("q1", [F4, D], I32, kind="ExternalInput")
    s1 = nc.dram_tensor("s1", [MC, D], F32, kind="ExternalInput")
    q2 = nc.dram_tensor("q2", [D, F4], I32, kind="ExternalInput")
    s2 = nc.dram_tensor("s2", [KC, F4], F32, kind="ExternalInput")
    ropc = nc.dram_tensor("ropc", [128, NTG * 64], BF16, kind="ExternalInput")
    rops = nc.dram_tensor("rops", [128, NTG * 64], BF16, kind="ExternalInput")
    identin = nc.dram_tensor("identin", [128, 128], BF16, kind="ExternalInput")
    selin = nc.dram_tensor("selin", [16, 1024], BF16, kind="ExternalInput")
    out_fm = nc.dram_tensor("out_fm", [D, TL], F32, kind="ExternalOutput")

    w1c = nc.dram_tensor("w1c", [F4, D], BF16, kind="Internal")
    w2c = nc.dram_tensor("w2c", [D, F4], BF16, kind="Internal")
    cc_in = nc.dram_tensor("cc_in", [128, 1032], BF16, kind="Internal")
    cc_out = nc.dram_tensor("cc_out", [128, 1032], BF16, kind="Internal")

    es = ExitStack()
    sb = lambda name, shape, dt: es.enter_context(nc.sbuf_tensor(name, shape, dt))
    ps = lambda name, shape, dt: es.enter_context(nc.psum_tensor(name, shape, dt))

    X = sb("X", [128, KC * TL], F32)
    arena = sb("arena", [128, 32768], BF16)
    wbuf = sb("wbuf", [128, 18432], BF16)
    ovl = sb("ovl", [128, 8192], BF16)
    rtile = ovl[:, 0:1024]
    rcos = ovl[:, 1024:2048]
    rsin = ovl[:, 2048:3072]
    kft = [ovl[:, 3072:4096]]
    vt = [ovl[:, 4096:5120]]
    denr = ovl[0:16, 5120:5632]
    ovf = ovl[:, :].bitcast(F32)
    dmax = ovf[0:16, 2816:3328]
    PQr = [(ovf[0:1, qq * 512:(qq + 1) * 512],
            ovf[0:1, 2048 + qq * 512: 2048 + (qq + 1) * 512]) for qq in range(4)]
    kvx = sb("kvx", [128, 1032], BF16)
    ksb = sb("ksb", [128, 128], BF16)
    sel = sb("sel", [16, 1024], BF16)
    mir = [sb("mir0", [128, QT], BF16)]
    sqt = [sb("sqt0", [128, QT], BF16)]
    tb1 = sb("tb1", [128, QT], BF16)
    tb2 = sb("tb2", [128, QT], BF16)
    z4t = sb("z4t", [128, QT], F32)
    rows = sb("rows", [1, 5 * QT], F32)
    sgam = sb("sgam", [128, KC], F32)
    sbet = sb("sbet", [128, KC], F32)
    sb1t = sb("sb1t", [128, MC], F32)
    sqb2 = sb("sqb2", [128, KC], F32)
    tabc = sb("tabc", [128, NTG * 64], BF16)
    tabs = sb("tabs", [128, NTG * 64], BF16)
    ident = sb("ident", [128, 128], BF16)
    ones_c = sb("ones_c", [128, 1], BF16)
    ones_r = sb("ones_r", [1, 128], F32)

    R_M, R_C, R_A1, R_P, R_Q = (rows[0:1, i * QT:(i + 1) * QT] for i in range(5))
    R_TS = R_C

    pB0 = ps("pB0", [128, 1024], F32)
    pB1 = ps("pB1", [128, 1024], F32)
    pSS = ps("pSS", [128, 512], F32)
    pM = ps("pM", [128, 512], F32)
    pC = ps("pC", [128, 512], F32)
    pTR = ps("pTR", [128, 1024], BF16)

    def Xv(k, c0, n):
        return X[:, k * TL + c0: k * TL + c0 + n]

    def Av(m):
        return arena[:, m * QT:(m + 1) * QT]

    def Hv(k, q):
        return arena[:, 16384 + k * TL + q * QT: 16384 + k * TL + q * QT + QT]

    def QF(p, c0, n):
        return arena[:, p * TL + c0: p * TL + c0 + n]

    def Y1(k, c0, n):
        return arena[:, 16384 + k * TL + c0: 16384 + k * TL + c0 + n]

    def W1S(buf):
        return wbuf[:, buf * 1024:(buf + 1) * 1024]

    def SCB(buf):
        return wbuf[:, 4096 + buf * 1024: 4096 + (buf + 1) * 1024]

    def W2S(buf):
        return wbuf[:, 6144 + buf * 4096: 6144 + (buf + 1) * 4096]

    SC2B = wbuf[:, 14336:18432]

    def PART(slot):
        return wbuf[:, slot * 8192:(slot + 1) * 8192]

    def WOS(buf):
        return wbuf[:, 16384 + buf * 1024: 16384 + (buf + 1) * 1024]

    G = {}

    def pb_wait(buf):
        P.wait("tensor", "v", G.get(("pbv", buf), 0))
        P.wait("tensor", "s", G.get(("pbs", buf), 0))

    def pb_used(buf, v=None, s=None):
        if v is not None:
            G[("pbv", buf)] = v
        if s is not None:
            G[("pbs", buf)] = s

    P.dma("gpsimd", sgam[:, :], gamT.ap(), "c0")
    P.dma("gpsimd", sbet[:, :], betT.ap(), "c0")
    P.dma("gpsimd", sb1t[:, :], b1T.ap(), "c0")
    P.dma("gpsimd", sqb2[:, :], b2T.ap(), "c0")
    P.dma("gpsimd", tabc[:, :], ropc.ap(), "c0")
    P.dma("gpsimd", tabs[:, :], rops.ap(), "c0")
    P.dma("gpsimd", ident[:, :], identin.ap(), "c0")
    P.dma("gpsimd", sel[:, :], selin.ap(), "c0")
    const_ld = P.cnt["c0"]

    xld = [P.dma("gpsimd", Xv(k, 0, TL), x_fm.ap()[k * 128:(k + 1) * 128, :], "x0")
           for k in range(KC)]

    P.op("vector", lambda E: E.memset(ones_c[:, :], 1.0), "v")
    P.op("vector", lambda E: E.memset(ones_r[:, :], 1.0), "v")
    P.wait("scalar", "c0", const_ld)
    P.op("scalar", lambda E: E.mul(sqb2[:, :], sqb2[:, :], 0.25), "s")

    def emit_ln_quarter(t0, mode, dst, pq=None):
        for k in range(KC):
            b = 0
            P.wait("vector", "x0", xld[-1])
            pv = G.get(("mir", b))
            if pv is not None:
                P.wait("vector", "t", pv)
            P.op("vector", lambda E, k=k, b=b, t0=t0: E.tensor_copy(
                mir[b][:, :], Xv(k, t0, QT)), "v")
            mv = P.op("vector", lambda E, b=b: E.tensor_mul(
                sqt[b][:, :], mir[b][:, :], mir[b][:, :]), "v")
            P.wait("tensor", "v", mv)
            if k == 0:
                P.wait("tensor", "v", G.get(("pss_rd_v"), 0))
                P.wait("tensor", "s", G.get(("pss_rd_s"), 0))
            P.op("tensor", lambda E, b=b, k=k: E.matmul(
                pSS[0:1, :], ones_c[:, :], mir[b][:, :],
                start=(k == 0), stop=(k == KC - 1)), "t")
            tm = P.op("tensor", lambda E, b=b, k=k: E.matmul(
                pSS[32:33, :], ones_c[:, :], sqt[b][:, :],
                start=(k == 0), stop=(k == KC - 1), tile_position=(0, 32)), "t")
            G[("mir", b)] = tm
        P.wait("vector", "t", None)
        P.wait("vector", "t", G.get(("rows_t",), 0))
        P.op("vector", lambda E: E.tensor_scalar_mul(R_M, pSS[0:1, :], -1.0 / D), "v")
        P.op("vector", lambda E: E.tensor_mul(R_TS, R_M, R_M), "v")
        P.op("vector", lambda E: E.scalar_tensor_tensor(
            R_TS, pSS[32:33, :], 1.0 / D, R_TS, ALU.mult, ALU.subtract), "v")
        G[("pss_rd_v")] = P.cnt["v"]
        cv = P.op("vector", lambda E: E.tensor_scalar(
            R_A1, R_TS, EPS, None, ALU.add), "v")
        P.wait("scalar", "v", cv)
        P.op("scalar", lambda E: E.activation(out=R_A1, in_=R_A1, func=AF.Ln), "s")
        ssn = P.op("scalar", lambda E: E.activation(out=R_A1, in_=R_A1, func=AF.Exp, scale=-0.5), "s")
        if mode == "ffn":
            cv2 = P.op("vector", lambda E: E.tensor_scalar(
                R_C, R_TS, 1.0 + EPS, EPS * EPS, ALU.mult, ALU.add), "v")
            P.wait("scalar", "v", cv2)
            P.op("scalar", lambda E: E.activation(out=R_C, in_=R_C, func=AF.Ln), "s")
            ssn = P.op("scalar", lambda E: E.activation(out=R_C, in_=R_C, func=AF.Exp, scale=-0.5), "s")
            P.wait("vector", "s", ssn)
            P.wait("vector", "t", G.get(("pq_t", t0 // QT), 0))
            P.op("vector", lambda E, pq=pq: E.tensor_scalar(pq[0], R_A1, 0.5, 1.0, ALU.mult, ALU.add), "v")
            P.op("vector", lambda E, pq=pq: E.scalar_tensor_tensor(pq[1], R_M, 0.5, R_A1, ALU.mult, ALU.mult), "v")
        rows_v, rows_s = P.cnt["v"], P.cnt["s"]
        P.wait("tensor", "v", rows_v)
        P.wait("tensor", "s", rows_s)
        P.wait("tensor", "v", G.get(("pmc_rd_v"), 0))
        P.wait("tensor", "s", G.get(("pmc_rd_s"), 0))
        P.op("tensor", lambda E: E.matmul(pM[:, :], ones_r[:, :], R_M, start=True, stop=True), "t")
        src2 = R_C if mode == "ffn" else R_A1
        bt = P.op("tensor", lambda E, s2=src2: E.matmul(pC[:, :], ones_r[:, :], s2, start=True, stop=True), "t")
        G[("rows_t",)] = bt
        P.wait("vector", "t", bt)
        P.wait("vector", "t", G.get(("dst_t",), 0))
        P.wait("scalar", "t", G.get(("dst_t",), 0))
        for k in range(KC):
            pv = G.get(("tb_s"))
            if pv is not None and mode == "ffn":
                P.wait("vector", "s", pv)
            P.op("vector", lambda E, k=k, t0=t0: E.tensor_add(
                tb1[:, :], Xv(k, t0, QT), pM[:, :]), "v")
            if mode == "ffn":
                tv = P.op("vector", lambda E: E.tensor_mul(tb2[:, :], tb1[:, :], pC[:, :]), "v")
                P.wait("scalar", "v", tv)
                smk = P.op("scalar", lambda E, k=k: E.activation(
                    out=dst(k), in_=tb2[:, :], func=AF.Identity,
                    bias=sbet[:, k:k + 1], scale=sgam[:, k:k + 1]), "s")
                G[("tb_s")] = smk
            else:
                P.op("vector", lambda E, k=k: E.tensor_mul(dst(k), tb1[:, :], pC[:, :]), "v")
        G[("pmc_rd_v")] = P.cnt["v"]
        G[("pmc_rd_s")] = P.cnt["s"]
        return P.cnt["v"], P.cnt["s"]

    def emit_ffn(first):
        h_marks = []
        for tc in range(NQ):
            h_marks.append(emit_ln_quarter(
                tc * QT, "ffn", lambda k, tc=tc: Hv(k, tc), pq=PQr[tc]))
        for q in range(NQ):
            t0 = q * QT
            h_v, h_s = h_marks[q]
            for m in range(MC):
                buf = m % 4
                ls = f"w1s{buf}"
                if first and q == 0:
                    scb = m % 2
                    P.wait("gpsimd", "t", G.get(("w1s", buf), 0))
                    P.wait("gpsimd", f"wbq{buf}", G.get(("wbq", buf), 0))
                    P.wait("gpsimd", "v", G.get(("scb", scb), 0))
                    P.dma("gpsimd", W1S(buf)[:, :],
                          q1.ap()[m * 128:(m + 1) * 128, :], ls)
                    lsc = P.dma("gpsimd", SCB(scb)[:, :],
                                bass.AP(s1, m * D, [[0, 128], [1, D]]), ls)
                    P.wait("vector", ls, lsc)
                    stt = P.op("vector", lambda E, buf=buf, scb=scb: E.scalar_tensor_tensor(
                        W1S(buf)[:, :], W1S(buf)[:, :], -128.0, SCB(scb)[:, :],
                        ALU.add, ALU.mult), "v")
                    G[("scb", scb)] = stt
                    P.wait("sync", "v", stt)
                    wbk = P.dma("sync", w1c.ap()[m * 128:(m + 1) * 128, :],
                                W1S(buf)[:, :], f"wbq{buf}")
                    G[("wbq", buf)] = wbk
                    w_ready = ("v", stt)
                else:
                    g = G.get(("w1s", buf))
                    if g is not None:
                        P.wait("sync", "t", g)
                    if first and q == 1 and m < 4:
                        for wb in range(4):
                            P.wait("sync", f"wbq{wb}", G.get(("wbq", wb), 0))
                    lw = P.dma("sync", W1S(buf)[:, :],
                               w1c.ap()[m * 128:(m + 1) * 128, :], ls)
                    w_ready = (ls, lw)
                pb = pB0 if m % 2 == 0 else pB1
                pb_wait(m % 2)
                P.wait("tensor", w_ready[0], w_ready[1])
                P.wait("tensor", "v", h_v)
                P.wait("tensor", "s", h_s)
                tmk = None
                for k in range(KC):
                    tmk = P.op("tensor", lambda E, pb=pb, buf=buf, k=k, q=q: E.matmul(
                        pb[:, 0:512], W1S(buf)[:, k * 128:(k + 1) * 128], Hv(k, q),
                        start=(k == 0), stop=(k == KC - 1)), "t")
                G[("w1s", buf)] = tmk
                P.wait("scalar", "t", tmk)
                smk = P.op("scalar", lambda E, pb=pb, m=m: E.activation(
                    out=Av(m), in_=pb[:, 0:512], func=AF.Silu,
                    bias=sb1t[:, m:m + 1], scale=1.0), "s")
                pb_used(m % 2, s=smk)
            G[("dst_t",)] = P.cnt["t"]
            a_ready_s = P.cnt["s"]
            pq_v = h_marks[q][0]
            P.wait("tensor", "v", pq_v)
            P.wait("tensor", "v", G.get(("pmc_rd_v"), 0))
            P.op("tensor", lambda E, q=q: E.matmul(pM[:, :], ones_r[:, :], PQr[q][0], start=True, stop=True), "t")
            pqb = P.op("tensor", lambda E, q=q: E.matmul(pC[:, :], ones_r[:, :], PQr[q][1], start=True, stop=True), "t")
            G[("pq_t", q)] = pqb
            G[("pmc_rd_v")] = 0
            G[("pmc_rd_s")] = 0
            for o in range(KC):
                buf = o % 2
                ls = "w2s0" if buf == 0 else "w2s1"
                if first and q == 0:
                    P.wait("gpsimd", "t", G.get(("w2s", buf), 0))
                    P.wait("gpsimd", f"wbo{buf}", G.get(("wbo", buf), 0))
                    P.wait("gpsimd", "v", G.get(("sc2",), 0))
                    P.dma("gpsimd", W2S(buf)[:, :],
                          q2.ap()[o * 128:(o + 1) * 128, :], ls)
                    lsc = P.dma("gpsimd", SC2B[:, :],
                                bass.AP(s2, o * F4, [[0, 128], [1, F4]]), ls)
                    P.wait("vector", ls, lsc)
                    stt = P.op("vector", lambda E, buf=buf: E.scalar_tensor_tensor(
                        W2S(buf)[:, :], W2S(buf)[:, :], -128.0, SC2B[:, :],
                        ALU.add, ALU.mult), "v")
                    G[("sc2",)] = stt
                    P.wait("sync", "v", stt)
                    wbk = P.dma("sync", w2c.ap()[o * 128:(o + 1) * 128, :],
                                W2S(buf)[:, :], f"wbo{buf}")
                    G[("wbo", buf)] = wbk
                    w_ready = ("v", stt)
                else:
                    g = G.get(("w2s", buf))
                    if g is not None:
                        P.wait("sync", "t", g)
                    if first and q == 1 and o < 2:
                        for wb in range(2):
                            P.wait("sync", f"wbo{wb}", G.get(("wbo", wb), 0))
                    lw = P.dma("sync", W2S(buf)[:, :],
                               w2c.ap()[o * 128:(o + 1) * 128, :], ls)
                    w_ready = (ls, lw)
                pb = pB0 if buf == 0 else pB1
                pb_wait(buf)
                P.wait("tensor", w_ready[0], w_ready[1])
                P.wait("tensor", "s", a_ready_s)
                tmk = None
                for k in range(MC):
                    tmk = P.op("tensor", lambda E, pb=pb, buf=buf, k=k: E.matmul(
                        pb[:, 0:512], W2S(buf)[:, k * 128:(k + 1) * 128], Av(k),
                        start=(k == 0), stop=(k == MC - 1)), "t")
                G[("w2s", buf)] = tmk
                zg = G.get(("z4_v"))
                if zg is not None:
                    P.wait("scalar", "v", zg)
                P.wait("scalar", "t", tmk)
                zs = P.op("scalar", lambda E, pb=pb, o=o: E.activation(
                    out=z4t[:, :], in_=pb[:, 0:512], func=AF.Identity,
                    bias=sqb2[:, o:o + 1], scale=0.25), "s")
                pb_used(buf, s=zs)
                P.wait("vector", "t", pqb)
                P.wait("vector", "s", zs)
                P.op("vector", lambda E, o=o, t0=t0: E.tensor_mul(
                    Xv(o, t0, QT), Xv(o, t0, QT), pM[:, :]), "v")
                zv = P.op("vector", lambda E, o=o, t0=t0: E.tensor_add(
                    Xv(o, t0, QT), Xv(o, t0, QT), z4t[:, :]), "v")
                G[("z4_v")] = zv
                vmk = P.op("vector", lambda E, o=o, t0=t0: E.tensor_add(
                    Xv(o, t0, QT), Xv(o, t0, QT), pC[:, :]), "v")
                G[("pmc_rd_v")] = vmk
                if not first:
                    P.wait("gpsimd", "v", vmk)
                    P.dma("gpsimd", out_fm.ap()[o * 128:(o + 1) * 128, t0:t0 + QT],
                          Xv(o, t0, QT), "gd")

    def emit_attn():
        ffn1_t = P.cnt["t"]

        y1_marks = []
        for tc in range(NQ):
            y1_marks.append(emit_ln_quarter(
                tc * QT, "attn", lambda k, tc=tc: Y1(k, tc * QT, QT)))

        P.wait("gpsimd", "t", ffn1_t)
        for wb in range(4):
            P.wait("gpsimd", f"wbq{wb}", None)
        for wb in range(2):
            P.wait("gpsimd", f"wbo{wb}", None)
        lq = P.dma("gpsimd", PART(0)[:, :].rearrange("p (k n) -> p k n", k=KC),
                   wqkv.ap()[:, 0:1024].rearrange("(k p) n -> p k n", p=128), "p0")

        def qkv_mms(part_slot, tg, pbank, pmark):
            mvv, mss = y1_marks[tg // 4]
            P.wait("tensor", "v", mvv)
            P.wait("tensor", pmark[0], pmark[1])
            tmk = None
            for h in range(2):
                for k in range(KC):
                    tmk = P.op("tensor", lambda E, pbank=pbank, k=k, h=h, tg=tg,
                               ps=part_slot: E.matmul(
                        pbank[:, h * 512:(h + 1) * 512],
                        Y1(k, (tg // 4) * QT + (tg % 4) * 128, 128),
                        PART(ps)[:, k * 1024 + h * 512: k * 1024 + (h + 1) * 512],
                        start=(k == 0), stop=(k == KC - 1)), "t")
            return tmk

        def rope_elu(tg, pbank, scale_q):
            sc = P.op("scalar", lambda E, pb=pbank: E.activation(
                out=rtile, in_=pb[:, :], func=AF.Identity), "s")
            P.wait("vector", "s", sc)
            P.op("vector", lambda E, tg=tg: E.tensor_tensor(
                rcos.rearrange("p (g j) -> p g j", g=16),
                rtile.rearrange("p (g j) -> p g j", g=16),
                bass.AP(tabc, tg * 64, [[NTG * 64, 128], [0, 16], [1, 64]]),
                ALU.mult), "v")
            P.op("vector", lambda E, tg=tg: E.tensor_tensor(
                bass.AP(ovl, 2048, [[8192, 128], [64, 16], [1, 32]]),
                bass.AP(ovl, 32, [[8192, 128], [64, 16], [1, 32]]),
                bass.AP(tabs, tg * 64, [[NTG * 64, 128], [0, 16], [1, 32]]),
                ALU.mult), "v")
            P.op("vector", lambda E, tg=tg: E.tensor_tensor(
                bass.AP(ovl, 2048 + 32, [[8192, 128], [64, 16], [1, 32]]),
                bass.AP(ovl, 0, [[8192, 128], [64, 16], [1, 32]]),
                bass.AP(tabs, tg * 64 + 32, [[NTG * 64, 128], [0, 16], [1, 32]]),
                ALU.mult), "v")
            P.op("vector", lambda E: E.tensor_add(rtile, rcos, rsin), "v")
            s = QSCALE if scale_q else 1.0
            mv = P.op("vector", lambda E, s=s: E.tensor_scalar(
                rcos, rtile, s, 0.0, ALU.mult, ALU.min), "v")
            P.wait("scalar", "v", mv)
            ev = P.op("scalar", lambda E: E.activation(out=rsin, in_=rcos, func=AF.Exp), "s")
            P.wait("vector", "s", ev)
            if scale_q:
                P.op("vector", lambda E, s=s: E.tensor_scalar(
                    rcos, rtile, s, 0.0, ALU.mult, ALU.max), "v")
                fv = P.op("vector", lambda E: E.tensor_add(rtile, rcos, rsin), "v")
            else:
                fv = P.op("vector", lambda E: E.scalar_tensor_tensor(
                    rtile, rtile, 0.0, rsin, ALU.max, ALU.add), "v")
            return fv

        tr_guard = None
        for tg in range(NTG):
            pb = pB0 if tg % 2 == 0 else pB1
            pb_wait(tg % 2)
            tmk = qkv_mms(0, tg, pb, ("p0", lq))
            P.wait("scalar", "t", tmk)
            fv = rope_elu(tg, pb, True)
            pb_used(tg % 2, s=P.cnt["s"])
            P.wait("tensor", "v", fv)
            if tr_guard is not None:
                P.wait("tensor", "s", tr_guard)
            ttk = None
            for p in range(NP):
                ttk = P.op("tensor", lambda E, p=p: E.transpose(
                    pTR[:, p * 128:(p + 1) * 128], rtile[:, p * 128:(p + 1) * 128],
                    ident[:, :]), "t")
            P.wait("scalar", "t", ttk)
            tr_guard = P.op("scalar", lambda E, tg=tg: E.activation(
                out=bass.AP(arena, tg * 128, [[32768, 128], [TL, NP], [1, 128]]),
                in_=pTR[:, :].rearrange("p (g j) -> p g j", g=NP),
                func=AF.Identity), "s")
        q_pass_t = P.cnt["t"]

        lk = P.dma("gpsimd", PART(1)[:, :].rearrange("p (k n) -> p k n", k=KC),
                   wqkv.ap()[:, 1024:2048].rearrange("(k p) n -> p k n", p=128), "p1")
        P.wait("gpsimd", "t", q_pass_t)
        lv = P.dma("gpsimd", PART(0)[:, :].rearrange("p (k n) -> p k n", k=KC),
                   wqkv.ap()[:, 2048:3072].rearrange("(k p) n -> p k n", p=128), "p0")

        kv_mm = None
        for tg in range(NTG):
            b = 0
            pb_wait(0)
            tmk = qkv_mms(1, tg, pB0, ("p1", lk))
            P.wait("scalar", "t", tmk)
            P.wait("scalar", "v", G.get(("kcp",), 0))
            fv = rope_elu(tg, pB0, False)
            pb_used(0, s=P.cnt["s"])
            kg = G.get(("kft", b))
            if kg is not None:
                P.wait("vector", "t", kg)
            kcp = P.op("vector", lambda E, b=b: E.tensor_copy(kft[b], rtile), "v")
            G[("kcp",)] = kcp
            pb_wait(1)
            tmv = qkv_mms(0, tg, pB1, ("p0", lv))
            P.wait("scalar", "t", tmv)
            vv = P.op("scalar", lambda E, b=b: E.activation(
                out=vt[b], in_=pB1[:, :], func=AF.Identity), "s")
            pb_used(1, s=vv)
            P.wait("tensor", "v", kcp)
            P.wait("tensor", "s", vv)
            for p in range(NP):
                pdst = pM if p < 4 else pC
                pcol = (p % 4) * 128
                P.op("tensor", lambda E, b=b, p=p, pdst=pdst, pcol=pcol, tg=tg: E.matmul(
                    pdst[:, pcol:pcol + 128], kft[b][:, p * 128:(p + 1) * 128],
                    vt[b][:, p * 128:(p + 1) * 128],
                    start=(tg == 0 and p % 4 == 0), stop=(tg == NTG - 1),
                    skip_group_check=True), "t")
                kv_mm = P.op("tensor", lambda E, b=b, p=p, tg=tg: E.matmul(
                    pSS[:, p:p + 1], kft[b][:, p * 128:(p + 1) * 128], ones_c[:, :],
                    start=(tg == 0 and p == 0), stop=(tg == NTG - 1),
                    skip_group_check=True), "t")
            G[("kft", b)] = kv_mm

        P.wait("vector", "t", None)
        P.op("vector", lambda E: E.tensor_copy(kvx[:, 0:512], pM[:, :]), "v")
        P.op("vector", lambda E: E.tensor_copy(kvx[:, 512:1024], pC[:, :]), "v")
        pk = P.op("vector", lambda E: E.tensor_copy(kvx[:, 1024:1032], pSS[:, 0:8]), "v")
        P.wait("gpsimd", "v", pk)
        P.dma("gpsimd", cc_in.ap(), kvx[:, :], "gd")
        P.wait("gpsimd", "gd", None)
        P.op("gpsimd", lambda E: E.collective_compute(
            "AllReduce", ALU.add,
            replica_groups=[list(range(NCORES))],
            ins=[cc_in.ap().opt()], outs=[cc_out.ap().opt()]), "cc")
        P.wait("gpsimd", "cc", None)
        arb = P.dma("gpsimd", kvx[:, :], cc_out.ap(), "cc2")
        P.wait("vector", "cc2", arb)
        P.op("vector", lambda E: E.memset(ksb[:, :], 0.0), "v")
        for p in range(NP):
            P.op("vector", lambda E, p=p: E.tensor_copy(
                ksb[0:64, p * 16 + 2 * p: p * 16 + 2 * p + 1],
                kvx[0:64, 1024 + p:1025 + p]), "v")
            P.op("vector", lambda E, p=p: E.tensor_copy(
                ksb[64:128, p * 16 + 2 * p + 1: p * 16 + 2 * p + 2],
                kvx[64:128, 1024 + p:1025 + p]), "v")
        kv_ready_v = P.cnt["v"]

        G[("qsc_v",)] = kv_ready_v
        den_s = None
        for tc in range(NQ):
            P.wait("tensor", "v", kv_ready_v)
            if den_s is not None:
                P.wait("tensor", "v", G[("dmax_v",)])
            dmk = None
            for p in range(NP):
                dmk = P.op("tensor", lambda E, p=p, tc=tc: E.matmul(
                    pSS[0:16, :], ksb[:, p * 16:(p + 1) * 16], QF(p, tc * QT, QT),
                    start=(p == 0), stop=(p == NP - 1)), "t")
            P.wait("vector", "t", dmk)
            P.wait("vector", "s", G.get(("dens",), 0))
            dvv = P.op("vector", lambda E: E.tensor_scalar_max(dmax, pSS[0:16, :], 1e-6), "v")
            G[("dmax_v",)] = dvv
            P.wait("scalar", "v", dvv)
            P.op("scalar", lambda E: E.activation(out=dmax, in_=dmax, func=AF.Ln), "s")
            den_s = P.op("scalar", lambda E: E.activation(
                out=denr, in_=dmax, func=AF.Exp, scale=-1.0), "s")
            G[("dens",)] = den_s
            for p in range(NP):
                i = (tc * NP + p) % 2
                P.wait("tensor", "s", den_s)
                P.wait("tensor", "v", G.get(("qsc_v",), 0))
                bmk = P.op("tensor", lambda E, p=p: E.matmul(
                    pM[:, :], sel[:, p * 128:(p + 1) * 128],
                    denr, start=True, stop=True), "t")
                P.wait("vector", "t", bmk)
                qv = P.op("vector", lambda E, p=p, tc=tc: E.tensor_mul(
                    QF(p, tc * QT, QT), QF(p, tc * QT, QT), pM[:, :]), "v")
                G[("qsc_v",)] = qv
                pb = pB0 if i == 0 else pB1
                pb_wait(i)
                P.wait("tensor", "v", qv)
                P.op("tensor", lambda E, p=p, tc=tc, pb=pb: E.matmul(
                    pb[0:64, 0:512], kvx[0:64, p * 128:p * 128 + 64],
                    bass.AP(arena, p * TL + tc * QT, [[32768, 64], [1, QT]]),
                    start=True, stop=True, tile_position=(0, 0)), "t")
                amk = P.op("tensor", lambda E, p=p, tc=tc, pb=pb: E.matmul(
                    pb[64:128, 0:512], kvx[64:128, p * 128 + 64:(p + 1) * 128],
                    bass.AP(arena, 64 * 32768 + p * TL + tc * QT, [[32768, 64], [1, QT]]),
                    start=True, stop=True, tile_position=(64, 64)), "t")
                P.wait("scalar", "t", amk)
                smk = P.op("scalar", lambda E, p=p, tc=tc, pb=pb: E.activation(
                    out=Y1(p, tc * QT, QT), in_=pb[:, 0:512], func=AF.Identity), "s")
                pb_used(i, s=smk)
        at_done_s = P.cnt["s"]

        wo_t = None
        for o in range(KC):
            buf = o % 2
            g = G.get(("wos", buf))
            if g is not None:
                P.wait("gpsimd", "t", g)
            if o < 2:
                P.wait("gpsimd", "t", q_pass_t)
            ls = "wos0" if buf == 0 else "wos1"
            lw = P.dma("gpsimd", WOS(buf)[:, :].rearrange("p (k j) -> p k j", k=KC),
                       wout.ap()[:, o * 128:(o + 1) * 128].rearrange("(k p) j -> p k j", p=128),
                       ls)
            for tc in range(NQ):
                i = (o * NQ + tc) % 2
                pb = pB0 if i == 0 else pB1
                pb_wait(i)
                P.wait("tensor", ls, lw)
                P.wait("tensor", "s", at_done_s)
                tmk = None
                for k in range(KC):
                    tmk = P.op("tensor", lambda E, o=o, tc=tc, k=k, pb=pb, buf=buf: E.matmul(
                        pb[:, 0:512], WOS(buf)[:, k * 128:(k + 1) * 128],
                        Y1(k, tc * QT, QT),
                        start=(k == 0), stop=(k == KC - 1)), "t")
                G[("wos", buf)] = tmk
                wo_t = tmk
                P.wait("vector", "t", tmk)
                vmk = P.op("vector", lambda E, o=o, tc=tc, pb=pb: E.tensor_add(
                    Xv(o, tc * QT, QT), Xv(o, tc * QT, QT), pb[:, 0:512]), "v")
                pb_used(i, v=vmk)

        G[("dst_t",)] = P.cnt["t"]
        G[("pmc_rd_v")] = P.cnt["v"]
        G[("pmc_rd_s")] = P.cnt["s"]
        G[("pss_rd_v")] = P.cnt["v"]
        for bb in range(4):
            G[("w1s", bb)] = wo_t
        G[("w2s", 0)] = wo_t
        G[("w2s", 1)] = wo_t
        G[("hq_gate",)] = wo_t

    import os as _os
    _phases = int(_os.environ.get("MK_PHASES", "3"))

    def dump_X():
        vnow = P.cnt["v"]
        P.wait("gpsimd", "v", vnow)
        for o in range(KC):
            P.dma("gpsimd", out_fm.ap()[o * 128:(o + 1) * 128, :], Xv(o, 0, TL), "gd")

    emit_ffn(first=True)
    if _phases == 1:
        dump_X()
    if _phases >= 2:
        emit_attn()
        if _phases == 2:
            dump_X()
    P.wait("vector", "t", None)
    P.wait("scalar", "t", None)
    emit_ffn(first=False)


    P.wait("gpsimd", "gd", None)
    P.wait("sync", "gd", P.cnt["gd"])
    for s in ("c0", "x0", "w1s0", "w1s1", "w1s2", "w1s3", "w2s0", "w2s1",
              "wbq0", "wbq1", "wbq2", "wbq3", "wbo0", "wbo1",
              "p0", "p1", "wos0", "wos1", "cc2"):
        P.wait("sync", s, None)

    with es:
        for s in list(P.cnt):
            P.handles[s] = es.enter_context(nc.semaphore(s))
        blk = es.enter_context(nc.Block())
        for eng in ENGS:
            stream = P.streams[eng]
            def body(E, stream=stream):
                for fn in stream:
                    fn(E)
            getattr(blk, eng)(body)
    return nc


def _rope_tables(core_id):
    pos = np.arange(core_id * TL, (core_id + 1) * TL, dtype=np.float64)
    inv = ROPE_BASE ** (-np.arange(0, 64, 2, dtype=np.float64) / 64.0)
    fr = pos[:, None] * inv[None, :]
    emb = np.concatenate([fr, fr], axis=1)
    cos = np.cos(emb).astype(np.float32)
    sin = np.sin(emb).astype(np.float32)
    ss = sin.copy()
    ss[:, 0:32] *= -1.0
    to_sb = lambda a: np.ascontiguousarray(
        a.reshape(NTG, 128, 64).transpose(1, 0, 2).reshape(128, NTG * 64)
    ).astype(ml_dtypes.bfloat16)
    return to_sb(cos), to_sb(ss)


def kernel(L0_attn_x, L0_attn_W_qkv, L0_attn_W_out, L0_ffn_ln_gamma,
           L0_ffn_ln_beta, L0_ffn_fc1_q, L0_ffn_fc1_scales, L0_ffn_fc1_bias,
           L0_ffn_fc2_q, L0_ffn_fc2_scales, L0_ffn_fc2_bias):
    x = np.ascontiguousarray(np.asarray(L0_attn_x, dtype=np.float32)[0].T)
    wqkv = np.ascontiguousarray(np.asarray(L0_attn_W_qkv, dtype=np.float32))
    wo = np.ascontiguousarray(np.asarray(L0_attn_W_out, dtype=np.float32))
    gamT = np.ascontiguousarray(np.asarray(L0_ffn_ln_gamma, np.float32).reshape(KC, 128).T)
    betT = np.ascontiguousarray(np.asarray(L0_ffn_ln_beta, np.float32).reshape(KC, 128).T)
    b1T = np.ascontiguousarray(np.asarray(L0_ffn_fc1_bias, np.float32).reshape(MC, 128).T)
    b2T = np.ascontiguousarray(np.asarray(L0_ffn_fc2_bias, np.float32).reshape(KC, 128).T)
    q1 = np.ascontiguousarray(np.asarray(L0_ffn_fc1_q, np.int32)
                              .reshape(KC, 128, MC, 128).transpose(2, 1, 0, 3)
                              .reshape(F4, D))
    s1 = np.ascontiguousarray(np.asarray(L0_ffn_fc1_scales, np.float32)
                              .reshape(KC, MC, 128).transpose(1, 0, 2).reshape(MC, D))
    q2 = np.ascontiguousarray(np.asarray(L0_ffn_fc2_q, np.int32)
                              .reshape(MC, 128, KC, 128).transpose(2, 1, 0, 3)
                              .reshape(D, F4))
    s2 = np.ascontiguousarray(np.asarray(L0_ffn_fc2_scales, np.float32)
                              .reshape(MC, KC, 128).transpose(1, 0, 2).reshape(KC, F4))
    iden = np.eye(128, dtype=ml_dtypes.bfloat16)
    selm = np.zeros((16, 1024), dtype=ml_dtypes.bfloat16)
    for p in range(NP):
        selm[2 * p, p * 128:p * 128 + 64] = 1
        selm[2 * p + 1, p * 128 + 64:(p + 1) * 128] = 1

    nc = build_program()
    in_maps = []
    for c in range(NCORES):
        tc, ts = _rope_tables(c)
        in_maps.append(dict(
            x_fm=np.ascontiguousarray(x[:, c * TL:(c + 1) * TL]),
            wqkv=wqkv, wout=wo, gamT=gamT, betT=betT, b1T=b1T, b2T=b2T,
            q1=q1, s1=s1, q2=q2, s2=s2, ropc=tc, rops=ts, identin=iden, selin=selm,
        ))
    res = run_bass_kernel_spmd(nc, in_maps, core_ids=list(range(NCORES)))
    full = np.concatenate([r["out_fm"] for r in res.results], axis=1)
    return np.ascontiguousarray(full.T)[None, :, :]


# version 07
# speedup vs baseline: 1.2914x; 1.0581x over previous
import numpy as np
import ml_dtypes
from contextlib import ExitStack

import concourse.bass as bass
import concourse.mybir as mybir
from concourse.bass_utils import run_bass_kernel_spmd

F32 = mybir.dt.float32
BF16 = mybir.dt.bfloat16
I32 = mybir.dt.int32
AF = mybir.ActivationFunctionType
ALU = mybir.AluOpType

NCORES = 8
D = 1024
TT = 16384
TL = TT // NCORES
KC = 8
F4 = 4096
MC = 32
NQ = 4
QT = TL // NQ
NTG = TL // 128
NP = 8
EPS = 1e-5
ROPE_BASE = 500000.0
QSCALE = 64.0 ** -0.5

ENGS = ("sync", "gpsimd", "tensor", "vector", "scalar")


class Plan:

    def __init__(self):
        self.streams = {e: [] for e in ENGS}
        self.cnt = {}
        self.handles = {}

    def new_sem(self, name):
        self.cnt[name] = 0

    def emit(self, eng, fn):
        self.streams[eng].append(fn)

    def wait(self, eng, sem, val=None):
        v = self.cnt[sem] if val is None else val
        if v is None or v <= 0:
            return
        self.emit(eng, lambda E, s=sem, v=v: E.wait_ge(self.handles[s], v))

    def op(self, eng, fn, sem=None, k=1):
        if sem is None:
            self.emit(eng, fn)
            return None
        self.cnt[sem] += k
        self.emit(eng, lambda E, fn=fn, s=sem, k=k: fn(E).then_inc(self.handles[s], k))
        return self.cnt[sem]

    def dma(self, eng, out, in_, sem):
        return self.op(eng, lambda E, o=out, i=in_: E.dma_start(out=o, in_=i), sem, 16)


def build_program():
    nc = bass.Bass("TRN2", target_bir_lowering=False)
    P = Plan()
    for s in ("sd", "gd", "t", "v", "s", "cc", "c0", "x0",
              "w1s0", "w1s1", "w1s2", "w1s3", "w2s0", "w2s1",
              "wbq0", "wbq1", "wbq2", "wbq3", "wbo0", "wbo1",
              "p0", "p1", "wos0", "wos1", "cc2"):
        P.new_sem(s)

    x_fm = nc.dram_tensor("x_fm", [D, TL], F32, kind="ExternalInput")
    wqkv = nc.dram_tensor("wqkv", [D, 3 * D], F32, kind="ExternalInput")
    wout = nc.dram_tensor("wout", [D, D], F32, kind="ExternalInput")
    gamT = nc.dram_tensor("gamT", [128, KC], F32, kind="ExternalInput")
    betT = nc.dram_tensor("betT", [128, KC], F32, kind="ExternalInput")
    b1T = nc.dram_tensor("b1T", [128, MC], F32, kind="ExternalInput")
    b2T = nc.dram_tensor("b2T", [128, KC], F32, kind="ExternalInput")
    q1 = nc.dram_tensor("q1", [F4, D], I32, kind="ExternalInput")
    s1 = nc.dram_tensor("s1", [MC, D], F32, kind="ExternalInput")
    q2 = nc.dram_tensor("q2", [D, F4], I32, kind="ExternalInput")
    s2 = nc.dram_tensor("s2", [KC, F4], F32, kind="ExternalInput")
    ropc = nc.dram_tensor("ropc", [128, NTG * 64], BF16, kind="ExternalInput")
    rops = nc.dram_tensor("rops", [128, NTG * 64], BF16, kind="ExternalInput")
    identin = nc.dram_tensor("identin", [128, 128], BF16, kind="ExternalInput")
    selin = nc.dram_tensor("selin", [16, 1024], BF16, kind="ExternalInput")
    out_fm = nc.dram_tensor("out_fm", [D, TL], F32, kind="ExternalOutput")

    w1c = nc.dram_tensor("w1c", [F4, D], BF16, kind="Internal")
    w2c = nc.dram_tensor("w2c", [D, F4], BF16, kind="Internal")
    cc_in = nc.dram_tensor("cc_in", [128, 1032], BF16, kind="Internal")
    cc_out = nc.dram_tensor("cc_out", [128, 1032], BF16, kind="Internal")

    es = ExitStack()
    sb = lambda name, shape, dt: es.enter_context(nc.sbuf_tensor(name, shape, dt))
    ps = lambda name, shape, dt: es.enter_context(nc.psum_tensor(name, shape, dt))

    X = sb("X", [128, KC * TL], F32)
    arena = sb("arena", [128, 32768], BF16)
    wbuf = sb("wbuf", [128, 18432], BF16)
    ovl = sb("ovl", [128, 8192], BF16)
    rtile = ovl[:, 0:1024]
    rcos = ovl[:, 1024:2048]
    rsin = ovl[:, 2048:3072]
    kft = [ovl[:, 3072:4096]]
    vt = [ovl[:, 4096:5120]]
    denr = ovl[0:16, 5120:5632]
    ovf = ovl[:, :].bitcast(F32)
    dmax = ovf[0:16, 2816:3328]
    PQr = [(ovf[0:1, qq * 512:(qq + 1) * 512],
            ovf[0:1, 2048 + qq * 512: 2048 + (qq + 1) * 512]) for qq in range(4)]
    kvx = sb("kvx", [128, 1032], BF16)
    ksb = sb("ksb", [128, 128], BF16)
    sel = sb("sel", [16, 1024], BF16)
    mir = [sb("mir0", [128, QT], BF16)]
    sqt = [sb("sqt0", [128, QT], BF16)]
    tb1 = sb("tb1", [128, QT], BF16)
    tb2 = sb("tb2", [128, QT], BF16)
    z4t = sb("z4t", [128, QT], F32)
    rows = sb("rows", [1, 5 * QT], F32)
    sgam = sb("sgam", [128, KC], F32)
    sbet = sb("sbet", [128, KC], F32)
    sb1t = sb("sb1t", [128, MC], F32)
    sqb2 = sb("sqb2", [128, KC], F32)
    tabc = sb("tabc", [128, NTG * 64], BF16)
    tabs = sb("tabs", [128, NTG * 64], BF16)
    ident = sb("ident", [128, 128], BF16)
    ones_c = sb("ones_c", [128, 1], BF16)
    ones_r = sb("ones_r", [1, 128], F32)

    R_M, R_C, R_A1, R_P, R_Q = (rows[0:1, i * QT:(i + 1) * QT] for i in range(5))
    R_TS = R_C

    pB0 = ps("pB0", [128, 1024], F32)
    pB1 = ps("pB1", [128, 1024], F32)
    pSS = ps("pSS", [128, 512], F32)
    pM = ps("pM", [128, 512], F32)
    pC = ps("pC", [128, 512], F32)
    pTR = ps("pTR", [128, 1024], BF16)

    def Xv(k, c0, n):
        return X[:, k * TL + c0: k * TL + c0 + n]

    def Av(m):
        return arena[:, m * QT:(m + 1) * QT]

    def Hv(k, q):
        return arena[:, 16384 + k * TL + q * QT: 16384 + k * TL + q * QT + QT]

    def QF(p, c0, n):
        return arena[:, p * TL + c0: p * TL + c0 + n]

    def Y1(k, c0, n):
        return arena[:, 16384 + k * TL + c0: 16384 + k * TL + c0 + n]

    def W1S(buf):
        return wbuf[:, buf * 1024:(buf + 1) * 1024]

    def SCB(buf):
        return wbuf[:, 4096 + buf * 1024: 4096 + (buf + 1) * 1024]

    def W2S(buf):
        return wbuf[:, 6144 + buf * 4096: 6144 + (buf + 1) * 4096]

    SC2B = wbuf[:, 14336:18432]

    def PART(slot):
        return wbuf[:, slot * 8192:(slot + 1) * 8192]

    def WOS(buf):
        return wbuf[:, 16384 + buf * 1024: 16384 + (buf + 1) * 1024]

    G = {}

    def pb_wait(buf):
        P.wait("tensor", "v", G.get(("pbv", buf), 0))
        P.wait("tensor", "s", G.get(("pbs", buf), 0))

    def pb_used(buf, v=None, s=None):
        if v is not None:
            G[("pbv", buf)] = v
        if s is not None:
            G[("pbs", buf)] = s

    P.dma("gpsimd", sgam[:, :], gamT.ap(), "c0")
    P.dma("gpsimd", sbet[:, :], betT.ap(), "c0")
    P.dma("gpsimd", sb1t[:, :], b1T.ap(), "c0")
    P.dma("gpsimd", sqb2[:, :], b2T.ap(), "c0")
    P.dma("gpsimd", tabc[:, :], ropc.ap(), "c0")
    P.dma("gpsimd", tabs[:, :], rops.ap(), "c0")
    P.dma("gpsimd", ident[:, :], identin.ap(), "c0")
    P.dma("gpsimd", sel[:, :], selin.ap(), "c0")
    const_ld = P.cnt["c0"]

    xld = [P.dma("gpsimd", Xv(k, 0, TL), x_fm.ap()[k * 128:(k + 1) * 128, :], "x0")
           for k in range(KC)]

    P.op("vector", lambda E: E.memset(ones_c[:, :], 1.0), "v")
    P.op("vector", lambda E: E.memset(ones_r[:, :], 1.0), "v")
    P.wait("scalar", "c0", const_ld)
    P.op("scalar", lambda E: E.mul(sqb2[:, :], sqb2[:, :], 0.25), "s")

    def emit_ln_quarter(t0, mode, dst, pq=None):
        for k in range(KC):
            b = 0
            P.wait("vector", "x0", xld[-1])
            pv = G.get(("mir", b))
            if pv is not None:
                P.wait("vector", "t", pv)
            P.op("vector", lambda E, k=k, b=b, t0=t0: E.tensor_copy(
                mir[b][:, :], Xv(k, t0, QT)), "v")
            mv = P.op("vector", lambda E, b=b: E.tensor_mul(
                sqt[b][:, :], mir[b][:, :], mir[b][:, :]), "v")
            P.wait("tensor", "v", mv)
            if k == 0:
                P.wait("tensor", "v", G.get(("pss_rd_v"), 0))
                P.wait("tensor", "s", G.get(("pss_rd_s"), 0))
            P.op("tensor", lambda E, b=b, k=k: E.matmul(
                pSS[0:1, :], ones_c[:, :], mir[b][:, :],
                start=(k == 0), stop=(k == KC - 1)), "t")
            tm = P.op("tensor", lambda E, b=b, k=k: E.matmul(
                pSS[32:33, :], ones_c[:, :], sqt[b][:, :],
                start=(k == 0), stop=(k == KC - 1), tile_position=(0, 32)), "t")
            G[("mir", b)] = tm
        P.wait("vector", "t", None)
        P.wait("vector", "t", G.get(("rows_t",), 0))
        P.op("vector", lambda E: E.tensor_scalar_mul(R_M, pSS[0:1, :], -1.0 / D), "v")
        P.op("vector", lambda E: E.tensor_mul(R_TS, R_M, R_M), "v")
        P.op("vector", lambda E: E.scalar_tensor_tensor(
            R_TS, pSS[32:33, :], 1.0 / D, R_TS, ALU.mult, ALU.subtract), "v")
        G[("pss_rd_v")] = P.cnt["v"]
        cv = P.op("vector", lambda E: E.tensor_scalar(
            R_A1, R_TS, EPS, None, ALU.add), "v")
        P.wait("scalar", "v", cv)
        P.op("scalar", lambda E: E.activation(out=R_A1, in_=R_A1, func=AF.Ln), "s")
        ssn = P.op("scalar", lambda E: E.activation(out=R_A1, in_=R_A1, func=AF.Exp, scale=-0.5), "s")
        if mode == "ffn":
            cv2 = P.op("vector", lambda E: E.tensor_scalar(
                R_C, R_TS, 1.0 + EPS, EPS * EPS, ALU.mult, ALU.add), "v")
            P.wait("scalar", "v", cv2)
            P.op("scalar", lambda E: E.activation(out=R_C, in_=R_C, func=AF.Ln), "s")
            ssn = P.op("scalar", lambda E: E.activation(out=R_C, in_=R_C, func=AF.Exp, scale=-0.5), "s")
            P.wait("vector", "s", ssn)
            P.wait("vector", "t", G.get(("pq_t", t0 // QT), 0))
            P.op("vector", lambda E, pq=pq: E.tensor_scalar(pq[0], R_A1, 0.5, 1.0, ALU.mult, ALU.add), "v")
            P.op("vector", lambda E, pq=pq: E.scalar_tensor_tensor(pq[1], R_M, 0.5, R_A1, ALU.mult, ALU.mult), "v")
        rows_v, rows_s = P.cnt["v"], P.cnt["s"]
        P.wait("tensor", "v", rows_v)
        P.wait("tensor", "s", rows_s)
        P.wait("tensor", "v", G.get(("pmc_rd_v"), 0))
        P.wait("tensor", "s", G.get(("pmc_rd_s"), 0))
        P.op("tensor", lambda E: E.matmul(pM[:, :], ones_r[:, :], R_M, start=True, stop=True), "t")
        src2 = R_C if mode == "ffn" else R_A1
        bt = P.op("tensor", lambda E, s2=src2: E.matmul(pC[:, :], ones_r[:, :], s2, start=True, stop=True), "t")
        G[("rows_t",)] = bt
        P.wait("vector", "t", bt)
        P.wait("vector", "t", G.get(("dst_t",), 0))
        P.wait("scalar", "t", G.get(("dst_t",), 0))
        for k in range(KC):
            pv = G.get(("tb_s"))
            if pv is not None and mode == "ffn":
                P.wait("vector", "s", pv)
            P.op("vector", lambda E, k=k, t0=t0: E.tensor_add(
                tb1[:, :], Xv(k, t0, QT), pM[:, :]), "v")
            if mode == "ffn":
                tv = P.op("vector", lambda E: E.tensor_mul(tb2[:, :], tb1[:, :], pC[:, :]), "v")
                P.wait("scalar", "v", tv)
                smk = P.op("scalar", lambda E, k=k: E.activation(
                    out=dst(k), in_=tb2[:, :], func=AF.Identity,
                    bias=sbet[:, k:k + 1], scale=sgam[:, k:k + 1]), "s")
                G[("tb_s")] = smk
            else:
                P.op("vector", lambda E, k=k: E.tensor_mul(dst(k), tb1[:, :], pC[:, :]), "v")
        G[("pmc_rd_v")] = P.cnt["v"]
        G[("pmc_rd_s")] = P.cnt["s"]
        return P.cnt["v"], P.cnt["s"]

    def emit_ffn(first):
        h_marks = [None] * NQ

        def lnq(tc):
            h_marks[tc] = emit_ln_quarter(
                tc * QT, "ffn", lambda k, tc=tc: Hv(k, tc), pq=PQr[tc])

        lnq(0)
        for q in range(NQ):
            t0 = q * QT
            h_v, h_s = h_marks[q]
            for m in range(MC):
                buf = m % 4
                ls = f"w1s{buf}"
                if first and q == 0:
                    scb = m % 2
                    P.wait("gpsimd", "t", G.get(("w1s", buf), 0))
                    P.wait("gpsimd", f"wbq{buf}", G.get(("wbq", buf), 0))
                    P.wait("gpsimd", "v", G.get(("scb", scb), 0))
                    P.dma("gpsimd", W1S(buf)[:, :],
                          q1.ap()[m * 128:(m + 1) * 128, :], ls)
                    lsc = P.dma("gpsimd", SCB(scb)[:, :],
                                bass.AP(s1, m * D, [[0, 128], [1, D]]), ls)
                    P.wait("vector", ls, lsc)
                    stt = P.op("vector", lambda E, buf=buf, scb=scb: E.scalar_tensor_tensor(
                        W1S(buf)[:, :], W1S(buf)[:, :], -128.0, SCB(scb)[:, :],
                        ALU.add, ALU.mult), "v")
                    G[("scb", scb)] = stt
                    P.wait("sync", "v", stt)
                    wbk = P.dma("sync", w1c.ap()[m * 128:(m + 1) * 128, :],
                                W1S(buf)[:, :], f"wbq{buf}")
                    G[("wbq", buf)] = wbk
                    w_ready = ("v", stt)
                else:
                    g = G.get(("w1s", buf))
                    if g is not None:
                        P.wait("sync", "t", g)
                    if first and q == 1 and m < 4:
                        for wb in range(4):
                            P.wait("sync", f"wbq{wb}", G.get(("wbq", wb), 0))
                    lw = P.dma("sync", W1S(buf)[:, :],
                               w1c.ap()[m * 128:(m + 1) * 128, :], ls)
                    w_ready = (ls, lw)
                pb = pB0 if m % 2 == 0 else pB1
                pb_wait(m % 2)
                P.wait("tensor", w_ready[0], w_ready[1])
                P.wait("tensor", "v", h_v)
                P.wait("tensor", "s", h_s)
                tmk = None
                for k in range(KC):
                    tmk = P.op("tensor", lambda E, pb=pb, buf=buf, k=k, q=q: E.matmul(
                        pb[:, 0:512], W1S(buf)[:, k * 128:(k + 1) * 128], Hv(k, q),
                        start=(k == 0), stop=(k == KC - 1)), "t")
                G[("w1s", buf)] = tmk
                P.wait("scalar", "t", tmk)
                smk = P.op("scalar", lambda E, pb=pb, m=m: E.activation(
                    out=Av(m), in_=pb[:, 0:512], func=AF.Silu,
                    bias=sb1t[:, m:m + 1], scale=1.0), "s")
                pb_used(m % 2, s=smk)
            G[("dst_t",)] = P.cnt["t"]
            a_ready_s = P.cnt["s"]
            if q + 1 < NQ and h_marks[q + 1] is None:
                lnq(q + 1)
            pq_v = h_marks[q][0]
            P.wait("tensor", "v", pq_v)
            P.wait("tensor", "v", G.get(("pmc_rd_v"), 0))
            P.op("tensor", lambda E, q=q: E.matmul(pM[:, :], ones_r[:, :], PQr[q][0], start=True, stop=True), "t")
            pqb = P.op("tensor", lambda E, q=q: E.matmul(pC[:, :], ones_r[:, :], PQr[q][1], start=True, stop=True), "t")
            G[("pq_t", q)] = pqb
            G[("pmc_rd_v")] = 0
            G[("pmc_rd_s")] = 0
            for o in range(KC):
                buf = o % 2
                ls = "w2s0" if buf == 0 else "w2s1"
                if first and q == 0:
                    P.wait("gpsimd", "t", G.get(("w2s", buf), 0))
                    P.wait("gpsimd", f"wbo{buf}", G.get(("wbo", buf), 0))
                    P.wait("gpsimd", "v", G.get(("sc2",), 0))
                    P.dma("gpsimd", W2S(buf)[:, :],
                          q2.ap()[o * 128:(o + 1) * 128, :], ls)
                    lsc = P.dma("gpsimd", SC2B[:, :],
                                bass.AP(s2, o * F4, [[0, 128], [1, F4]]), ls)
                    P.wait("vector", ls, lsc)
                    stt = P.op("vector", lambda E, buf=buf: E.scalar_tensor_tensor(
                        W2S(buf)[:, :], W2S(buf)[:, :], -128.0, SC2B[:, :],
                        ALU.add, ALU.mult), "v")
                    G[("sc2",)] = stt
                    P.wait("sync", "v", stt)
                    wbk = P.dma("sync", w2c.ap()[o * 128:(o + 1) * 128, :],
                                W2S(buf)[:, :], f"wbo{buf}")
                    G[("wbo", buf)] = wbk
                    w_ready = ("v", stt)
                else:
                    g = G.get(("w2s", buf))
                    if g is not None:
                        P.wait("sync", "t", g)
                    if first and q == 1 and o < 2:
                        for wb in range(2):
                            P.wait("sync", f"wbo{wb}", G.get(("wbo", wb), 0))
                    lw = P.dma("sync", W2S(buf)[:, :],
                               w2c.ap()[o * 128:(o + 1) * 128, :], ls)
                    w_ready = (ls, lw)
                pb = pB0 if buf == 0 else pB1
                pb_wait(buf)
                P.wait("tensor", w_ready[0], w_ready[1])
                P.wait("tensor", "s", a_ready_s)
                tmk = None
                for k in range(MC):
                    tmk = P.op("tensor", lambda E, pb=pb, buf=buf, k=k: E.matmul(
                        pb[:, 0:512], W2S(buf)[:, k * 128:(k + 1) * 128], Av(k),
                        start=(k == 0), stop=(k == MC - 1)), "t")
                G[("w2s", buf)] = tmk
                zg = G.get(("z4_v"))
                if zg is not None:
                    P.wait("scalar", "v", zg)
                P.wait("scalar", "t", tmk)
                zs = P.op("scalar", lambda E, pb=pb, o=o: E.activation(
                    out=z4t[:, :], in_=pb[:, 0:512], func=AF.Identity,
                    bias=sqb2[:, o:o + 1], scale=0.25), "s")
                pb_used(buf, s=zs)
                P.wait("vector", "t", pqb)
                P.wait("vector", "s", zs)
                P.op("vector", lambda E, o=o, t0=t0: E.tensor_mul(
                    Xv(o, t0, QT), Xv(o, t0, QT), pM[:, :]), "v")
                zv = P.op("vector", lambda E, o=o, t0=t0: E.tensor_add(
                    Xv(o, t0, QT), Xv(o, t0, QT), z4t[:, :]), "v")
                G[("z4_v")] = zv
                vmk = P.op("vector", lambda E, o=o, t0=t0: E.tensor_add(
                    Xv(o, t0, QT), Xv(o, t0, QT), pC[:, :]), "v")
                G[("pmc_rd_v")] = vmk
                if not first:
                    P.wait("gpsimd", "v", vmk)
                    P.dma("gpsimd", out_fm.ap()[o * 128:(o + 1) * 128, t0:t0 + QT],
                          Xv(o, t0, QT), "gd")
            if q + 2 < NQ and h_marks[q + 2] is None:
                lnq(q + 2)

    def emit_attn():
        ffn1_t = P.cnt["t"]

        y1_marks = []
        for tc in range(NQ):
            y1_marks.append(emit_ln_quarter(
                tc * QT, "attn", lambda k, tc=tc: Y1(k, tc * QT, QT)))

        P.wait("gpsimd", "t", ffn1_t)
        for wb in range(4):
            P.wait("gpsimd", f"wbq{wb}", None)
        for wb in range(2):
            P.wait("gpsimd", f"wbo{wb}", None)
        lq = P.dma("gpsimd", PART(0)[:, :].rearrange("p (k n) -> p k n", k=KC),
                   wqkv.ap()[:, 0:1024].rearrange("(k p) n -> p k n", p=128), "p0")

        def qkv_mms(part_slot, tg, pbank, pmark):
            mvv, mss = y1_marks[tg // 4]
            P.wait("tensor", "v", mvv)
            P.wait("tensor", pmark[0], pmark[1])
            tmk = None
            for h in range(2):
                for k in range(KC):
                    tmk = P.op("tensor", lambda E, pbank=pbank, k=k, h=h, tg=tg,
                               ps=part_slot: E.matmul(
                        pbank[:, h * 512:(h + 1) * 512],
                        Y1(k, (tg // 4) * QT + (tg % 4) * 128, 128),
                        PART(ps)[:, k * 1024 + h * 512: k * 1024 + (h + 1) * 512],
                        start=(k == 0), stop=(k == KC - 1)), "t")
            return tmk

        def rope_elu(tg, pbank, scale_q):
            sc = P.op("scalar", lambda E, pb=pbank: E.activation(
                out=rtile, in_=pb[:, :], func=AF.Identity), "s")
            P.wait("vector", "s", sc)
            P.op("vector", lambda E, tg=tg: E.tensor_tensor(
                rcos.rearrange("p (g j) -> p g j", g=16),
                rtile.rearrange("p (g j) -> p g j", g=16),
                bass.AP(tabc, tg * 64, [[NTG * 64, 128], [0, 16], [1, 64]]),
                ALU.mult), "v")
            P.op("vector", lambda E, tg=tg: E.tensor_tensor(
                bass.AP(ovl, 2048, [[8192, 128], [64, 16], [1, 32]]),
                bass.AP(ovl, 32, [[8192, 128], [64, 16], [1, 32]]),
                bass.AP(tabs, tg * 64, [[NTG * 64, 128], [0, 16], [1, 32]]),
                ALU.mult), "v")
            P.op("vector", lambda E, tg=tg: E.tensor_tensor(
                bass.AP(ovl, 2048 + 32, [[8192, 128], [64, 16], [1, 32]]),
                bass.AP(ovl, 0, [[8192, 128], [64, 16], [1, 32]]),
                bass.AP(tabs, tg * 64 + 32, [[NTG * 64, 128], [0, 16], [1, 32]]),
                ALU.mult), "v")
            P.op("vector", lambda E: E.tensor_add(rtile, rcos, rsin), "v")
            s = QSCALE if scale_q else 1.0
            mv = P.op("vector", lambda E, s=s: E.tensor_scalar(
                rcos, rtile, s, 0.0, ALU.mult, ALU.min), "v")
            P.wait("scalar", "v", mv)
            ev = P.op("scalar", lambda E: E.activation(out=rsin, in_=rcos, func=AF.Exp), "s")
            P.wait("vector", "s", ev)
            if scale_q:
                P.op("vector", lambda E, s=s: E.tensor_scalar(
                    rcos, rtile, s, 0.0, ALU.mult, ALU.max), "v")
                fv = P.op("vector", lambda E: E.tensor_add(rtile, rcos, rsin), "v")
            else:
                fv = P.op("vector", lambda E: E.scalar_tensor_tensor(
                    rtile, rtile, 0.0, rsin, ALU.max, ALU.add), "v")
            return fv

        tr_guard = None
        pend = None

        def flush_tr():
            nonlocal tr_guard, pend
            if pend is None:
                return
            tg0, fv0 = pend
            P.wait("tensor", "v", fv0)
            if tr_guard is not None:
                P.wait("tensor", "s", tr_guard)
            ttk = None
            for p in range(NP):
                ttk = P.op("tensor", lambda E, p=p: E.transpose(
                    pTR[:, p * 128:(p + 1) * 128], rtile[:, p * 128:(p + 1) * 128],
                    ident[:, :]), "t")
            P.wait("scalar", "t", ttk)
            tr_guard = P.op("scalar", lambda E, tg0=tg0: E.activation(
                out=bass.AP(arena, tg0 * 128, [[32768, 128], [TL, NP], [1, 128]]),
                in_=pTR[:, :].rearrange("p (g j) -> p g j", g=NP),
                func=AF.Identity), "s")
            pend = None

        for tg in range(NTG):
            pb = pB0 if tg % 2 == 0 else pB1
            pb_wait(tg % 2)
            tmk = qkv_mms(0, tg, pb, ("p0", lq))
            flush_tr()
            P.wait("scalar", "t", tmk)
            P.wait("scalar", "s", tr_guard or 0)
            fv = rope_elu(tg, pb, True)
            pb_used(tg % 2, s=P.cnt["s"])
            pend = (tg, fv)
        flush_tr()
        q_pass_t = P.cnt["t"]

        lk = P.dma("gpsimd", PART(1)[:, :].rearrange("p (k n) -> p k n", k=KC),
                   wqkv.ap()[:, 1024:2048].rearrange("(k p) n -> p k n", p=128), "p1")
        P.wait("gpsimd", "t", q_pass_t)
        lv = P.dma("gpsimd", PART(0)[:, :].rearrange("p (k n) -> p k n", k=KC),
                   wqkv.ap()[:, 2048:3072].rearrange("(k p) n -> p k n", p=128), "p0")

        kv_mm = None
        kv_pend = None

        def flush_kv():
            nonlocal kv_mm, kv_pend
            if kv_pend is None:
                return
            tg0, kcp0, vv0 = kv_pend
            P.wait("tensor", "v", kcp0)
            P.wait("tensor", "s", vv0)
            for p in range(NP):
                pdst = pM if p < 4 else pC
                pcol = (p % 4) * 128
                P.op("tensor", lambda E, p=p, pdst=pdst, pcol=pcol, tg0=tg0: E.matmul(
                    pdst[:, pcol:pcol + 128], kft[0][:, p * 128:(p + 1) * 128],
                    vt[0][:, p * 128:(p + 1) * 128],
                    start=(tg0 == 0 and p % 4 == 0), stop=(tg0 == NTG - 1),
                    skip_group_check=True), "t")
                kv_mm = P.op("tensor", lambda E, p=p, tg0=tg0: E.matmul(
                    pSS[:, p:p + 1], kft[0][:, p * 128:(p + 1) * 128], ones_c[:, :],
                    start=(tg0 == 0 and p == 0), stop=(tg0 == NTG - 1),
                    skip_group_check=True), "t")
            G[("kft", 0)] = kv_mm
            kv_pend = None

        for tg in range(NTG):
            pb_wait(0)
            tmk = qkv_mms(1, tg, pB0, ("p1", lk))
            pb_wait(1)
            tmv = qkv_mms(0, tg, pB1, ("p0", lv))
            flush_kv()
            P.wait("scalar", "t", tmk)
            P.wait("scalar", "v", G.get(("kcp",), 0))
            fv = rope_elu(tg, pB0, False)
            pb_used(0, s=P.cnt["s"])
            kg = G.get(("kft", 0))
            if kg is not None:
                P.wait("vector", "t", kg)
            kcp = P.op("vector", lambda E: E.tensor_copy(kft[0], rtile), "v")
            G[("kcp",)] = kcp
            P.wait("scalar", "t", tmv)
            P.wait("scalar", "t", G.get(("kft", 0), 0))
            vv = P.op("scalar", lambda E: E.activation(
                out=vt[0], in_=pB1[:, :], func=AF.Identity), "s")
            pb_used(1, s=vv)
            kv_pend = (tg, kcp, vv)
            if True:
                pass
        flush_kv()

        P.wait("vector", "t", None)
        P.op("vector", lambda E: E.tensor_copy(kvx[:, 0:512], pM[:, :]), "v")
        P.op("vector", lambda E: E.tensor_copy(kvx[:, 512:1024], pC[:, :]), "v")
        pk = P.op("vector", lambda E: E.tensor_copy(kvx[:, 1024:1032], pSS[:, 0:8]), "v")
        P.wait("gpsimd", "v", pk)
        P.dma("gpsimd", cc_in.ap(), kvx[:, :], "gd")
        P.wait("gpsimd", "gd", None)
        P.op("gpsimd", lambda E: E.collective_compute(
            "AllReduce", ALU.add,
            replica_groups=[list(range(NCORES))],
            ins=[cc_in.ap().opt()], outs=[cc_out.ap().opt()]), "cc")
        P.wait("gpsimd", "cc", None)
        arb = P.dma("gpsimd", kvx[:, :], cc_out.ap(), "cc2")
        P.wait("vector", "cc2", arb)
        P.op("vector", lambda E: E.memset(ksb[:, :], 0.0), "v")
        for p in range(NP):
            P.op("vector", lambda E, p=p: E.tensor_copy(
                ksb[0:64, p * 16 + 2 * p: p * 16 + 2 * p + 1],
                kvx[0:64, 1024 + p:1025 + p]), "v")
            P.op("vector", lambda E, p=p: E.tensor_copy(
                ksb[64:128, p * 16 + 2 * p + 1: p * 16 + 2 * p + 2],
                kvx[64:128, 1024 + p:1025 + p]), "v")
        kv_ready_v = P.cnt["v"]

        G[("qsc_v",)] = kv_ready_v
        den_s = None
        for tc in range(NQ):
            P.wait("tensor", "v", kv_ready_v)
            if den_s is not None:
                P.wait("tensor", "v", G[("dmax_v",)])
            dmk = None
            for p in range(NP):
                dmk = P.op("tensor", lambda E, p=p, tc=tc: E.matmul(
                    pSS[0:16, :], ksb[:, p * 16:(p + 1) * 16], QF(p, tc * QT, QT),
                    start=(p == 0), stop=(p == NP - 1)), "t")
            P.wait("vector", "t", dmk)
            P.wait("vector", "s", G.get(("dens",), 0))
            dvv = P.op("vector", lambda E: E.tensor_scalar_max(dmax, pSS[0:16, :], 1e-6), "v")
            G[("dmax_v",)] = dvv
            P.wait("scalar", "v", dvv)
            P.op("scalar", lambda E: E.activation(out=dmax, in_=dmax, func=AF.Ln), "s")
            den_s = P.op("scalar", lambda E: E.activation(
                out=denr, in_=dmax, func=AF.Exp, scale=-1.0), "s")
            G[("dens",)] = den_s
            for p in range(NP):
                i = (tc * NP + p) % 2
                P.wait("tensor", "s", den_s)
                P.wait("tensor", "v", G.get(("qsc_v",), 0))
                bmk = P.op("tensor", lambda E, p=p: E.matmul(
                    pM[:, :], sel[:, p * 128:(p + 1) * 128],
                    denr, start=True, stop=True), "t")
                P.wait("vector", "t", bmk)
                qv = P.op("vector", lambda E, p=p, tc=tc: E.tensor_mul(
                    QF(p, tc * QT, QT), QF(p, tc * QT, QT), pM[:, :]), "v")
                G[("qsc_v",)] = qv
                pb = pB0 if i == 0 else pB1
                pb_wait(i)
                P.wait("tensor", "v", qv)
                P.op("tensor", lambda E, p=p, tc=tc, pb=pb: E.matmul(
                    pb[0:64, 0:512], kvx[0:64, p * 128:p * 128 + 64],
                    bass.AP(arena, p * TL + tc * QT, [[32768, 64], [1, QT]]),
                    start=True, stop=True, tile_position=(0, 0)), "t")
                amk = P.op("tensor", lambda E, p=p, tc=tc, pb=pb: E.matmul(
                    pb[64:128, 0:512], kvx[64:128, p * 128 + 64:(p + 1) * 128],
                    bass.AP(arena, 64 * 32768 + p * TL + tc * QT, [[32768, 64], [1, QT]]),
                    start=True, stop=True, tile_position=(64, 64)), "t")
                P.wait("scalar", "t", amk)
                smk = P.op("scalar", lambda E, p=p, tc=tc, pb=pb: E.activation(
                    out=Y1(p, tc * QT, QT), in_=pb[:, 0:512], func=AF.Identity), "s")
                pb_used(i, s=smk)
        at_done_s = P.cnt["s"]

        wo_t = None
        for o in range(KC):
            buf = o % 2
            g = G.get(("wos", buf))
            if g is not None:
                P.wait("gpsimd", "t", g)
            if o < 2:
                P.wait("gpsimd", "t", q_pass_t)
            ls = "wos0" if buf == 0 else "wos1"
            lw = P.dma("gpsimd", WOS(buf)[:, :].rearrange("p (k j) -> p k j", k=KC),
                       wout.ap()[:, o * 128:(o + 1) * 128].rearrange("(k p) j -> p k j", p=128),
                       ls)
            for tc in range(NQ):
                i = (o * NQ + tc) % 2
                pb = pB0 if i == 0 else pB1
                pb_wait(i)
                P.wait("tensor", ls, lw)
                P.wait("tensor", "s", at_done_s)
                tmk = None
                for k in range(KC):
                    tmk = P.op("tensor", lambda E, o=o, tc=tc, k=k, pb=pb, buf=buf: E.matmul(
                        pb[:, 0:512], WOS(buf)[:, k * 128:(k + 1) * 128],
                        Y1(k, tc * QT, QT),
                        start=(k == 0), stop=(k == KC - 1)), "t")
                G[("wos", buf)] = tmk
                wo_t = tmk
                P.wait("vector", "t", tmk)
                vmk = P.op("vector", lambda E, o=o, tc=tc, pb=pb: E.tensor_add(
                    Xv(o, tc * QT, QT), Xv(o, tc * QT, QT), pb[:, 0:512]), "v")
                pb_used(i, v=vmk)

        G[("dst_t",)] = P.cnt["t"]
        G[("pmc_rd_v")] = P.cnt["v"]
        G[("pmc_rd_s")] = P.cnt["s"]
        G[("pss_rd_v")] = P.cnt["v"]
        for bb in range(4):
            G[("w1s", bb)] = wo_t
        G[("w2s", 0)] = wo_t
        G[("w2s", 1)] = wo_t
        G[("hq_gate",)] = wo_t

    import os as _os
    _phases = int(_os.environ.get("MK_PHASES", "3"))

    def dump_X():
        vnow = P.cnt["v"]
        P.wait("gpsimd", "v", vnow)
        for o in range(KC):
            P.dma("gpsimd", out_fm.ap()[o * 128:(o + 1) * 128, :], Xv(o, 0, TL), "gd")

    emit_ffn(first=True)
    if _phases == 1:
        dump_X()
    if _phases >= 2:
        emit_attn()
        if _phases == 2:
            dump_X()
    P.wait("vector", "t", None)
    P.wait("scalar", "t", None)
    emit_ffn(first=False)


    P.wait("gpsimd", "gd", None)
    P.wait("sync", "gd", P.cnt["gd"])
    for s in ("c0", "x0", "w1s0", "w1s1", "w1s2", "w1s3", "w2s0", "w2s1",
              "wbq0", "wbq1", "wbq2", "wbq3", "wbo0", "wbo1",
              "p0", "p1", "wos0", "wos1", "cc2"):
        P.wait("sync", s, None)

    with es:
        for s in list(P.cnt):
            P.handles[s] = es.enter_context(nc.semaphore(s))
        blk = es.enter_context(nc.Block())
        for eng in ENGS:
            stream = P.streams[eng]
            def body(E, stream=stream):
                for fn in stream:
                    fn(E)
            getattr(blk, eng)(body)
    return nc


def _rope_tables(core_id):
    pos = np.arange(core_id * TL, (core_id + 1) * TL, dtype=np.float64)
    inv = ROPE_BASE ** (-np.arange(0, 64, 2, dtype=np.float64) / 64.0)
    fr = pos[:, None] * inv[None, :]
    emb = np.concatenate([fr, fr], axis=1)
    cos = np.cos(emb).astype(np.float32)
    sin = np.sin(emb).astype(np.float32)
    ss = sin.copy()
    ss[:, 0:32] *= -1.0
    to_sb = lambda a: np.ascontiguousarray(
        a.reshape(NTG, 128, 64).transpose(1, 0, 2).reshape(128, NTG * 64)
    ).astype(ml_dtypes.bfloat16)
    return to_sb(cos), to_sb(ss)


def kernel(L0_attn_x, L0_attn_W_qkv, L0_attn_W_out, L0_ffn_ln_gamma,
           L0_ffn_ln_beta, L0_ffn_fc1_q, L0_ffn_fc1_scales, L0_ffn_fc1_bias,
           L0_ffn_fc2_q, L0_ffn_fc2_scales, L0_ffn_fc2_bias):
    x = np.ascontiguousarray(np.asarray(L0_attn_x, dtype=np.float32)[0].T)
    wqkv = np.ascontiguousarray(np.asarray(L0_attn_W_qkv, dtype=np.float32))
    wo = np.ascontiguousarray(np.asarray(L0_attn_W_out, dtype=np.float32))
    gamT = np.ascontiguousarray(np.asarray(L0_ffn_ln_gamma, np.float32).reshape(KC, 128).T)
    betT = np.ascontiguousarray(np.asarray(L0_ffn_ln_beta, np.float32).reshape(KC, 128).T)
    b1T = np.ascontiguousarray(np.asarray(L0_ffn_fc1_bias, np.float32).reshape(MC, 128).T)
    b2T = np.ascontiguousarray(np.asarray(L0_ffn_fc2_bias, np.float32).reshape(KC, 128).T)
    q1 = np.ascontiguousarray(np.asarray(L0_ffn_fc1_q, np.int32)
                              .reshape(KC, 128, MC, 128).transpose(2, 1, 0, 3)
                              .reshape(F4, D))
    s1 = np.ascontiguousarray(np.asarray(L0_ffn_fc1_scales, np.float32)
                              .reshape(KC, MC, 128).transpose(1, 0, 2).reshape(MC, D))
    q2 = np.ascontiguousarray(np.asarray(L0_ffn_fc2_q, np.int32)
                              .reshape(MC, 128, KC, 128).transpose(2, 1, 0, 3)
                              .reshape(D, F4))
    s2 = np.ascontiguousarray(np.asarray(L0_ffn_fc2_scales, np.float32)
                              .reshape(MC, KC, 128).transpose(1, 0, 2).reshape(KC, F4))
    iden = np.eye(128, dtype=ml_dtypes.bfloat16)
    selm = np.zeros((16, 1024), dtype=ml_dtypes.bfloat16)
    for p in range(NP):
        selm[2 * p, p * 128:p * 128 + 64] = 1
        selm[2 * p + 1, p * 128 + 64:(p + 1) * 128] = 1

    nc = build_program()
    in_maps = []
    for c in range(NCORES):
        tc, ts = _rope_tables(c)
        in_maps.append(dict(
            x_fm=np.ascontiguousarray(x[:, c * TL:(c + 1) * TL]),
            wqkv=wqkv, wout=wo, gamT=gamT, betT=betT, b1T=b1T, b2T=b2T,
            q1=q1, s1=s1, q2=q2, s2=s2, ropc=tc, rops=ts, identin=iden, selin=selm,
        ))
    res = run_bass_kernel_spmd(nc, in_maps, core_ids=list(range(NCORES)))
    full = np.concatenate([r["out_fm"] for r in res.results], axis=1)
    return np.ascontiguousarray(full.T)[None, :, :]


# version 08
# speedup vs baseline: 1.3018x; 1.0080x over previous
import numpy as np
import ml_dtypes
from contextlib import ExitStack

import concourse.bass as bass
import concourse.mybir as mybir
from concourse.bass_utils import run_bass_kernel_spmd

F32 = mybir.dt.float32
BF16 = mybir.dt.bfloat16
I32 = mybir.dt.int32
AF = mybir.ActivationFunctionType
ALU = mybir.AluOpType

NCORES = 8
D = 1024
TT = 16384
TL = TT // NCORES
KC = 8
F4 = 4096
MC = 32
NQ = 4
QT = TL // NQ
NTG = TL // 128
NP = 8
EPS = 1e-5
ROPE_BASE = 500000.0
QSCALE = 64.0 ** -0.5

ENGS = ("sync", "gpsimd", "tensor", "vector", "scalar")


class Plan:

    def __init__(self):
        self.streams = {e: [] for e in ENGS}
        self.cnt = {}
        self.handles = {}

    def new_sem(self, name):
        self.cnt[name] = 0

    def emit(self, eng, fn):
        self.streams[eng].append(fn)

    def wait(self, eng, sem, val=None):
        v = self.cnt[sem] if val is None else val
        if v is None or v <= 0:
            return
        self.emit(eng, lambda E, s=sem, v=v: E.wait_ge(self.handles[s], v))

    def op(self, eng, fn, sem=None, k=1):
        if sem is None:
            self.emit(eng, fn)
            return None
        self.cnt[sem] += k
        self.emit(eng, lambda E, fn=fn, s=sem, k=k: fn(E).then_inc(self.handles[s], k))
        return self.cnt[sem]

    def dma(self, eng, out, in_, sem):
        return self.op(eng, lambda E, o=out, i=in_: E.dma_start(out=o, in_=i), sem, 16)


def build_program():
    nc = bass.Bass("TRN2", target_bir_lowering=False)
    P = Plan()
    for s in ("sd", "gd", "t", "v", "s", "cc", "c0", "x0",
              "w1s0", "w1s1", "w1s2", "w1s3", "w2s0", "w2s1",
              "wbq0", "wbq1", "wbq2", "wbq3", "wbo0", "wbo1",
              "p0", "p1", "wos0", "wos1", "cc2"):
        P.new_sem(s)

    x_fm = nc.dram_tensor("x_fm", [D, TL], F32, kind="ExternalInput")
    wqkv = nc.dram_tensor("wqkv", [D, 3 * D], F32, kind="ExternalInput")
    wout = nc.dram_tensor("wout", [D, D], F32, kind="ExternalInput")
    gamT = nc.dram_tensor("gamT", [128, KC], F32, kind="ExternalInput")
    betT = nc.dram_tensor("betT", [128, KC], F32, kind="ExternalInput")
    b1T = nc.dram_tensor("b1T", [128, MC], F32, kind="ExternalInput")
    b2T = nc.dram_tensor("b2T", [128, KC], F32, kind="ExternalInput")
    q1 = nc.dram_tensor("q1", [F4, D], I32, kind="ExternalInput")
    s1 = nc.dram_tensor("s1", [MC, D], F32, kind="ExternalInput")
    q2 = nc.dram_tensor("q2", [D, F4], I32, kind="ExternalInput")
    s2 = nc.dram_tensor("s2", [KC, F4], F32, kind="ExternalInput")
    ropc = nc.dram_tensor("ropc", [128, NTG * 64], BF16, kind="ExternalInput")
    rops = nc.dram_tensor("rops", [128, NTG * 64], BF16, kind="ExternalInput")
    identin = nc.dram_tensor("identin", [128, 128], BF16, kind="ExternalInput")
    selin = nc.dram_tensor("selin", [16, 1024], BF16, kind="ExternalInput")
    out_fm = nc.dram_tensor("out_fm", [D, TL], F32, kind="ExternalOutput")

    w1c = nc.dram_tensor("w1c", [F4, D], BF16, kind="Internal")
    w2c = nc.dram_tensor("w2c", [D, F4], BF16, kind="Internal")
    cc_in = nc.dram_tensor("cc_in", [128, 1032], BF16, kind="Internal")
    cc_out = nc.dram_tensor("cc_out", [128, 1032], BF16, kind="Internal")

    es = ExitStack()
    sb = lambda name, shape, dt: es.enter_context(nc.sbuf_tensor(name, shape, dt))
    ps = lambda name, shape, dt: es.enter_context(nc.psum_tensor(name, shape, dt))

    X = sb("X", [128, KC * TL], F32)
    arena = sb("arena", [128, 32768], BF16)
    wbuf = sb("wbuf", [128, 18432], BF16)
    ovl = sb("ovl", [128, 8192], BF16)
    rtile = ovl[:, 0:1024]
    rcos = ovl[:, 1024:2048]
    rsin = ovl[:, 2048:3072]
    kft = [ovl[:, 3072:4096]]
    vt = [ovl[:, 4096:5120]]
    denr = ovl[0:16, 5120:5632]
    ovf = ovl[:, :].bitcast(F32)
    dmax = ovf[0:16, 2816:3328]
    PQr = [(ovf[0:1, qq * 512:(qq + 1) * 512],
            ovf[0:1, 2048 + qq * 512: 2048 + (qq + 1) * 512]) for qq in range(4)]
    kvx = sb("kvx", [128, 1032], BF16)
    ksb = sb("ksb", [128, 128], BF16)
    sel = sb("sel", [16, 1024], BF16)
    mir = [sb("mir0", [128, QT], BF16)]
    sqt = [sb("sqt0", [128, QT], BF16)]
    tb1 = sb("tb1", [128, QT], BF16)
    tb2 = sb("tb2", [128, QT], BF16)
    z4t = sb("z4t", [128, QT], F32)
    rows = sb("rows", [1, 5 * QT], F32)
    sgam = sb("sgam", [128, KC], F32)
    sbet = sb("sbet", [128, KC], F32)
    sb1t = sb("sb1t", [128, MC], F32)
    sqb2 = sb("sqb2", [128, KC], F32)
    tabc = sb("tabc", [128, NTG * 64], BF16)
    tabs = sb("tabs", [128, NTG * 64], BF16)
    ident = sb("ident", [128, 128], BF16)
    ones_c = sb("ones_c", [128, 1], BF16)
    ones_r = sb("ones_r", [1, 128], F32)

    R_M, R_C, R_A1, R_P, R_Q = (rows[0:1, i * QT:(i + 1) * QT] for i in range(5))
    R_TS = R_C

    pB0 = ps("pB0", [128, 1024], F32)
    pB1 = ps("pB1", [128, 1024], F32)
    pSS = ps("pSS", [128, 512], F32)
    pM = ps("pM", [128, 512], F32)
    pC = ps("pC", [128, 512], F32)
    pTR = ps("pTR", [128, 1024], BF16)

    def Xv(k, c0, n):
        return X[:, k * TL + c0: k * TL + c0 + n]

    def Av(m):
        return arena[:, m * QT:(m + 1) * QT]

    def Hv(k, q):
        return arena[:, 16384 + k * TL + q * QT: 16384 + k * TL + q * QT + QT]

    def QF(p, c0, n):
        return arena[:, p * TL + c0: p * TL + c0 + n]

    def Y1(k, c0, n):
        return arena[:, 16384 + k * TL + c0: 16384 + k * TL + c0 + n]

    def W1S(buf):
        return wbuf[:, buf * 1024:(buf + 1) * 1024]

    def SCB(buf):
        return wbuf[:, 4096 + buf * 1024: 4096 + (buf + 1) * 1024]

    def W2S(buf):
        return wbuf[:, 6144 + buf * 4096: 6144 + (buf + 1) * 4096]

    SC2B = wbuf[:, 14336:18432]

    def PART(slot):
        return wbuf[:, slot * 8192:(slot + 1) * 8192]

    def WOS(buf):
        return wbuf[:, 16384 + buf * 1024: 16384 + (buf + 1) * 1024]

    G = {}

    def pb_wait(buf):
        P.wait("tensor", "v", G.get(("pbv", buf), 0))
        P.wait("tensor", "s", G.get(("pbs", buf), 0))

    def pb_used(buf, v=None, s=None):
        if v is not None:
            G[("pbv", buf)] = v
        if s is not None:
            G[("pbs", buf)] = s

    P.dma("gpsimd", sgam[:, :], gamT.ap(), "c0")
    P.dma("gpsimd", sbet[:, :], betT.ap(), "c0")
    P.dma("gpsimd", sb1t[:, :], b1T.ap(), "c0")
    P.dma("gpsimd", sqb2[:, :], b2T.ap(), "c0")
    P.dma("gpsimd", tabc[:, :], ropc.ap(), "c0")
    P.dma("gpsimd", tabs[:, :], rops.ap(), "c0")
    P.dma("gpsimd", ident[:, :], identin.ap(), "c0")
    P.dma("gpsimd", sel[:, :], selin.ap(), "c0")
    const_ld = P.cnt["c0"]

    xld = [P.dma("gpsimd", Xv(k, 0, TL), x_fm.ap()[k * 128:(k + 1) * 128, :], "x0")
           for k in range(KC)]

    P.op("vector", lambda E: E.memset(ones_c[:, :], 1.0), "v")
    P.op("vector", lambda E: E.memset(ones_r[:, :], 1.0), "v")
    P.wait("scalar", "c0", const_ld)
    P.op("scalar", lambda E: E.mul(sqb2[:, :], sqb2[:, :], 0.25), "s")

    def emit_ln_quarter(t0, mode, dst, pq=None):
        for k in range(KC):
            b = 0
            P.wait("vector", "x0", xld[-1])
            pv = G.get(("mir", b))
            if pv is not None:
                P.wait("vector", "t", pv)
            P.op("vector", lambda E, k=k, b=b, t0=t0: E.tensor_copy(
                mir[b][:, :], Xv(k, t0, QT)), "v")
            mv = P.op("vector", lambda E, b=b: E.tensor_mul(
                sqt[b][:, :], mir[b][:, :], mir[b][:, :]), "v")
            P.wait("tensor", "v", mv)
            if k == 0:
                P.wait("tensor", "v", G.get(("pss_rd_v"), 0))
                P.wait("tensor", "s", G.get(("pss_rd_s"), 0))
            P.op("tensor", lambda E, b=b, k=k: E.matmul(
                pSS[0:1, :], ones_c[:, :], mir[b][:, :],
                start=(k == 0), stop=(k == KC - 1)), "t")
            tm = P.op("tensor", lambda E, b=b, k=k: E.matmul(
                pSS[32:33, :], ones_c[:, :], sqt[b][:, :],
                start=(k == 0), stop=(k == KC - 1), tile_position=(0, 32)), "t")
            G[("mir", b)] = tm
        P.wait("vector", "t", None)
        P.wait("vector", "t", G.get(("rows_t",), 0))
        P.op("vector", lambda E: E.tensor_scalar_mul(R_M, pSS[0:1, :], -1.0 / D), "v")
        P.op("vector", lambda E: E.tensor_mul(R_TS, R_M, R_M), "v")
        P.op("vector", lambda E: E.scalar_tensor_tensor(
            R_TS, pSS[32:33, :], 1.0 / D, R_TS, ALU.mult, ALU.subtract), "v")
        G[("pss_rd_v")] = P.cnt["v"]
        cv = P.op("vector", lambda E: E.tensor_scalar(
            R_A1, R_TS, EPS, None, ALU.add), "v")
        P.wait("scalar", "v", cv)
        P.op("scalar", lambda E: E.activation(out=R_A1, in_=R_A1, func=AF.Ln), "s")
        ssn = P.op("scalar", lambda E: E.activation(out=R_A1, in_=R_A1, func=AF.Exp, scale=-0.5), "s")
        if mode == "ffn":
            cv2 = P.op("vector", lambda E: E.tensor_scalar(
                R_C, R_TS, 1.0 + EPS, EPS * EPS, ALU.mult, ALU.add), "v")
            P.wait("scalar", "v", cv2)
            P.op("scalar", lambda E: E.activation(out=R_C, in_=R_C, func=AF.Ln), "s")
            ssn = P.op("scalar", lambda E: E.activation(out=R_C, in_=R_C, func=AF.Exp, scale=-0.5), "s")
            P.wait("vector", "s", ssn)
            P.wait("vector", "t", G.get(("pq_t", t0 // QT), 0))
            P.op("vector", lambda E, pq=pq: E.tensor_scalar(pq[0], R_A1, 0.5, 1.0, ALU.mult, ALU.add), "v")
            P.op("vector", lambda E, pq=pq: E.scalar_tensor_tensor(pq[1], R_M, 0.5, R_A1, ALU.mult, ALU.mult), "v")
        rows_v, rows_s = P.cnt["v"], P.cnt["s"]
        P.wait("tensor", "v", rows_v)
        P.wait("tensor", "s", rows_s)
        if mode == "ffn":
            bM = pB0[:, 512:1024]
            bC = pB1[:, 512:1024]
            P.wait("tensor", "v", G.get(("mc_rd_v"), 0))
            P.wait("tensor", "s", G.get(("mc_rd_s"), 0))
        else:
            bM = pM[:, :]
            bC = pC[:, :]
            P.wait("tensor", "v", G.get(("pmc_rd_v"), 0))
            P.wait("tensor", "s", G.get(("pmc_rd_s"), 0))
        P.op("tensor", lambda E, bM=bM: E.matmul(bM, ones_r[:, :], R_M, start=True, stop=True), "t")
        src2 = R_C if mode == "ffn" else R_A1
        bt = P.op("tensor", lambda E, s2=src2, bC=bC: E.matmul(bC, ones_r[:, :], s2, start=True, stop=True), "t")
        G[("rows_t",)] = bt
        P.wait("vector", "t", bt)
        P.wait("vector", "t", G.get(("dst_t",), 0))
        P.wait("scalar", "t", G.get(("dst_t",), 0))
        for k in range(KC):
            pv = G.get(("tb_s"))
            if pv is not None and mode == "ffn":
                P.wait("vector", "s", pv)
            P.op("vector", lambda E, k=k, t0=t0, bM=bM: E.tensor_add(
                tb1[:, :], Xv(k, t0, QT), bM), "v")
            if mode == "ffn":
                tv = P.op("vector", lambda E, bC=bC: E.tensor_mul(tb2[:, :], tb1[:, :], bC), "v")
                P.wait("scalar", "v", tv)
                smk = P.op("scalar", lambda E, k=k: E.activation(
                    out=dst(k), in_=tb2[:, :], func=AF.Identity,
                    bias=sbet[:, k:k + 1], scale=sgam[:, k:k + 1]), "s")
                G[("tb_s")] = smk
            else:
                P.op("vector", lambda E, k=k, bC=bC: E.tensor_mul(dst(k), tb1[:, :], bC), "v")
        if mode == "ffn":
            G[("mc_rd_v")] = P.cnt["v"]
            G[("mc_rd_s")] = P.cnt["s"]
        else:
            G[("pmc_rd_v")] = P.cnt["v"]
            G[("pmc_rd_s")] = P.cnt["s"]
        return P.cnt["v"], P.cnt["s"]

    def emit_ffn(first):
        h_marks = [None] * NQ

        def lnq(tc):
            h_marks[tc] = emit_ln_quarter(
                tc * QT, "ffn", lambda k, tc=tc: Hv(k, tc), pq=PQr[tc])

        lnq(0)
        for q in range(NQ):
            t0 = q * QT
            h_v, h_s = h_marks[q]
            for m in range(MC):
                buf = m % 4
                ls = f"w1s{buf}"
                if first and q == 0:
                    scb = m % 2
                    P.wait("gpsimd", "t", G.get(("w1s", buf), 0))
                    P.wait("gpsimd", f"wbq{buf}", G.get(("wbq", buf), 0))
                    P.wait("gpsimd", "v", G.get(("scb", scb), 0))
                    P.dma("gpsimd", W1S(buf)[:, :],
                          q1.ap()[m * 128:(m + 1) * 128, :], ls)
                    lsc = P.dma("gpsimd", SCB(scb)[:, :],
                                bass.AP(s1, m * D, [[0, 128], [1, D]]), ls)
                    P.wait("vector", ls, lsc)
                    stt = P.op("vector", lambda E, buf=buf, scb=scb: E.scalar_tensor_tensor(
                        W1S(buf)[:, :], W1S(buf)[:, :], -128.0, SCB(scb)[:, :],
                        ALU.add, ALU.mult), "v")
                    G[("scb", scb)] = stt
                    P.wait("sync", "v", stt)
                    wbk = P.dma("sync", w1c.ap()[m * 128:(m + 1) * 128, :],
                                W1S(buf)[:, :], f"wbq{buf}")
                    G[("wbq", buf)] = wbk
                    w_ready = ("v", stt)
                else:
                    g = G.get(("w1s", buf))
                    if g is not None:
                        P.wait("sync", "t", g)
                    if first and q == 1 and m < 4:
                        for wb in range(4):
                            P.wait("sync", f"wbq{wb}", G.get(("wbq", wb), 0))
                    lw = P.dma("sync", W1S(buf)[:, :],
                               w1c.ap()[m * 128:(m + 1) * 128, :], ls)
                    w_ready = (ls, lw)
                pb = pB0 if m % 2 == 0 else pB1
                pb_wait(m % 2)
                P.wait("tensor", w_ready[0], w_ready[1])
                P.wait("tensor", "v", h_v)
                P.wait("tensor", "s", h_s)
                tmk = None
                for k in range(KC):
                    tmk = P.op("tensor", lambda E, pb=pb, buf=buf, k=k, q=q: E.matmul(
                        pb[:, 0:512], W1S(buf)[:, k * 128:(k + 1) * 128], Hv(k, q),
                        start=(k == 0), stop=(k == KC - 1)), "t")
                G[("w1s", buf)] = tmk
                P.wait("scalar", "t", tmk)
                smk = P.op("scalar", lambda E, pb=pb, m=m: E.activation(
                    out=Av(m), in_=pb[:, 0:512], func=AF.Silu,
                    bias=sb1t[:, m:m + 1], scale=1.0), "s")
                pb_used(m % 2, s=smk)
            G[("dst_t",)] = P.cnt["t"]
            a_ready_s = P.cnt["s"]
            pq_v = h_marks[q][0]
            P.wait("tensor", "v", pq_v)
            P.wait("tensor", "v", G.get(("pmc_rd_v"), 0))
            P.op("tensor", lambda E, q=q: E.matmul(pM[:, :], ones_r[:, :], PQr[q][0], start=True, stop=True), "t")
            pqb = P.op("tensor", lambda E, q=q: E.matmul(pC[:, :], ones_r[:, :], PQr[q][1], start=True, stop=True), "t")
            G[("pq_t", q)] = pqb
            G[("pmc_rd_v")] = 0
            G[("pmc_rd_s")] = 0
            for o in range(KC):
                buf = o % 2
                ls = "w2s0" if buf == 0 else "w2s1"
                if first and q == 0:
                    P.wait("gpsimd", "t", G.get(("w2s", buf), 0))
                    P.wait("gpsimd", f"wbo{buf}", G.get(("wbo", buf), 0))
                    P.wait("gpsimd", "v", G.get(("sc2",), 0))
                    P.dma("gpsimd", W2S(buf)[:, :],
                          q2.ap()[o * 128:(o + 1) * 128, :], ls)
                    lsc = P.dma("gpsimd", SC2B[:, :],
                                bass.AP(s2, o * F4, [[0, 128], [1, F4]]), ls)
                    P.wait("vector", ls, lsc)
                    stt = P.op("vector", lambda E, buf=buf: E.scalar_tensor_tensor(
                        W2S(buf)[:, :], W2S(buf)[:, :], -128.0, SC2B[:, :],
                        ALU.add, ALU.mult), "v")
                    G[("sc2",)] = stt
                    P.wait("sync", "v", stt)
                    wbk = P.dma("sync", w2c.ap()[o * 128:(o + 1) * 128, :],
                                W2S(buf)[:, :], f"wbo{buf}")
                    G[("wbo", buf)] = wbk
                    w_ready = ("v", stt)
                else:
                    g = G.get(("w2s", buf))
                    if g is not None:
                        P.wait("sync", "t", g)
                    if first and q == 1 and o < 2:
                        for wb in range(2):
                            P.wait("sync", f"wbo{wb}", G.get(("wbo", wb), 0))
                    lw = P.dma("sync", W2S(buf)[:, :],
                               w2c.ap()[o * 128:(o + 1) * 128, :], ls)
                    w_ready = (ls, lw)
                pb = pB0 if buf == 0 else pB1
                pb_wait(buf)
                P.wait("tensor", w_ready[0], w_ready[1])
                P.wait("tensor", "s", a_ready_s)
                tmk = None
                for k in range(MC):
                    tmk = P.op("tensor", lambda E, pb=pb, buf=buf, k=k: E.matmul(
                        pb[:, 0:512], W2S(buf)[:, k * 128:(k + 1) * 128], Av(k),
                        start=(k == 0), stop=(k == MC - 1)), "t")
                G[("w2s", buf)] = tmk
                zg = G.get(("z4_v"))
                if zg is not None:
                    P.wait("scalar", "v", zg)
                P.wait("scalar", "t", tmk)
                zs = P.op("scalar", lambda E, pb=pb, o=o: E.activation(
                    out=z4t[:, :], in_=pb[:, 0:512], func=AF.Identity,
                    bias=sqb2[:, o:o + 1], scale=0.25), "s")
                pb_used(buf, s=zs)
                P.wait("vector", "t", pqb)
                P.wait("vector", "s", zs)
                P.op("vector", lambda E, o=o, t0=t0: E.tensor_mul(
                    Xv(o, t0, QT), Xv(o, t0, QT), pM[:, :]), "v")
                zv = P.op("vector", lambda E, o=o, t0=t0: E.tensor_add(
                    Xv(o, t0, QT), Xv(o, t0, QT), z4t[:, :]), "v")
                G[("z4_v")] = zv
                vmk = P.op("vector", lambda E, o=o, t0=t0: E.tensor_add(
                    Xv(o, t0, QT), Xv(o, t0, QT), pC[:, :]), "v")
                G[("pmc_rd_v")] = vmk
                if o == 1 and q + 1 < NQ and h_marks[q + 1] is None:
                    lnq(q + 1)
                if not first:
                    P.wait("gpsimd", "v", vmk)
                    P.dma("gpsimd", out_fm.ap()[o * 128:(o + 1) * 128, t0:t0 + QT],
                          Xv(o, t0, QT), "gd")

    def emit_attn():
        ffn1_t = P.cnt["t"]

        y1_marks = [None] * NQ

        def lnq_a(tc):
            y1_marks[tc] = emit_ln_quarter(
                tc * QT, "attn", lambda k, tc=tc: Y1(k, tc * QT, QT))

        lnq_a(0)

        P.wait("gpsimd", "t", ffn1_t)
        for wb in range(4):
            P.wait("gpsimd", f"wbq{wb}", None)
        for wb in range(2):
            P.wait("gpsimd", f"wbo{wb}", None)
        lq = P.dma("gpsimd", PART(0)[:, :].rearrange("p (k n) -> p k n", k=KC),
                   wqkv.ap()[:, 0:1024].rearrange("(k p) n -> p k n", p=128), "p0")

        def qkv_mms(part_slot, tg, pbank, pmark):
            mvv, mss = y1_marks[tg // 4]
            P.wait("tensor", "v", mvv)
            P.wait("tensor", pmark[0], pmark[1])
            tmk = None
            for h in range(2):
                for k in range(KC):
                    tmk = P.op("tensor", lambda E, pbank=pbank, k=k, h=h, tg=tg,
                               ps=part_slot: E.matmul(
                        pbank[:, h * 512:(h + 1) * 512],
                        Y1(k, (tg // 4) * QT + (tg % 4) * 128, 128),
                        PART(ps)[:, k * 1024 + h * 512: k * 1024 + (h + 1) * 512],
                        start=(k == 0), stop=(k == KC - 1)), "t")
            return tmk

        def rope_elu(tg, pbank, scale_q):
            sc = P.op("scalar", lambda E, pb=pbank: E.activation(
                out=rtile, in_=pb[:, :], func=AF.Identity), "s")
            P.wait("vector", "s", sc)
            P.op("vector", lambda E, tg=tg: E.tensor_tensor(
                rcos.rearrange("p (g j) -> p g j", g=16),
                rtile.rearrange("p (g j) -> p g j", g=16),
                bass.AP(tabc, tg * 64, [[NTG * 64, 128], [0, 16], [1, 64]]),
                ALU.mult), "v")
            P.op("vector", lambda E, tg=tg: E.tensor_tensor(
                bass.AP(ovl, 2048, [[8192, 128], [64, 16], [1, 32]]),
                bass.AP(ovl, 32, [[8192, 128], [64, 16], [1, 32]]),
                bass.AP(tabs, tg * 64, [[NTG * 64, 128], [0, 16], [1, 32]]),
                ALU.mult), "v")
            P.op("vector", lambda E, tg=tg: E.tensor_tensor(
                bass.AP(ovl, 2048 + 32, [[8192, 128], [64, 16], [1, 32]]),
                bass.AP(ovl, 0, [[8192, 128], [64, 16], [1, 32]]),
                bass.AP(tabs, tg * 64 + 32, [[NTG * 64, 128], [0, 16], [1, 32]]),
                ALU.mult), "v")
            P.op("vector", lambda E: E.tensor_add(rtile, rcos, rsin), "v")
            s = QSCALE if scale_q else 1.0
            mv = P.op("vector", lambda E, s=s: E.tensor_scalar(
                rcos, rtile, s, 0.0, ALU.mult, ALU.min), "v")
            P.wait("scalar", "v", mv)
            ev = P.op("scalar", lambda E: E.activation(out=rsin, in_=rcos, func=AF.Exp), "s")
            P.wait("vector", "s", ev)
            if scale_q:
                P.op("vector", lambda E, s=s: E.tensor_scalar(
                    rcos, rtile, s, 0.0, ALU.mult, ALU.max), "v")
                fv = P.op("vector", lambda E: E.tensor_add(rtile, rcos, rsin), "v")
            else:
                fv = P.op("vector", lambda E: E.scalar_tensor_tensor(
                    rtile, rtile, 0.0, rsin, ALU.max, ALU.add), "v")
            return fv

        tr_guard = None
        pend = None

        def flush_tr():
            nonlocal tr_guard, pend
            if pend is None:
                return
            tg0, fv0 = pend
            P.wait("tensor", "v", fv0)
            if tr_guard is not None:
                P.wait("tensor", "s", tr_guard)
            ttk = None
            for p in range(NP):
                ttk = P.op("tensor", lambda E, p=p: E.transpose(
                    pTR[:, p * 128:(p + 1) * 128], rtile[:, p * 128:(p + 1) * 128],
                    ident[:, :]), "t")
            P.wait("scalar", "t", ttk)
            tr_guard = P.op("scalar", lambda E, tg0=tg0: E.activation(
                out=bass.AP(arena, tg0 * 128, [[32768, 128], [TL, NP], [1, 128]]),
                in_=pTR[:, :].rearrange("p (g j) -> p g j", g=NP),
                func=AF.Identity), "s")
            pend = None

        for tg in range(NTG):
            if tg % 4 == 1 and tg // 4 + 1 < NQ and y1_marks[tg // 4 + 1] is None:
                lnq_a(tg // 4 + 1)
            pb = pB0 if tg % 2 == 0 else pB1
            pb_wait(tg % 2)
            tmk = qkv_mms(0, tg, pb, ("p0", lq))
            flush_tr()
            P.wait("scalar", "t", tmk)
            P.wait("scalar", "s", tr_guard or 0)
            fv = rope_elu(tg, pb, True)
            pb_used(tg % 2, s=P.cnt["s"])
            pend = (tg, fv)
        flush_tr()
        q_pass_t = P.cnt["t"]

        lk = P.dma("gpsimd", PART(1)[:, :].rearrange("p (k n) -> p k n", k=KC),
                   wqkv.ap()[:, 1024:2048].rearrange("(k p) n -> p k n", p=128), "p1")
        P.wait("gpsimd", "t", q_pass_t)
        lv = P.dma("gpsimd", PART(0)[:, :].rearrange("p (k n) -> p k n", k=KC),
                   wqkv.ap()[:, 2048:3072].rearrange("(k p) n -> p k n", p=128), "p0")

        kv_mm = None
        kv_pend = None

        def flush_kv():
            nonlocal kv_mm, kv_pend
            if kv_pend is None:
                return
            tg0, kcp0, vv0 = kv_pend
            P.wait("tensor", "v", kcp0)
            P.wait("tensor", "s", vv0)
            for p in range(NP):
                pdst = pM if p < 4 else pC
                pcol = (p % 4) * 128
                P.op("tensor", lambda E, p=p, pdst=pdst, pcol=pcol, tg0=tg0: E.matmul(
                    pdst[:, pcol:pcol + 128], kft[0][:, p * 128:(p + 1) * 128],
                    vt[0][:, p * 128:(p + 1) * 128],
                    start=(tg0 == 0 and p % 4 == 0), stop=(tg0 == NTG - 1),
                    skip_group_check=True), "t")
                kv_mm = P.op("tensor", lambda E, p=p, tg0=tg0: E.matmul(
                    pSS[:, p:p + 1], kft[0][:, p * 128:(p + 1) * 128], ones_c[:, :],
                    start=(tg0 == 0 and p == 0), stop=(tg0 == NTG - 1),
                    skip_group_check=True), "t")
            G[("kft", 0)] = kv_mm
            kv_pend = None

        for tg in range(NTG):
            pb_wait(0)
            tmk = qkv_mms(1, tg, pB0, ("p1", lk))
            pb_wait(1)
            tmv = qkv_mms(0, tg, pB1, ("p0", lv))
            flush_kv()
            P.wait("scalar", "t", tmk)
            P.wait("scalar", "v", G.get(("kcp",), 0))
            fv = rope_elu(tg, pB0, False)
            pb_used(0, s=P.cnt["s"])
            kg = G.get(("kft", 0))
            if kg is not None:
                P.wait("vector", "t", kg)
            kcp = P.op("vector", lambda E: E.tensor_copy(kft[0], rtile), "v")
            G[("kcp",)] = kcp
            P.wait("scalar", "t", tmv)
            P.wait("scalar", "t", G.get(("kft", 0), 0))
            vv = P.op("scalar", lambda E: E.activation(
                out=vt[0], in_=pB1[:, :], func=AF.Identity), "s")
            pb_used(1, s=vv)
            kv_pend = (tg, kcp, vv)
            if True:
                pass
        flush_kv()

        P.wait("vector", "t", None)
        P.op("vector", lambda E: E.tensor_copy(kvx[:, 0:512], pM[:, :]), "v")
        P.op("vector", lambda E: E.tensor_copy(kvx[:, 512:1024], pC[:, :]), "v")
        pk = P.op("vector", lambda E: E.tensor_copy(kvx[:, 1024:1032], pSS[:, 0:8]), "v")
        P.wait("gpsimd", "v", pk)
        P.dma("gpsimd", cc_in.ap(), kvx[:, :], "gd")
        P.wait("gpsimd", "gd", None)
        P.op("gpsimd", lambda E: E.collective_compute(
            "AllReduce", ALU.add,
            replica_groups=[list(range(NCORES))],
            ins=[cc_in.ap().opt()], outs=[cc_out.ap().opt()]), "cc")
        P.wait("gpsimd", "cc", None)
        arb = P.dma("gpsimd", kvx[:, :], cc_out.ap(), "cc2")
        P.wait("vector", "cc2", arb)
        P.op("vector", lambda E: E.memset(ksb[:, :], 0.0), "v")
        for p in range(NP):
            P.op("vector", lambda E, p=p: E.tensor_copy(
                ksb[0:64, p * 16 + 2 * p: p * 16 + 2 * p + 1],
                kvx[0:64, 1024 + p:1025 + p]), "v")
            P.op("vector", lambda E, p=p: E.tensor_copy(
                ksb[64:128, p * 16 + 2 * p + 1: p * 16 + 2 * p + 2],
                kvx[64:128, 1024 + p:1025 + p]), "v")
        kv_ready_v = P.cnt["v"]

        G[("qsc_v",)] = kv_ready_v
        den_s = None
        for tc in range(NQ):
            P.wait("tensor", "v", kv_ready_v)
            if den_s is not None:
                P.wait("tensor", "v", G[("dmax_v",)])
            dmk = None
            for p in range(NP):
                dmk = P.op("tensor", lambda E, p=p, tc=tc: E.matmul(
                    pSS[0:16, :], ksb[:, p * 16:(p + 1) * 16], QF(p, tc * QT, QT),
                    start=(p == 0), stop=(p == NP - 1)), "t")
            P.wait("vector", "t", dmk)
            P.wait("vector", "s", G.get(("dens",), 0))
            dvv = P.op("vector", lambda E: E.tensor_scalar_max(dmax, pSS[0:16, :], 1e-6), "v")
            G[("dmax_v",)] = dvv
            P.wait("scalar", "v", dvv)
            P.op("scalar", lambda E: E.activation(out=dmax, in_=dmax, func=AF.Ln), "s")
            den_s = P.op("scalar", lambda E: E.activation(
                out=denr, in_=dmax, func=AF.Exp, scale=-1.0), "s")
            G[("dens",)] = den_s
            for p in range(NP):
                i = (tc * NP + p) % 2
                P.wait("tensor", "s", den_s)
                P.wait("tensor", "v", G.get(("qsc_v",), 0))
                bmk = P.op("tensor", lambda E, p=p: E.matmul(
                    pM[:, :], sel[:, p * 128:(p + 1) * 128],
                    denr, start=True, stop=True), "t")
                P.wait("vector", "t", bmk)
                qv = P.op("vector", lambda E, p=p, tc=tc: E.tensor_mul(
                    QF(p, tc * QT, QT), QF(p, tc * QT, QT), pM[:, :]), "v")
                G[("qsc_v",)] = qv
                pb = pB0 if i == 0 else pB1
                pb_wait(i)
                P.wait("tensor", "v", qv)
                P.op("tensor", lambda E, p=p, tc=tc, pb=pb: E.matmul(
                    pb[0:64, 0:512], kvx[0:64, p * 128:p * 128 + 64],
                    bass.AP(arena, p * TL + tc * QT, [[32768, 64], [1, QT]]),
                    start=True, stop=True, tile_position=(0, 0)), "t")
                amk = P.op("tensor", lambda E, p=p, tc=tc, pb=pb: E.matmul(
                    pb[64:128, 0:512], kvx[64:128, p * 128 + 64:(p + 1) * 128],
                    bass.AP(arena, 64 * 32768 + p * TL + tc * QT, [[32768, 64], [1, QT]]),
                    start=True, stop=True, tile_position=(64, 64)), "t")
                P.wait("scalar", "t", amk)
                smk = P.op("scalar", lambda E, p=p, tc=tc, pb=pb: E.activation(
                    out=Y1(p, tc * QT, QT), in_=pb[:, 0:512], func=AF.Identity), "s")
                pb_used(i, s=smk)
        at_done_s = P.cnt["s"]

        wo_t = None
        for o in range(KC):
            buf = o % 2
            g = G.get(("wos", buf))
            if g is not None:
                P.wait("gpsimd", "t", g)
            if o < 2:
                P.wait("gpsimd", "t", q_pass_t)
            ls = "wos0" if buf == 0 else "wos1"
            lw = P.dma("gpsimd", WOS(buf)[:, :].rearrange("p (k j) -> p k j", k=KC),
                       wout.ap()[:, o * 128:(o + 1) * 128].rearrange("(k p) j -> p k j", p=128),
                       ls)
            for tc in range(NQ):
                i = (o * NQ + tc) % 2
                pb = pB0 if i == 0 else pB1
                pb_wait(i)
                P.wait("tensor", ls, lw)
                P.wait("tensor", "s", at_done_s)
                tmk = None
                for k in range(KC):
                    tmk = P.op("tensor", lambda E, o=o, tc=tc, k=k, pb=pb, buf=buf: E.matmul(
                        pb[:, 0:512], WOS(buf)[:, k * 128:(k + 1) * 128],
                        Y1(k, tc * QT, QT),
                        start=(k == 0), stop=(k == KC - 1)), "t")
                G[("wos", buf)] = tmk
                wo_t = tmk
                P.wait("vector", "t", tmk)
                vmk = P.op("vector", lambda E, o=o, tc=tc, pb=pb: E.tensor_add(
                    Xv(o, tc * QT, QT), Xv(o, tc * QT, QT), pb[:, 0:512]), "v")
                pb_used(i, v=vmk)

        G[("mc_rd_v")] = P.cnt["v"]
        G[("mc_rd_s")] = P.cnt["s"]
        G[("dst_t",)] = P.cnt["t"]
        G[("pmc_rd_v")] = P.cnt["v"]
        G[("pmc_rd_s")] = P.cnt["s"]
        G[("pss_rd_v")] = P.cnt["v"]
        for bb in range(4):
            G[("w1s", bb)] = wo_t
        G[("w2s", 0)] = wo_t
        G[("w2s", 1)] = wo_t
        G[("hq_gate",)] = wo_t

    import os as _os
    _phases = int(_os.environ.get("MK_PHASES", "3"))

    def dump_X():
        vnow = P.cnt["v"]
        P.wait("gpsimd", "v", vnow)
        for o in range(KC):
            P.dma("gpsimd", out_fm.ap()[o * 128:(o + 1) * 128, :], Xv(o, 0, TL), "gd")

    emit_ffn(first=True)
    if _phases == 1:
        dump_X()
    if _phases >= 2:
        emit_attn()
        if _phases == 2:
            dump_X()
    P.wait("vector", "t", None)
    P.wait("scalar", "t", None)
    emit_ffn(first=False)


    P.wait("gpsimd", "gd", None)
    P.wait("sync", "gd", P.cnt["gd"])
    for s in ("c0", "x0", "w1s0", "w1s1", "w1s2", "w1s3", "w2s0", "w2s1",
              "wbq0", "wbq1", "wbq2", "wbq3", "wbo0", "wbo1",
              "p0", "p1", "wos0", "wos1", "cc2"):
        P.wait("sync", s, None)

    with es:
        for s in list(P.cnt):
            P.handles[s] = es.enter_context(nc.semaphore(s))
        blk = es.enter_context(nc.Block())
        for eng in ENGS:
            stream = P.streams[eng]
            def body(E, stream=stream):
                for fn in stream:
                    fn(E)
            getattr(blk, eng)(body)
    return nc


def _rope_tables(core_id):
    pos = np.arange(core_id * TL, (core_id + 1) * TL, dtype=np.float64)
    inv = ROPE_BASE ** (-np.arange(0, 64, 2, dtype=np.float64) / 64.0)
    fr = pos[:, None] * inv[None, :]
    emb = np.concatenate([fr, fr], axis=1)
    cos = np.cos(emb).astype(np.float32)
    sin = np.sin(emb).astype(np.float32)
    ss = sin.copy()
    ss[:, 0:32] *= -1.0
    to_sb = lambda a: np.ascontiguousarray(
        a.reshape(NTG, 128, 64).transpose(1, 0, 2).reshape(128, NTG * 64)
    ).astype(ml_dtypes.bfloat16)
    return to_sb(cos), to_sb(ss)


def kernel(L0_attn_x, L0_attn_W_qkv, L0_attn_W_out, L0_ffn_ln_gamma,
           L0_ffn_ln_beta, L0_ffn_fc1_q, L0_ffn_fc1_scales, L0_ffn_fc1_bias,
           L0_ffn_fc2_q, L0_ffn_fc2_scales, L0_ffn_fc2_bias):
    x = np.ascontiguousarray(np.asarray(L0_attn_x, dtype=np.float32)[0].T)
    wqkv = np.ascontiguousarray(np.asarray(L0_attn_W_qkv, dtype=np.float32))
    wo = np.ascontiguousarray(np.asarray(L0_attn_W_out, dtype=np.float32))
    gamT = np.ascontiguousarray(np.asarray(L0_ffn_ln_gamma, np.float32).reshape(KC, 128).T)
    betT = np.ascontiguousarray(np.asarray(L0_ffn_ln_beta, np.float32).reshape(KC, 128).T)
    b1T = np.ascontiguousarray(np.asarray(L0_ffn_fc1_bias, np.float32).reshape(MC, 128).T)
    b2T = np.ascontiguousarray(np.asarray(L0_ffn_fc2_bias, np.float32).reshape(KC, 128).T)
    q1 = np.ascontiguousarray(np.asarray(L0_ffn_fc1_q, np.int32)
                              .reshape(KC, 128, MC, 128).transpose(2, 1, 0, 3)
                              .reshape(F4, D))
    s1 = np.ascontiguousarray(np.asarray(L0_ffn_fc1_scales, np.float32)
                              .reshape(KC, MC, 128).transpose(1, 0, 2).reshape(MC, D))
    q2 = np.ascontiguousarray(np.asarray(L0_ffn_fc2_q, np.int32)
                              .reshape(MC, 128, KC, 128).transpose(2, 1, 0, 3)
                              .reshape(D, F4))
    s2 = np.ascontiguousarray(np.asarray(L0_ffn_fc2_scales, np.float32)
                              .reshape(MC, KC, 128).transpose(1, 0, 2).reshape(KC, F4))
    iden = np.eye(128, dtype=ml_dtypes.bfloat16)
    selm = np.zeros((16, 1024), dtype=ml_dtypes.bfloat16)
    for p in range(NP):
        selm[2 * p, p * 128:p * 128 + 64] = 1
        selm[2 * p + 1, p * 128 + 64:(p + 1) * 128] = 1

    nc = build_program()
    in_maps = []
    for c in range(NCORES):
        tc, ts = _rope_tables(c)
        in_maps.append(dict(
            x_fm=np.ascontiguousarray(x[:, c * TL:(c + 1) * TL]),
            wqkv=wqkv, wout=wo, gamT=gamT, betT=betT, b1T=b1T, b2T=b2T,
            q1=q1, s1=s1, q2=q2, s2=s2, ropc=tc, rops=ts, identin=iden, selin=selm,
        ))
    res = run_bass_kernel_spmd(nc, in_maps, core_ids=list(range(NCORES)))
    full = np.concatenate([r["out_fm"] for r in res.results], axis=1)
    return np.ascontiguousarray(full.T)[None, :, :]


# version 15
# speedup vs baseline: 1.3351x; 1.0256x over previous
import numpy as np
import ml_dtypes
from contextlib import ExitStack

import concourse.bass as bass
import concourse.mybir as mybir
from concourse.bass_utils import run_bass_kernel_spmd

F32 = mybir.dt.float32
BF16 = mybir.dt.bfloat16
I32 = mybir.dt.int32
AF = mybir.ActivationFunctionType
ALU = mybir.AluOpType

NCORES = 8
D = 1024
TT = 16384
TL = TT // NCORES
KC = 8
F4 = 4096
MC = 32
NQ = 4
QT = TL // NQ
NTG = TL // 128
NP = 8
EPS = 1e-5
ROPE_BASE = 500000.0
QSCALE = 64.0 ** -0.5

ENGS = ("sync", "gpsimd", "tensor", "vector", "scalar")


class Plan:

    def __init__(self):
        self.streams = {e: [] for e in ENGS}
        self.cnt = {}
        self.handles = {}

    def new_sem(self, name):
        self.cnt[name] = 0

    def emit(self, eng, fn):
        self.streams[eng].append(fn)

    def wait(self, eng, sem, val=None):
        v = self.cnt[sem] if val is None else val
        if v is None or v <= 0:
            return
        self.emit(eng, lambda E, s=sem, v=v: E.wait_ge(self.handles[s], v))

    def op(self, eng, fn, sem=None, k=1):
        if sem is None:
            self.emit(eng, fn)
            return None
        self.cnt[sem] += k
        self.emit(eng, lambda E, fn=fn, s=sem, k=k: fn(E).then_inc(self.handles[s], k))
        return self.cnt[sem]

    def dma(self, eng, out, in_, sem):
        return self.op(eng, lambda E, o=out, i=in_: E.dma_start(out=o, in_=i), sem, 16)


def build_program():
    nc = bass.Bass("TRN2", target_bir_lowering=False)
    P = Plan()
    for s in ("sd", "gd", "t", "v", "s", "cc", "c0", "x0",
              "w1s0", "w1s1", "w1s2", "w1s3", "w2s0", "w2s1",
              "wbq0", "wbq1", "wbq2", "wbq3", "wbo0", "wbo1",
              "p0", "p1", "wos0", "wos1", "cc2", "cc3", "gp"):
        P.new_sem(s)

    x_fm = nc.dram_tensor("x_fm", [D, TL], F32, kind="ExternalInput")
    wqkv = nc.dram_tensor("wqkv", [D, 3 * D], F32, kind="ExternalInput")
    wout = nc.dram_tensor("wout", [D, D], F32, kind="ExternalInput")
    gamT = nc.dram_tensor("gamT", [128, KC], F32, kind="ExternalInput")
    betT = nc.dram_tensor("betT", [128, KC], F32, kind="ExternalInput")
    b1T = nc.dram_tensor("b1T", [128, MC], F32, kind="ExternalInput")
    b2T = nc.dram_tensor("b2T", [128, KC], F32, kind="ExternalInput")
    q1 = nc.dram_tensor("q1", [F4, D], I32, kind="ExternalInput")
    s1 = nc.dram_tensor("s1", [MC, D], F32, kind="ExternalInput")
    q2 = nc.dram_tensor("q2", [D, F4], I32, kind="ExternalInput")
    s2 = nc.dram_tensor("s2", [KC, F4], F32, kind="ExternalInput")
    ropc = nc.dram_tensor("ropc", [128, NTG * 64], BF16, kind="ExternalInput")
    rops = nc.dram_tensor("rops", [128, NTG * 64], BF16, kind="ExternalInput")
    identin = nc.dram_tensor("identin", [128, 128], BF16, kind="ExternalInput")
    selin = nc.dram_tensor("selin", [16, 1024], BF16, kind="ExternalInput")
    out_fm = nc.dram_tensor("out_fm", [D, TL], F32, kind="ExternalOutput")

    w1c = nc.dram_tensor("w1c", [F4, D], BF16, kind="Internal")
    w2c = nc.dram_tensor("w2c", [D, F4], BF16, kind="Internal")
    cc_in = nc.dram_tensor("cc_in", [128, 1032], BF16, kind="Internal")
    cc_out = nc.dram_tensor("cc_out", [8 * 128, 1032], BF16, kind="Internal")

    es = ExitStack()
    sb = lambda name, shape, dt: es.enter_context(nc.sbuf_tensor(name, shape, dt))
    ps = lambda name, shape, dt: es.enter_context(nc.psum_tensor(name, shape, dt))

    X = sb("X", [128, KC * TL], F32)
    arena = sb("arena", [128, 32768], BF16)
    wbuf = sb("wbuf", [128, 18432], BF16)
    ovl = sb("ovl", [128, 8192], BF16)
    rtile = ovl[:, 0:1024]
    rcos = ovl[:, 1024:2048]
    rsin = ovl[:, 2048:3072]
    kft = [ovl[:, 3072:4096]]
    vt = [ovl[:, 4096:5120]]
    denr = ovl[0:16, 5120:5632]
    ovf = ovl[:, :].bitcast(F32)
    dmax = ovf[0:16, 2816:3328]
    PQr = [(ovf[0:1, qq * 512:(qq + 1) * 512],
            ovf[0:1, 2048 + qq * 512: 2048 + (qq + 1) * 512]) for qq in range(4)]
    kvx = sb("kvx", [128, 1032], BF16)
    kvt = [sb(f"kvt{i}", [128, 1032], BF16) for i in range(2)]
    ksb = sb("ksb", [128, 128], BF16)
    sel = sb("sel", [16, 1024], BF16)
    mir = [sb("mir0", [128, QT], BF16)]
    sqt = [sb("sqt0", [128, QT], BF16)]
    tb1 = sb("tb1", [128, QT], BF16)
    tb2 = sb("tb2", [128, QT], BF16)
    z4t = sb("z4t", [128, QT], F32)
    rows = sb("rows", [1, 3 * QT], F32)
    sgam = sb("sgam", [128, KC], F32)
    sbet = sb("sbet", [128, KC], F32)
    sb1t = sb("sb1t", [128, MC], F32)
    sqb2 = sb("sqb2", [128, KC], F32)
    tabc = sb("tabc", [128, NTG * 64], BF16)
    tabs = sb("tabs", [128, NTG * 64], BF16)
    ident = sb("ident", [128, 128], BF16)
    ones_c = sb("ones_c", [128, 1], BF16)
    ones_r = sb("ones_r", [1, 128], F32)

    R_M, R_C, R_A1 = (rows[0:1, i * QT:(i + 1) * QT] for i in range(3))
    R_TS = R_C

    pB0 = ps("pB0", [128, 1024], F32)
    pB1 = ps("pB1", [128, 1024], F32)
    pSS = ps("pSS", [128, 512], F32)
    pM = ps("pM", [128, 512], F32)
    pC = ps("pC", [128, 512], F32)
    pTR = ps("pTR", [128, 1024], BF16)

    def Xv(k, c0, n):
        return X[:, k * TL + c0: k * TL + c0 + n]

    def Av(m):
        return arena[:, m * QT:(m + 1) * QT]

    def Hv(k, q):
        return arena[:, 16384 + k * TL + q * QT: 16384 + k * TL + q * QT + QT]

    def QF(p, c0, n):
        return arena[:, p * TL + c0: p * TL + c0 + n]

    def Y1(k, c0, n):
        return arena[:, 16384 + k * TL + c0: 16384 + k * TL + c0 + n]

    def W1S(buf):
        return wbuf[:, buf * 1024:(buf + 1) * 1024]

    def SCB(buf):
        return wbuf[:, 4096 + buf * 1024: 4096 + (buf + 1) * 1024]

    def W2S(buf):
        return wbuf[:, 6144 + buf * 4096: 6144 + (buf + 1) * 4096]

    SC2B = wbuf[:, 14336:18432]

    def PART(slot):
        return wbuf[:, slot * 8192:(slot + 1) * 8192]

    def WOS(buf):
        return wbuf[:, 16384 + buf * 1024: 16384 + (buf + 1) * 1024]

    G = {}

    def pb_wait(buf):
        P.wait("tensor", "v", G.get(("pbv", buf), 0))
        P.wait("tensor", "s", G.get(("pbs", buf), 0))

    def pb_used(buf, v=None, s=None):
        if v is not None:
            G[("pbv", buf)] = v
        if s is not None:
            G[("pbs", buf)] = s

    P.dma("gpsimd", sgam[:, :], gamT.ap(), "c0")
    P.dma("gpsimd", sbet[:, :], betT.ap(), "c0")
    P.dma("gpsimd", sb1t[:, :], b1T.ap(), "c0")
    P.dma("gpsimd", sqb2[:, :], b2T.ap(), "c0")
    P.dma("gpsimd", tabc[:, :], ropc.ap(), "c0")
    P.dma("gpsimd", tabs[:, :], rops.ap(), "c0")
    P.dma("gpsimd", ident[:, :], identin.ap(), "c0")
    P.dma("gpsimd", sel[:, :], selin.ap(), "c0")
    const_ld = P.cnt["c0"]

    xld = [P.dma("gpsimd", Xv(k, 0, TL), x_fm.ap()[k * 128:(k + 1) * 128, :], "x0")
           for k in range(KC)]

    P.op("vector", lambda E: E.memset(ones_c[:, :], 1.0), "v")
    P.op("vector", lambda E: E.memset(ones_r[:, :], 1.0), "v")
    P.wait("scalar", "c0", const_ld)
    P.op("scalar", lambda E: E.mul(sqb2[:, :], sqb2[:, :], 0.25), "s")

    def emit_ln_quarter(t0, mode, dst, pq=None):
        for k in range(KC):
            b = 0
            P.wait("vector", "x0", xld[-1])
            pv = G.get(("mir", b))
            if pv is not None:
                P.wait("vector", "t", pv)
            P.op("vector", lambda E, k=k, b=b, t0=t0: E.tensor_copy(
                mir[b][:, :], Xv(k, t0, QT)), "v")
            mv = P.op("vector", lambda E, b=b: E.tensor_mul(
                sqt[b][:, :], mir[b][:, :], mir[b][:, :]), "v")
            P.wait("tensor", "v", mv)
            if k == 0:
                P.wait("tensor", "v", G.get(("pss_rd_v"), 0))
                P.wait("tensor", "s", G.get(("pss_rd_s"), 0))
            P.op("tensor", lambda E, b=b, k=k: E.matmul(
                pSS[0:1, :], ones_c[:, :], mir[b][:, :],
                start=(k == 0), stop=(k == KC - 1)), "t")
            tm = P.op("tensor", lambda E, b=b, k=k: E.matmul(
                pSS[32:33, :], ones_c[:, :], sqt[b][:, :],
                start=(k == 0), stop=(k == KC - 1), tile_position=(0, 32)), "t")
            G[("mir", b)] = tm
        P.wait("vector", "t", None)
        P.wait("vector", "t", G.get(("rows_t",), 0))
        P.op("vector", lambda E: E.tensor_scalar_mul(R_M, pSS[0:1, :], -1.0 / D), "v")
        P.op("vector", lambda E: E.tensor_mul(R_TS, R_M, R_M), "v")
        P.op("vector", lambda E: E.scalar_tensor_tensor(
            R_TS, pSS[32:33, :], 1.0 / D, R_TS, ALU.mult, ALU.subtract), "v")
        G[("pss_rd_v")] = P.cnt["v"]
        cv = P.op("vector", lambda E: E.tensor_scalar(
            R_A1, R_TS, EPS, None, ALU.add), "v")
        P.wait("scalar", "v", cv)
        P.op("scalar", lambda E: E.activation(out=R_A1, in_=R_A1, func=AF.Ln), "s")
        ssn = P.op("scalar", lambda E: E.activation(out=R_A1, in_=R_A1, func=AF.Exp, scale=-0.5), "s")
        if mode == "ffn":
            cv2 = P.op("vector", lambda E: E.tensor_scalar(
                R_C, R_TS, 1.0 + EPS, EPS * EPS, ALU.mult, ALU.add), "v")
            P.wait("scalar", "v", cv2)
            P.op("scalar", lambda E: E.activation(out=R_C, in_=R_C, func=AF.Ln), "s")
            ssn = P.op("scalar", lambda E: E.activation(out=R_C, in_=R_C, func=AF.Exp, scale=-0.5), "s")
            P.wait("vector", "s", ssn)
            P.wait("vector", "t", G.get(("pq_t", t0 // QT), 0))
            P.op("vector", lambda E, pq=pq: E.tensor_scalar(pq[0], R_A1, 0.5, 1.0, ALU.mult, ALU.add), "v")
            P.op("vector", lambda E, pq=pq: E.scalar_tensor_tensor(pq[1], R_M, 0.5, R_A1, ALU.mult, ALU.mult), "v")
        rows_v, rows_s = P.cnt["v"], P.cnt["s"]
        P.wait("tensor", "v", rows_v)
        P.wait("tensor", "s", rows_s)
        bM = pB0[:, 512:1024]
        bC = pB1[:, 512:1024]
        P.wait("tensor", "v", G.get(("mc_rd_v"), 0))
        P.wait("tensor", "s", G.get(("mc_rd_s"), 0))
        P.op("tensor", lambda E, bM=bM: E.matmul(bM, ones_r[:, :], R_M, start=True, stop=True), "t")
        src2 = R_C if mode == "ffn" else R_A1
        bt = P.op("tensor", lambda E, s2=src2, bC=bC: E.matmul(bC, ones_r[:, :], s2, start=True, stop=True), "t")
        G[("rows_t",)] = bt
        P.wait("vector", "t", bt)
        P.wait("vector", "t", G.get(("dst_t",), 0))
        P.wait("scalar", "t", G.get(("dst_t",), 0))
        for k in range(KC):
            pv = G.get(("tb_s"))
            if pv is not None and mode == "ffn":
                P.wait("vector", "s", pv)
            P.op("vector", lambda E, k=k, t0=t0, bM=bM: E.tensor_add(
                tb1[:, :], Xv(k, t0, QT), bM), "v")
            if mode == "ffn":
                tv = P.op("vector", lambda E, bC=bC: E.tensor_mul(tb2[:, :], tb1[:, :], bC), "v")
                P.wait("scalar", "v", tv)
                smk = P.op("scalar", lambda E, k=k: E.activation(
                    out=dst(k), in_=tb2[:, :], func=AF.Identity,
                    bias=sbet[:, k:k + 1], scale=sgam[:, k:k + 1]), "s")
                G[("tb_s")] = smk
            else:
                P.op("vector", lambda E, k=k, bC=bC: E.tensor_mul(dst(k), tb1[:, :], bC), "v")
        G[("mc_rd_v")] = P.cnt["v"]
        G[("mc_rd_s")] = P.cnt["s"]
        return P.cnt["v"], P.cnt["s"]

    def emit_ffn(first, post_q=None):
        h_marks = [None] * NQ

        def lnq(tc):
            h_marks[tc] = emit_ln_quarter(
                tc * QT, "ffn", lambda k, tc=tc: Hv(k, tc), pq=PQr[tc])

        lnq(0)
        for q in range(NQ):
            t0 = q * QT
            h_v, h_s = h_marks[q]
            for m in range(MC):
                buf = m % 4
                ls = f"w1s{buf}"
                if first and q == 0:
                    scb = m % 2
                    P.wait("gpsimd", "t", G.get(("w1s", buf), 0))
                    P.wait("gpsimd", f"wbq{buf}", G.get(("wbq", buf), 0))
                    P.wait("gpsimd", "v", G.get(("scb", scb), 0))
                    P.dma("gpsimd", W1S(buf)[:, :],
                          q1.ap()[m * 128:(m + 1) * 128, :], ls)
                    lsc = P.dma("gpsimd", SCB(scb)[:, :],
                                bass.AP(s1, m * D, [[0, 128], [1, D]]), ls)
                    P.wait("vector", ls, lsc)
                    stt = P.op("vector", lambda E, buf=buf, scb=scb: E.scalar_tensor_tensor(
                        W1S(buf)[:, :], W1S(buf)[:, :], -128.0, SCB(scb)[:, :],
                        ALU.add, ALU.mult), "v")
                    G[("scb", scb)] = stt
                    P.wait("sync", "v", stt)
                    wbk = P.dma("sync", w1c.ap()[m * 128:(m + 1) * 128, :],
                                W1S(buf)[:, :], f"wbq{buf}")
                    G[("wbq", buf)] = wbk
                    w_ready = ("v", stt)
                else:
                    g = G.get(("w1s", buf))
                    if g is not None:
                        P.wait("sync", "t", g)
                    if first and q == 1 and m < 4:
                        for wb in range(4):
                            P.wait("sync", f"wbq{wb}", G.get(("wbq", wb), 0))
                    lw = P.dma("sync", W1S(buf)[:, :],
                               w1c.ap()[m * 128:(m + 1) * 128, :], ls)
                    w_ready = (ls, lw)
                pb = pB0 if m % 2 == 0 else pB1
                pb_wait(m % 2)
                P.wait("tensor", w_ready[0], w_ready[1])
                P.wait("tensor", "v", h_v)
                P.wait("tensor", "s", h_s)
                tmk = None
                for k in range(KC):
                    tmk = P.op("tensor", lambda E, pb=pb, buf=buf, k=k, q=q: E.matmul(
                        pb[:, 0:512], W1S(buf)[:, k * 128:(k + 1) * 128], Hv(k, q),
                        start=(k == 0), stop=(k == KC - 1)), "t")
                G[("w1s", buf)] = tmk
                P.wait("scalar", "t", tmk)
                smk = P.op("scalar", lambda E, pb=pb, m=m: E.activation(
                    out=Av(m), in_=pb[:, 0:512], func=AF.Silu,
                    bias=sb1t[:, m:m + 1], scale=1.0), "s")
                pb_used(m % 2, s=smk)
            G[("dst_t",)] = P.cnt["t"]
            a_ready_s = P.cnt["s"]
            pq_v = h_marks[q][0]
            P.wait("tensor", "v", pq_v)
            P.wait("tensor", "v", G.get(("pmc_rd_v"), 0))
            P.op("tensor", lambda E, q=q: E.matmul(pM[:, :], ones_r[:, :], PQr[q][0], start=True, stop=True), "t")
            pqb = P.op("tensor", lambda E, q=q: E.matmul(pC[:, :], ones_r[:, :], PQr[q][1], start=True, stop=True), "t")
            G[("pq_t", q)] = pqb
            G[("pmc_rd_v")] = 0
            G[("pmc_rd_s")] = 0
            for o in range(KC):
                buf = o % 2
                ls = "w2s0" if buf == 0 else "w2s1"
                if first and q == 0:
                    P.wait("gpsimd", "t", G.get(("w2s", buf), 0))
                    P.wait("gpsimd", f"wbo{buf}", G.get(("wbo", buf), 0))
                    P.wait("gpsimd", "v", G.get(("sc2",), 0))
                    P.dma("gpsimd", W2S(buf)[:, :],
                          q2.ap()[o * 128:(o + 1) * 128, :], ls)
                    lsc = P.dma("gpsimd", SC2B[:, :],
                                bass.AP(s2, o * F4, [[0, 128], [1, F4]]), ls)
                    P.wait("vector", ls, lsc)
                    stt = P.op("vector", lambda E, buf=buf: E.scalar_tensor_tensor(
                        W2S(buf)[:, :], W2S(buf)[:, :], -128.0, SC2B[:, :],
                        ALU.add, ALU.mult), "v")
                    G[("sc2",)] = stt
                    P.wait("sync", "v", stt)
                    wbk = P.dma("sync", w2c.ap()[o * 128:(o + 1) * 128, :],
                                W2S(buf)[:, :], f"wbo{buf}")
                    G[("wbo", buf)] = wbk
                    w_ready = ("v", stt)
                else:
                    g = G.get(("w2s", buf))
                    if g is not None:
                        P.wait("sync", "t", g)
                    if first and q == 1 and o < 2:
                        for wb in range(2):
                            P.wait("sync", f"wbo{wb}", G.get(("wbo", wb), 0))
                    lw = P.dma("sync", W2S(buf)[:, :],
                               w2c.ap()[o * 128:(o + 1) * 128, :], ls)
                    w_ready = (ls, lw)
                pb = pB0 if buf == 0 else pB1
                pb_wait(buf)
                P.wait("tensor", w_ready[0], w_ready[1])
                P.wait("tensor", "s", a_ready_s)
                tmk = None
                for k in range(MC):
                    tmk = P.op("tensor", lambda E, pb=pb, buf=buf, k=k: E.matmul(
                        pb[:, 0:512], W2S(buf)[:, k * 128:(k + 1) * 128], Av(k),
                        start=(k == 0), stop=(k == MC - 1)), "t")
                G[("w2s", buf)] = tmk
                zg = G.get(("z4_v"))
                if zg is not None:
                    P.wait("scalar", "v", zg)
                P.wait("scalar", "t", tmk)
                zs = P.op("scalar", lambda E, pb=pb, o=o: E.activation(
                    out=z4t[:, :], in_=pb[:, 0:512], func=AF.Identity,
                    bias=sqb2[:, o:o + 1], scale=0.25), "s")
                pb_used(buf, s=zs)
                P.wait("vector", "t", pqb)
                P.wait("vector", "s", zs)
                P.op("vector", lambda E, o=o, t0=t0: E.tensor_mul(
                    Xv(o, t0, QT), Xv(o, t0, QT), pM[:, :]), "v")
                zv = P.op("vector", lambda E, o=o, t0=t0: E.tensor_add(
                    Xv(o, t0, QT), Xv(o, t0, QT), z4t[:, :]), "v")
                G[("z4_v")] = zv
                vmk = P.op("vector", lambda E, o=o, t0=t0: E.tensor_add(
                    Xv(o, t0, QT), Xv(o, t0, QT), pC[:, :]), "v")
                G[("pmc_rd_v")] = vmk
                if o == 1 and q + 1 < NQ and h_marks[q + 1] is None:
                    lnq(q + 1)
                if not first:
                    P.wait("gpsimd", "v", vmk)
                    P.dma("gpsimd", out_fm.ap()[o * 128:(o + 1) * 128, t0:t0 + QT],
                          Xv(o, t0, QT), "gd")
            if post_q is not None:
                post_q(q)

    y1_marks = [None] * NQ

    def lnq_a(tc):
        y1_marks[tc] = emit_ln_quarter(
            tc * QT, "attn", lambda k, tc=tc: Y1(k, tc * QT, QT))

    def emit_attn():
        ffn1_t = P.cnt["t"]
        for tc in range(NQ):
            if y1_marks[tc] is None:
                lnq_a(tc)

        P.wait("gpsimd", "t", ffn1_t)
        for wb in range(4):
            P.wait("gpsimd", f"wbq{wb}", None)
        for wb in range(2):
            P.wait("gpsimd", f"wbo{wb}", None)
        lk = P.dma("gpsimd", PART(0)[:, :].rearrange("p (k n) -> p k n", k=KC),
                   wqkv.ap()[:, 1024:2048].rearrange("(k p) n -> p k n", p=128), "p0")
        lv = P.dma("gpsimd", PART(1)[:, :].rearrange("p (k n) -> p k n", k=KC),
                   wqkv.ap()[:, 2048:3072].rearrange("(k p) n -> p k n", p=128), "p1")

        def qkv_mms(part_slot, tg, pbank, pmark):
            mvv, mss = y1_marks[tg // 4]
            P.wait("tensor", "v", mvv)
            P.wait("tensor", pmark[0], pmark[1])
            tmk = None
            for h in range(2):
                for k in range(KC):
                    tmk = P.op("tensor", lambda E, pbank=pbank, k=k, h=h, tg=tg,
                               ps=part_slot: E.matmul(
                        pbank[:, h * 512:(h + 1) * 512],
                        Y1(k, (tg // 4) * QT + (tg % 4) * 128, 128),
                        PART(ps)[:, k * 1024 + h * 512: k * 1024 + (h + 1) * 512],
                        start=(k == 0), stop=(k == KC - 1)), "t")
            return tmk

        def rope_elu(tg, pbank, scale_q):
            sc = P.op("scalar", lambda E, pb=pbank: E.activation(
                out=rtile, in_=pb[:, :], func=AF.Identity), "s")
            P.wait("vector", "s", sc)
            P.op("vector", lambda E, tg=tg: E.tensor_tensor(
                rcos.rearrange("p (g j) -> p g j", g=16),
                rtile.rearrange("p (g j) -> p g j", g=16),
                bass.AP(tabc, tg * 64, [[NTG * 64, 128], [0, 16], [1, 64]]),
                ALU.mult), "v")
            P.op("vector", lambda E, tg=tg: E.tensor_tensor(
                bass.AP(ovl, 2048, [[8192, 128], [64, 16], [1, 32]]),
                bass.AP(ovl, 32, [[8192, 128], [64, 16], [1, 32]]),
                bass.AP(tabs, tg * 64, [[NTG * 64, 128], [0, 16], [1, 32]]),
                ALU.mult), "v")
            P.op("vector", lambda E, tg=tg: E.tensor_tensor(
                bass.AP(ovl, 2048 + 32, [[8192, 128], [64, 16], [1, 32]]),
                bass.AP(ovl, 0, [[8192, 128], [64, 16], [1, 32]]),
                bass.AP(tabs, tg * 64 + 32, [[NTG * 64, 128], [0, 16], [1, 32]]),
                ALU.mult), "v")
            P.op("vector", lambda E: E.tensor_add(rtile, rcos, rsin), "v")
            s = QSCALE if scale_q else 1.0
            mv = P.op("vector", lambda E, s=s: E.tensor_scalar(
                rcos, rtile, s, 0.0, ALU.mult, ALU.min), "v")
            P.wait("scalar", "v", mv)
            ev = P.op("scalar", lambda E: E.activation(out=rsin, in_=rcos, func=AF.Exp), "s")
            P.wait("vector", "s", ev)
            if scale_q:
                P.op("vector", lambda E, s=s: E.tensor_scalar(
                    rcos, rtile, s, 0.0, ALU.mult, ALU.max), "v")
                fv = P.op("vector", lambda E: E.tensor_add(rtile, rcos, rsin), "v")
            else:
                fv = P.op("vector", lambda E: E.scalar_tensor_tensor(
                    rtile, rtile, 0.0, rsin, ALU.max, ALU.add), "v")
            return fv

        kv_mm = None
        kv_pend = None

        def flush_kv():
            nonlocal kv_mm, kv_pend
            if kv_pend is None:
                return
            tg0, kcp0, vv0 = kv_pend
            P.wait("tensor", "v", kcp0)
            P.wait("tensor", "s", vv0)
            for p in range(NP):
                pdst = pM if p < 4 else pC
                pcol = (p % 4) * 128
                P.op("tensor", lambda E, p=p, pdst=pdst, pcol=pcol, tg0=tg0: E.matmul(
                    pdst[:, pcol:pcol + 128], kft[0][:, p * 128:(p + 1) * 128],
                    vt[0][:, p * 128:(p + 1) * 128],
                    start=(tg0 == 0 and p % 4 == 0), stop=(tg0 == NTG - 1),
                    skip_group_check=True), "t")
                kv_mm = P.op("tensor", lambda E, p=p, tg0=tg0: E.matmul(
                    pSS[:, p:p + 1], kft[0][:, p * 128:(p + 1) * 128], ones_c[:, :],
                    start=(tg0 == 0 and p == 0), stop=(tg0 == NTG - 1),
                    skip_group_check=True), "t")
            G[("kft", 0)] = kv_mm
            kv_pend = None

        for tg in range(NTG):
            pb_wait(0)
            tmk = qkv_mms(0, tg, pB0, ("p0", lk))
            G[("k_done",)] = tmk
            pb_wait(1)
            tmv = qkv_mms(1, tg, pB1, ("p1", lv))
            flush_kv()
            P.wait("scalar", "t", tmk)
            P.wait("scalar", "v", G.get(("kcp",), 0))
            fv = rope_elu(tg, pB0, False)
            pb_used(0, s=P.cnt["s"])
            kg = G.get(("kft", 0))
            if kg is not None:
                P.wait("vector", "t", kg)
            kcp = P.op("vector", lambda E: E.tensor_copy(kft[0], rtile), "v")
            G[("kcp",)] = kcp
            P.wait("scalar", "t", tmv)
            P.wait("scalar", "t", G.get(("kft", 0), 0))
            vv = P.op("scalar", lambda E: E.activation(
                out=vt[0], in_=pB1[:, :], func=AF.Identity), "s")
            pb_used(1, s=vv)
            kv_pend = (tg, kcp, vv)
            if True:
                pass
        flush_kv()

        P.wait("vector", "t", None)
        P.op("vector", lambda E: E.tensor_copy(kvx[:, 0:512], pM[:, :]), "v")
        P.op("vector", lambda E: E.tensor_copy(kvx[:, 512:1024], pC[:, :]), "v")
        pk = P.op("vector", lambda E: E.tensor_copy(kvx[:, 1024:1032], pSS[:, 0:8]), "v")
        P.wait("gpsimd", "v", pk)
        P.dma("gpsimd", cc_in.ap(), kvx[:, :], "gd")
        P.wait("gpsimd", "gd", None)
        P.op("gpsimd", lambda E: E.collective_compute(
            "AllGather", ALU.bypass,
            replica_groups=[list(range(NCORES))],
            ins=[cc_in.ap().opt()], outs=[cc_out.ap().opt()]), "cc")

        P.wait("gpsimd", "t", G[("k_done",)])
        lq = P.dma("gpsimd", PART(0)[:, :].rearrange("p (k n) -> p k n", k=KC),
                   wqkv.ap()[:, 0:1024].rearrange("(k p) n -> p k n", p=128), "p0")

        tr_guard = None
        pend = None

        def flush_tr():
            nonlocal tr_guard, pend
            if pend is None:
                return
            tg0, fv0 = pend
            P.wait("tensor", "v", fv0)
            if tr_guard is not None:
                P.wait("tensor", "s", tr_guard)
            ttk = None
            for p in range(NP):
                ttk = P.op("tensor", lambda E, p=p: E.transpose(
                    pTR[:, p * 128:(p + 1) * 128], rtile[:, p * 128:(p + 1) * 128],
                    ident[:, :]), "t")
            P.wait("scalar", "t", ttk)
            tr_guard = P.op("scalar", lambda E, tg0=tg0: E.activation(
                out=bass.AP(arena, tg0 * 128, [[32768, 128], [TL, NP], [1, 128]]),
                in_=pTR[:, :].rearrange("p (g j) -> p g j", g=NP),
                func=AF.Identity), "s")
            pend = None

        for tg in range(NTG):
            pb = pB0 if tg % 2 == 0 else pB1
            pb_wait(tg % 2)
            tmk = qkv_mms(0, tg, pb, ("p0", lq))
            flush_tr()
            P.wait("scalar", "t", tmk)
            P.wait("scalar", "s", tr_guard or 0)
            fv = rope_elu(tg, pb, True)
            pb_used(tg % 2, s=P.cnt["s"])
            pend = (tg, fv)
        flush_tr()
        q_pass_t = P.cnt["t"]

        lk = P.dma("gpsimd", PART(1)[:, :].rearrange("p (k n) -> p k n", k=KC),
                   wqkv.ap()[:, 1024:2048].rearrange("(k p) n -> p k n", p=128), "p1")
        P.wait("gpsimd", "t", q_pass_t)
        lv = P.dma("gpsimd", PART(0)[:, :].rearrange("p (k n) -> p k n", k=KC),
                   wqkv.ap()[:, 2048:3072].rearrange("(k p) n -> p k n", p=128), "p0")

        P.wait("gpsimd", "cc", None)
        a0 = P.dma("gpsimd", kvx[:, :], cc_out.ap()[0:128, :], "gd")
        P.wait("gpsimd", "gd", None)
        marks = {}
        marks[1] = P.dma("gpsimd", kvt[1][:, :], cc_out.ap()[128:256, :], "cc3")
        marks[2] = P.dma("gpsimd", kvt[0][:, :], cc_out.ap()[256:384, :], "cc2")
        gsum = None
        for c in range(1, 8):
            sem = "cc3" if c % 2 == 1 else "cc2"
            P.wait("gpsimd", sem, marks[c])
            gsum = P.op("gpsimd", lambda E, c=c: E.tensor_add(
                kvx[:, :], kvx[:, :], kvt[c % 2][:, :]), "gp")
            if c + 2 < 8:
                nsem = "cc3" if (c + 2) % 2 == 1 else "cc2"
                marks[c + 2] = P.dma("gpsimd", kvt[c % 2][:, :],
                                     cc_out.ap()[(c + 2) * 128:(c + 3) * 128, :], nsem)

        P.wait("vector", "gp", gsum)
        P.op("vector", lambda E: E.memset(ksb[:, :], 0.0), "v")
        for p in range(NP):
            P.op("vector", lambda E, p=p: E.tensor_copy(
                ksb[0:64, p * 16 + 2 * p: p * 16 + 2 * p + 1],
                kvx[0:64, 1024 + p:1025 + p]), "v")
            P.op("vector", lambda E, p=p: E.tensor_copy(
                ksb[64:128, p * 16 + 2 * p + 1: p * 16 + 2 * p + 2],
                kvx[64:128, 1024 + p:1025 + p]), "v")
        kv_ready_v = P.cnt["v"]

        G[("qsc_v",)] = kv_ready_v
        den_s = None
        for tc in range(NQ):
            P.wait("tensor", "v", kv_ready_v)
            if den_s is not None:
                P.wait("tensor", "v", G[("dmax_v",)])
            dmk = None
            for p in range(NP):
                dmk = P.op("tensor", lambda E, p=p, tc=tc: E.matmul(
                    pSS[0:16, :], ksb[:, p * 16:(p + 1) * 16], QF(p, tc * QT, QT),
                    start=(p == 0), stop=(p == NP - 1)), "t")
            P.wait("vector", "t", dmk)
            P.wait("vector", "s", G.get(("dens",), 0))
            dvv = P.op("vector", lambda E: E.tensor_scalar_max(dmax, pSS[0:16, :], 1e-6), "v")
            G[("dmax_v",)] = dvv
            P.wait("scalar", "v", dvv)
            P.op("scalar", lambda E: E.activation(out=dmax, in_=dmax, func=AF.Ln), "s")
            den_s = P.op("scalar", lambda E: E.activation(
                out=denr, in_=dmax, func=AF.Exp, scale=-1.0), "s")
            G[("dens",)] = den_s
            for p in range(NP):
                i = (tc * NP + p) % 2
                P.wait("tensor", "s", den_s)
                P.wait("tensor", "v", G.get(("qsc_v",), 0))
                bmk = P.op("tensor", lambda E, p=p: E.matmul(
                    pM[:, :], sel[:, p * 128:(p + 1) * 128],
                    denr, start=True, stop=True), "t")
                P.wait("vector", "t", bmk)
                qv = P.op("vector", lambda E, p=p, tc=tc: E.tensor_mul(
                    QF(p, tc * QT, QT), QF(p, tc * QT, QT), pM[:, :]), "v")
                G[("qsc_v",)] = qv
                pb = pB0 if i == 0 else pB1
                pb_wait(i)
                P.wait("tensor", "v", qv)
                P.op("tensor", lambda E, p=p, tc=tc, pb=pb: E.matmul(
                    pb[0:64, 0:512], kvx[0:64, p * 128:p * 128 + 64],
                    bass.AP(arena, p * TL + tc * QT, [[32768, 64], [1, QT]]),
                    start=True, stop=True, tile_position=(0, 0)), "t")
                amk = P.op("tensor", lambda E, p=p, tc=tc, pb=pb: E.matmul(
                    pb[64:128, 0:512], kvx[64:128, p * 128 + 64:(p + 1) * 128],
                    bass.AP(arena, 64 * 32768 + p * TL + tc * QT, [[32768, 64], [1, QT]]),
                    start=True, stop=True, tile_position=(64, 64)), "t")
                P.wait("scalar", "t", amk)
                smk = P.op("scalar", lambda E, p=p, tc=tc, pb=pb: E.activation(
                    out=Y1(p, tc * QT, QT), in_=pb[:, 0:512], func=AF.Identity), "s")
                pb_used(i, s=smk)
        at_done_s = P.cnt["s"]

        wo_t = None
        for o in range(KC):
            buf = o % 2
            g = G.get(("wos", buf))
            if g is not None:
                P.wait("gpsimd", "t", g)
            if o < 2:
                P.wait("gpsimd", "t", q_pass_t)
            ls = "wos0" if buf == 0 else "wos1"
            lw = P.dma("gpsimd", WOS(buf)[:, :].rearrange("p (k j) -> p k j", k=KC),
                       wout.ap()[:, o * 128:(o + 1) * 128].rearrange("(k p) j -> p k j", p=128),
                       ls)
            for tc in range(NQ):
                i = (o * NQ + tc) % 2
                pb = pB0 if i == 0 else pB1
                pb_wait(i)
                P.wait("tensor", ls, lw)
                P.wait("tensor", "s", at_done_s)
                tmk = None
                for k in range(KC):
                    tmk = P.op("tensor", lambda E, o=o, tc=tc, k=k, pb=pb, buf=buf: E.matmul(
                        pb[:, 0:512], WOS(buf)[:, k * 128:(k + 1) * 128],
                        Y1(k, tc * QT, QT),
                        start=(k == 0), stop=(k == KC - 1)), "t")
                G[("wos", buf)] = tmk
                wo_t = tmk
                P.wait("vector", "t", tmk)
                vmk = P.op("vector", lambda E, o=o, tc=tc, pb=pb: E.tensor_add(
                    Xv(o, tc * QT, QT), Xv(o, tc * QT, QT), pb[:, 0:512]), "v")
                pb_used(i, v=vmk)

        G[("mc_rd_v")] = P.cnt["v"]
        G[("mc_rd_s")] = P.cnt["s"]
        G[("dst_t",)] = P.cnt["t"]
        G[("pmc_rd_v")] = P.cnt["v"]
        G[("pmc_rd_s")] = P.cnt["s"]
        G[("pss_rd_v")] = P.cnt["v"]
        for bb in range(4):
            G[("w1s", bb)] = wo_t
        G[("w2s", 0)] = wo_t
        G[("w2s", 1)] = wo_t
        G[("hq_gate",)] = wo_t

    import os as _os
    _phases = int(_os.environ.get("MK_PHASES", "3"))

    def dump_X():
        vnow = P.cnt["v"]
        P.wait("gpsimd", "v", vnow)
        for o in range(KC):
            P.dma("gpsimd", out_fm.ap()[o * 128:(o + 1) * 128, :], Xv(o, 0, TL), "gd")

    emit_ffn(first=True, post_q=lambda q: lnq_a(q))
    if _phases == 1:
        dump_X()
    if _phases >= 2:
        emit_attn()
        if _phases == 2:
            dump_X()
    P.wait("vector", "t", None)
    P.wait("scalar", "t", None)
    emit_ffn(first=False)


    P.wait("gpsimd", "gd", None)
    P.wait("sync", "gd", P.cnt["gd"])
    for s in ("c0", "x0", "w1s0", "w1s1", "w1s2", "w1s3", "w2s0", "w2s1",
              "wbq0", "wbq1", "wbq2", "wbq3", "wbo0", "wbo1",
              "p0", "p1", "wos0", "wos1", "cc2"):
        P.wait("sync", s, None)

    with es:
        for s in list(P.cnt):
            P.handles[s] = es.enter_context(nc.semaphore(s))
        blk = es.enter_context(nc.Block())
        for eng in ENGS:
            stream = P.streams[eng]
            def body(E, stream=stream):
                for fn in stream:
                    fn(E)
            getattr(blk, eng)(body)
    return nc


def _rope_tables(core_id):
    pos = np.arange(core_id * TL, (core_id + 1) * TL, dtype=np.float64)
    inv = ROPE_BASE ** (-np.arange(0, 64, 2, dtype=np.float64) / 64.0)
    fr = pos[:, None] * inv[None, :]
    emb = np.concatenate([fr, fr], axis=1)
    cos = np.cos(emb).astype(np.float32)
    sin = np.sin(emb).astype(np.float32)
    ss = sin.copy()
    ss[:, 0:32] *= -1.0
    to_sb = lambda a: np.ascontiguousarray(
        a.reshape(NTG, 128, 64).transpose(1, 0, 2).reshape(128, NTG * 64)
    ).astype(ml_dtypes.bfloat16)
    return to_sb(cos), to_sb(ss)


def kernel(L0_attn_x, L0_attn_W_qkv, L0_attn_W_out, L0_ffn_ln_gamma,
           L0_ffn_ln_beta, L0_ffn_fc1_q, L0_ffn_fc1_scales, L0_ffn_fc1_bias,
           L0_ffn_fc2_q, L0_ffn_fc2_scales, L0_ffn_fc2_bias):
    x = np.ascontiguousarray(np.asarray(L0_attn_x, dtype=np.float32)[0].T)
    wqkv = np.ascontiguousarray(np.asarray(L0_attn_W_qkv, dtype=np.float32))
    wo = np.ascontiguousarray(np.asarray(L0_attn_W_out, dtype=np.float32))
    gamT = np.ascontiguousarray(np.asarray(L0_ffn_ln_gamma, np.float32).reshape(KC, 128).T)
    betT = np.ascontiguousarray(np.asarray(L0_ffn_ln_beta, np.float32).reshape(KC, 128).T)
    b1T = np.ascontiguousarray(np.asarray(L0_ffn_fc1_bias, np.float32).reshape(MC, 128).T)
    b2T = np.ascontiguousarray(np.asarray(L0_ffn_fc2_bias, np.float32).reshape(KC, 128).T)
    q1 = np.ascontiguousarray(np.asarray(L0_ffn_fc1_q, np.int32)
                              .reshape(KC, 128, MC, 128).transpose(2, 1, 0, 3)
                              .reshape(F4, D))
    s1 = np.ascontiguousarray(np.asarray(L0_ffn_fc1_scales, np.float32)
                              .reshape(KC, MC, 128).transpose(1, 0, 2).reshape(MC, D))
    q2 = np.ascontiguousarray(np.asarray(L0_ffn_fc2_q, np.int32)
                              .reshape(MC, 128, KC, 128).transpose(2, 1, 0, 3)
                              .reshape(D, F4))
    s2 = np.ascontiguousarray(np.asarray(L0_ffn_fc2_scales, np.float32)
                              .reshape(MC, KC, 128).transpose(1, 0, 2).reshape(KC, F4))
    iden = np.eye(128, dtype=ml_dtypes.bfloat16)
    selm = np.zeros((16, 1024), dtype=ml_dtypes.bfloat16)
    for p in range(NP):
        selm[2 * p, p * 128:p * 128 + 64] = 1
        selm[2 * p + 1, p * 128 + 64:(p + 1) * 128] = 1

    nc = build_program()
    in_maps = []
    for c in range(NCORES):
        tc, ts = _rope_tables(c)
        in_maps.append(dict(
            x_fm=np.ascontiguousarray(x[:, c * TL:(c + 1) * TL]),
            wqkv=wqkv, wout=wo, gamT=gamT, betT=betT, b1T=b1T, b2T=b2T,
            q1=q1, s1=s1, q2=q2, s2=s2, ropc=tc, rops=ts, identin=iden, selin=selm,
        ))
    res = run_bass_kernel_spmd(nc, in_maps, core_ids=list(range(NCORES)))
    full = np.concatenate([r["out_fm"] for r in res.results], axis=1)
    return np.ascontiguousarray(full.T)[None, :, :]


# version 16
# speedup vs baseline: 1.3450x; 1.0074x over previous
import numpy as np
import ml_dtypes
from contextlib import ExitStack

import concourse.bass as bass
import concourse.mybir as mybir
from concourse.bass_utils import run_bass_kernel_spmd

F32 = mybir.dt.float32
BF16 = mybir.dt.bfloat16
I32 = mybir.dt.int32
AF = mybir.ActivationFunctionType
ALU = mybir.AluOpType

NCORES = 8
D = 1024
TT = 16384
TL = TT // NCORES
KC = 8
F4 = 4096
MC = 32
NQ = 4
QT = TL // NQ
NTG = TL // 128
NP = 8
EPS = 1e-5
ROPE_BASE = 500000.0
QSCALE = 64.0 ** -0.5

ENGS = ("sync", "gpsimd", "tensor", "vector", "scalar")


class Plan:

    def __init__(self):
        self.streams = {e: [] for e in ENGS}
        self.cnt = {}
        self.handles = {}

    def new_sem(self, name):
        self.cnt[name] = 0

    def emit(self, eng, fn):
        self.streams[eng].append(fn)

    def wait(self, eng, sem, val=None):
        v = self.cnt[sem] if val is None else val
        if v is None or v <= 0:
            return
        self.emit(eng, lambda E, s=sem, v=v: E.wait_ge(self.handles[s], v))

    def op(self, eng, fn, sem=None, k=1):
        if sem is None:
            self.emit(eng, fn)
            return None
        self.cnt[sem] += k
        self.emit(eng, lambda E, fn=fn, s=sem, k=k: fn(E).then_inc(self.handles[s], k))
        return self.cnt[sem]

    def dma(self, eng, out, in_, sem):
        return self.op(eng, lambda E, o=out, i=in_: E.dma_start(out=o, in_=i), sem, 16)


def build_program():
    nc = bass.Bass("TRN2", target_bir_lowering=False)
    P = Plan()
    for s in ("sd", "gd", "t", "v", "s", "cc", "c0", "x0",
              "w1s0", "w1s1", "w1s2", "w1s3", "w2s0", "w2s1",
              "wbq0", "wbq1", "wbq2", "wbq3", "wbo0", "wbo1",
              "p0", "p1", "wos0", "wos1", "cc2", "cc3", "gp",
              "qi0", "qi1", "qi2", "qi3", "f20", "f21", "sc2l"):
        P.new_sem(s)

    x_fm = nc.dram_tensor("x_fm", [D, TL], F32, kind="ExternalInput")
    wqkv = nc.dram_tensor("wqkv", [D, 3 * D], F32, kind="ExternalInput")
    wout = nc.dram_tensor("wout", [D, D], F32, kind="ExternalInput")
    gamT = nc.dram_tensor("gamT", [128, KC], F32, kind="ExternalInput")
    betT = nc.dram_tensor("betT", [128, KC], F32, kind="ExternalInput")
    b1T = nc.dram_tensor("b1T", [128, MC], F32, kind="ExternalInput")
    b2T = nc.dram_tensor("b2T", [128, KC], F32, kind="ExternalInput")
    q1 = nc.dram_tensor("q1", [F4, D], I32, kind="ExternalInput")
    s1 = nc.dram_tensor("s1", [MC, D], F32, kind="ExternalInput")
    q2 = nc.dram_tensor("q2", [D, F4], I32, kind="ExternalInput")
    s2 = nc.dram_tensor("s2", [KC, F4], F32, kind="ExternalInput")
    ropc = nc.dram_tensor("ropc", [128, NTG * 64], BF16, kind="ExternalInput")
    rops = nc.dram_tensor("rops", [128, NTG * 64], BF16, kind="ExternalInput")
    identin = nc.dram_tensor("identin", [128, 128], BF16, kind="ExternalInput")
    selin = nc.dram_tensor("selin", [16, 1024], BF16, kind="ExternalInput")
    out_fm = nc.dram_tensor("out_fm", [D, TL], F32, kind="ExternalOutput")

    w1c = nc.dram_tensor("w1c", [F4, D], BF16, kind="Internal")
    w2c = nc.dram_tensor("w2c", [D, F4], BF16, kind="Internal")
    cc_in = nc.dram_tensor("cc_in", [128, 1032], BF16, kind="Internal")
    cc_out = nc.dram_tensor("cc_out", [8 * 128, 1032], BF16, kind="Internal")

    es = ExitStack()
    sb = lambda name, shape, dt: es.enter_context(nc.sbuf_tensor(name, shape, dt))
    ps = lambda name, shape, dt: es.enter_context(nc.psum_tensor(name, shape, dt))

    X = sb("X", [128, KC * TL], F32)
    arena = sb("arena", [128, 32768], BF16)
    wbuf = sb("wbuf", [128, 18432], BF16)
    ovl = sb("ovl", [128, 8192], BF16)
    rtile = ovl[:, 0:1024]
    rcos = ovl[:, 1024:2048]
    rsin = ovl[:, 2048:3072]
    kft = [ovl[:, 3072:4096]]
    vt = [ovl[:, 4096:5120]]
    denr = ovl[0:16, 5120:5632]
    ovf = ovl[:, :].bitcast(F32)
    dmax = ovf[0:16, 2816:3328]
    PQr = [(ovf[0:1, qq * 512:(qq + 1) * 512],
            ovf[0:1, 2048 + qq * 512: 2048 + (qq + 1) * 512]) for qq in range(4)]
    kvx = sb("kvx", [128, 1032], BF16)
    kvt = [sb(f"kvt{i}", [128, 1032], BF16) for i in range(2)]
    ksb = sb("ksb", [128, 128], BF16)
    sel = sb("sel", [16, 1024], BF16)
    mir = [sb("mir0", [128, QT], BF16)]
    sqt = [sb("sqt0", [128, QT], BF16)]
    tb1 = sb("tb1", [128, QT], BF16)
    tb2 = sb("tb2", [128, QT], BF16)
    z4t = sb("z4t", [128, QT], F32)
    rows = sb("rows", [1, 3 * QT], F32)
    sgam = sb("sgam", [128, KC], F32)
    sbet = sb("sbet", [128, KC], F32)
    sb1t = sb("sb1t", [128, MC], F32)
    sqb2 = sb("sqb2", [128, KC], F32)
    tabc = sb("tabc", [128, NTG * 64], BF16)
    tabs = sb("tabs", [128, NTG * 64], BF16)
    ident = sb("ident", [128, 128], BF16)
    ones_c = sb("ones_c", [128, 1], BF16)
    ones_r = sb("ones_r", [1, 128], F32)
    n128 = sb("n128", [128, 1], F32)

    R_M, R_C, R_A1 = (rows[0:1, i * QT:(i + 1) * QT] for i in range(3))
    R_TS = R_C

    pB0 = ps("pB0", [128, 1024], F32)
    pB1 = ps("pB1", [128, 1024], F32)
    pSS = ps("pSS", [128, 512], F32)
    pM = ps("pM", [128, 512], F32)
    pC = ps("pC", [128, 512], F32)
    pTR = ps("pTR", [128, 1024], BF16)

    def Xv(k, c0, n):
        return X[:, k * TL + c0: k * TL + c0 + n]

    def Av(m):
        return arena[:, m * QT:(m + 1) * QT]

    def Hv(k, q):
        return arena[:, 16384 + k * TL + q * QT: 16384 + k * TL + q * QT + QT]

    def QF(p, c0, n):
        return arena[:, p * TL + c0: p * TL + c0 + n]

    def Y1(k, c0, n):
        return arena[:, 16384 + k * TL + c0: 16384 + k * TL + c0 + n]

    def W1S(buf):
        return wbuf[:, buf * 1024:(buf + 1) * 1024]

    def SCB(buf):
        return wbuf[:, 4096 + buf * 1024: 4096 + (buf + 1) * 1024]

    def W2S(buf):
        return wbuf[:, 6144 + buf * 4096: 6144 + (buf + 1) * 4096]

    def QI1(buf):
        return wbuf[:, 6144 + buf * 2048: 6144 + (buf + 1) * 2048].bitcast(I32)

    def ST2(j):
        return arena[:, 16384 + j * 2048 + 1024:
                     16384 + j * 2048 + 2048].bitcast(I32)

    SC2B = wbuf[:, 14336:18432]

    def PART(slot):
        return wbuf[:, slot * 8192:(slot + 1) * 8192]

    def WOS(buf):
        return wbuf[:, 16384 + buf * 1024: 16384 + (buf + 1) * 1024]

    G = {}

    def pb_wait(buf):
        P.wait("tensor", "v", G.get(("pbv", buf), 0))
        P.wait("tensor", "s", G.get(("pbs", buf), 0))

    def pb_used(buf, v=None, s=None):
        if v is not None:
            G[("pbv", buf)] = v
        if s is not None:
            G[("pbs", buf)] = s

    P.dma("gpsimd", sgam[:, :], gamT.ap(), "c0")
    P.dma("gpsimd", sbet[:, :], betT.ap(), "c0")
    P.dma("gpsimd", sb1t[:, :], b1T.ap(), "c0")
    P.dma("gpsimd", sqb2[:, :], b2T.ap(), "c0")
    P.dma("gpsimd", tabc[:, :], ropc.ap(), "c0")
    P.dma("gpsimd", tabs[:, :], rops.ap(), "c0")
    P.dma("gpsimd", ident[:, :], identin.ap(), "c0")
    P.dma("gpsimd", sel[:, :], selin.ap(), "c0")
    const_ld = P.cnt["c0"]

    xld = [P.dma("gpsimd", Xv(k, 0, TL), x_fm.ap()[k * 128:(k + 1) * 128, :], "x0")
           for k in range(KC)]

    P.op("vector", lambda E: E.memset(ones_c[:, :], 1.0), "v")
    P.op("vector", lambda E: E.memset(ones_r[:, :], 1.0), "v")
    n128_v = P.op("vector", lambda E: E.memset(n128[:, :], -128.0), "v")
    P.wait("scalar", "c0", const_ld)
    P.op("scalar", lambda E: E.mul(sqb2[:, :], sqb2[:, :], 0.25), "s")

    def emit_ln_quarter(t0, mode, dst, pq=None):
        for k in range(KC):
            b = 0
            P.wait("vector", "x0", xld[-1])
            pv = G.get(("mir", b))
            if pv is not None:
                P.wait("vector", "t", pv)
            P.op("vector", lambda E, k=k, b=b, t0=t0: E.tensor_copy(
                mir[b][:, :], Xv(k, t0, QT)), "v")
            mv = P.op("vector", lambda E, b=b: E.tensor_mul(
                sqt[b][:, :], mir[b][:, :], mir[b][:, :]), "v")
            P.wait("tensor", "v", mv)
            if k == 0:
                P.wait("tensor", "v", G.get(("pss_rd_v"), 0))
                P.wait("tensor", "s", G.get(("pss_rd_s"), 0))
            P.op("tensor", lambda E, b=b, k=k: E.matmul(
                pSS[0:1, :], ones_c[:, :], mir[b][:, :],
                start=(k == 0), stop=(k == KC - 1)), "t")
            tm = P.op("tensor", lambda E, b=b, k=k: E.matmul(
                pSS[32:33, :], ones_c[:, :], sqt[b][:, :],
                start=(k == 0), stop=(k == KC - 1), tile_position=(0, 32)), "t")
            G[("mir", b)] = tm
        P.wait("vector", "t", None)
        P.wait("vector", "t", G.get(("rows_t",), 0))
        P.op("vector", lambda E: E.tensor_scalar_mul(R_M, pSS[0:1, :], -1.0 / D), "v")
        P.op("vector", lambda E: E.tensor_mul(R_TS, R_M, R_M), "v")
        P.op("vector", lambda E: E.scalar_tensor_tensor(
            R_TS, pSS[32:33, :], 1.0 / D, R_TS, ALU.mult, ALU.subtract), "v")
        G[("pss_rd_v")] = P.cnt["v"]
        cv = P.op("vector", lambda E: E.tensor_scalar(
            R_A1, R_TS, EPS, None, ALU.add), "v")
        P.wait("scalar", "v", cv)
        P.op("scalar", lambda E: E.activation(out=R_A1, in_=R_A1, func=AF.Ln), "s")
        ssn = P.op("scalar", lambda E: E.activation(out=R_A1, in_=R_A1, func=AF.Exp, scale=-0.5), "s")
        if mode == "ffn":
            cv2 = P.op("vector", lambda E: E.tensor_scalar(
                R_C, R_TS, 1.0 + EPS, EPS * EPS, ALU.mult, ALU.add), "v")
            P.wait("scalar", "v", cv2)
            P.op("scalar", lambda E: E.activation(out=R_C, in_=R_C, func=AF.Ln), "s")
            ssn = P.op("scalar", lambda E: E.activation(out=R_C, in_=R_C, func=AF.Exp, scale=-0.5), "s")
            P.wait("vector", "s", ssn)
            P.wait("vector", "t", G.get(("pq_t", t0 // QT), 0))
            P.op("vector", lambda E, pq=pq: E.tensor_scalar(pq[0], R_A1, 0.5, 1.0, ALU.mult, ALU.add), "v")
            P.op("vector", lambda E, pq=pq: E.scalar_tensor_tensor(pq[1], R_M, 0.5, R_A1, ALU.mult, ALU.mult), "v")
        rows_v, rows_s = P.cnt["v"], P.cnt["s"]
        P.wait("tensor", "v", rows_v)
        P.wait("tensor", "s", rows_s)
        bM = pB0[:, 512:1024]
        bC = pB1[:, 512:1024]
        P.wait("tensor", "v", G.get(("mc_rd_v"), 0))
        P.wait("tensor", "s", G.get(("mc_rd_s"), 0))
        P.op("tensor", lambda E, bM=bM: E.matmul(bM, ones_r[:, :], R_M, start=True, stop=True), "t")
        src2 = R_C if mode == "ffn" else R_A1
        bt = P.op("tensor", lambda E, s2=src2, bC=bC: E.matmul(bC, ones_r[:, :], s2, start=True, stop=True), "t")
        G[("rows_t",)] = bt
        P.wait("vector", "t", bt)
        P.wait("vector", "t", G.get(("dst_t",), 0))
        P.wait("scalar", "t", G.get(("dst_t",), 0))
        for k in range(KC):
            pv = G.get(("tb_s"))
            if pv is not None and mode == "ffn":
                P.wait("vector", "s", pv)
            P.op("vector", lambda E, k=k, t0=t0, bM=bM: E.tensor_add(
                tb1[:, :], Xv(k, t0, QT), bM), "v")
            if mode == "ffn":
                tv = P.op("vector", lambda E, bC=bC: E.tensor_mul(tb2[:, :], tb1[:, :], bC), "v")
                P.wait("scalar", "v", tv)
                smk = P.op("scalar", lambda E, k=k: E.activation(
                    out=dst(k), in_=tb2[:, :], func=AF.Identity,
                    bias=sbet[:, k:k + 1], scale=sgam[:, k:k + 1]), "s")
                G[("tb_s")] = smk
            else:
                P.op("vector", lambda E, k=k, bC=bC: E.tensor_mul(dst(k), tb1[:, :], bC), "v")
        G[("mc_rd_v")] = P.cnt["v"]
        G[("mc_rd_s")] = P.cnt["s"]
        return P.cnt["v"], P.cnt["s"]

    def emit_ffn(first, post_q=None):
        h_marks = [None] * NQ

        def lnq(tc):
            h_marks[tc] = emit_ln_quarter(
                tc * QT, "ffn", lambda k, tc=tc: Hv(k, tc), pq=PQr[tc])

        import os as _os2
        _old1 = _os2.environ.get("MK_OLD1", "1") == "1"
        _old2 = _os2.environ.get("MK_OLD2", "0") == "1"
        lnq(0)
        for q in range(NQ):
            t0 = q * QT
            h_v, h_s = h_marks[q]
            if first and q == 0 and not _old1:
                w1rdy, w1dma, w1scl = {}, {}, {}

                def dma1(m):
                    buf, scb = m % 4, m % 2
                    qiq = "sync" if m % 2 == 0 else "scalar"
                    cvp = G.get(("qis", buf))
                    if cvp is not None:
                        P.wait(qiq, "s", cvp)
                    w1dma[m] = P.dma(qiq, QI1(buf),
                                     q1.ap()[m * 128:(m + 1) * 128, :], f"qi{buf}")
                    P.wait("gpsimd", "v", G.get(("scb", scb), 0))
                    w1scl[m] = P.dma("gpsimd", SCB(scb)[:, :],
                                     bass.AP(s1, m * D, [[0, 128], [1, D]]),
                                     "sc2l")

                def wb1(m):
                    buf = m % 4
                    qw = "sync" if m % 2 == 0 else "scalar"
                    P.wait(qw, "v", w1rdy[m][1])
                    G[("wbq", buf)] = P.dma(
                        qw, w1c.ap()[m * 128:(m + 1) * 128, :],
                        W1S(buf)[:, :], f"wbq{buf}")

                def conv1(m):
                    buf, scb = m % 4, m % 2
                    if m - 2 >= 0:
                        wb1(m - 2)
                    P.wait("scalar", f"qi{buf}", w1dma[m])
                    P.wait("scalar", "t", G.get(("w1s", buf), 0))
                    P.wait("scalar", f"wbq{buf}", G.get(("wbq", buf), 0))
                    P.wait("scalar", "v", n128_v)
                    cvm = P.op("scalar", lambda E, buf=buf: E.activation(
                        out=W1S(buf)[:, :], in_=QI1(buf), func=AF.Identity,
                        bias=n128[:, 0:1], scale=1.0), "s")
                    G[("qis", buf)] = cvm
                    P.wait("vector", "sc2l", w1scl[m])
                    P.wait("vector", "s", cvm)
                    vm = P.op("vector", lambda E, buf=buf, scb=scb: E.tensor_mul(
                        W1S(buf)[:, :], W1S(buf)[:, :], SCB(scb)[:, :]), "v")
                    G[("scb", scb)] = vm
                    w1rdy[m] = ("v", vm)

                dma1(0)
                dma1(1)
                conv1(0)
                conv1(1)
                dma1(2)
                dma1(3)
            for m in range(MC):
                buf = m % 4
                ls = f"w1s{buf}"
                if first and q == 0 and _old1:
                    scb = m % 2
                    P.wait("gpsimd", "t", G.get(("w1s", buf), 0))
                    P.wait("gpsimd", f"wbq{buf}", G.get(("wbq", buf), 0))
                    P.wait("gpsimd", "v", G.get(("scb", scb), 0))
                    P.dma("gpsimd", W1S(buf)[:, :],
                          q1.ap()[m * 128:(m + 1) * 128, :], ls)
                    lsc = P.dma("gpsimd", SCB(scb)[:, :],
                                bass.AP(s1, m * D, [[0, 128], [1, D]]), ls)
                    P.wait("vector", ls, lsc)
                    stt = P.op("vector", lambda E, buf=buf, scb=scb: E.scalar_tensor_tensor(
                        W1S(buf)[:, :], W1S(buf)[:, :], -128.0, SCB(scb)[:, :],
                        ALU.add, ALU.mult), "v")
                    G[("scb", scb)] = stt
                    P.wait("sync", "v", stt)
                    wbk = P.dma("sync", w1c.ap()[m * 128:(m + 1) * 128, :],
                                W1S(buf)[:, :], f"wbq{buf}")
                    G[("wbq", buf)] = wbk
                    w_ready = ("v", stt)
                elif first and q == 0:
                    if m + 2 < MC:
                        conv1(m + 2)
                    if m + 4 < MC:
                        dma1(m + 4)
                    w_ready = w1rdy[m]
                else:
                    g = G.get(("w1s", buf))
                    if g is not None:
                        P.wait("sync", "t", g)
                    if first and q == 1 and m < 4:
                        for wb in range(4):
                            P.wait("sync", f"wbq{wb}", G.get(("wbq", wb), 0))
                    lw = P.dma("sync", W1S(buf)[:, :],
                               w1c.ap()[m * 128:(m + 1) * 128, :], ls)
                    w_ready = (ls, lw)
                pb = pB0 if m % 2 == 0 else pB1
                pb_wait(m % 2)
                P.wait("tensor", w_ready[0], w_ready[1])
                P.wait("tensor", "v", h_v)
                P.wait("tensor", "s", h_s)
                tmk = None
                for k in range(KC):
                    tmk = P.op("tensor", lambda E, pb=pb, buf=buf, k=k, q=q: E.matmul(
                        pb[:, 0:512], W1S(buf)[:, k * 128:(k + 1) * 128], Hv(k, q),
                        start=(k == 0), stop=(k == KC - 1)), "t")
                G[("w1s", buf)] = tmk
                P.wait("scalar", "t", tmk)
                smk = P.op("scalar", lambda E, pb=pb, m=m: E.activation(
                    out=Av(m), in_=pb[:, 0:512], func=AF.Silu,
                    bias=sb1t[:, m:m + 1], scale=1.0), "s")
                pb_used(m % 2, s=smk)
            if first and q == 0 and not _old1:
                wb1(MC - 2)
                wb1(MC - 1)
            G[("dst_t",)] = P.cnt["t"]
            a_ready_s = P.cnt["s"]
            pq_v = h_marks[q][0]
            P.wait("tensor", "v", pq_v)
            P.wait("tensor", "v", G.get(("pmc_rd_v"), 0))
            P.op("tensor", lambda E, q=q: E.matmul(pM[:, :], ones_r[:, :], PQr[q][0], start=True, stop=True), "t")
            pqb = P.op("tensor", lambda E, q=q: E.matmul(pC[:, :], ones_r[:, :], PQr[q][1], start=True, stop=True), "t")
            G[("pq_t", q)] = pqb
            G[("pmc_rd_v")] = 0
            G[("pmc_rd_s")] = 0

            def stage2(o, h):
                for b in range(4):
                    j = h * 4 + b
                    qq = "sync" if b < 2 else "scalar"
                    if b in (0, 2):
                        cvp = G.get(("f2s", h))
                        if cvp is not None:
                            P.wait(qq, "s", cvp)
                    P.dma(qq, ST2(j),
                          q2.ap()[o * 128:(o + 1) * 128,
                                  h * 2048 + b * 512: h * 2048 + (b + 1) * 512],
                          f"f2{h}")
                G[("f2ld", h)] = P.cnt[f"f2{h}"]

            if first and q == 0 and not _old2:
                f2rdy = {}

                def wb2(o):
                    buf2 = o % 2
                    qw = "sync" if o % 2 == 0 else "scalar"
                    P.wait(qw, "v", f2rdy[o][1])
                    G[("wbo", buf2)] = P.dma(
                        qw, w2c.ap()[o * 128:(o + 1) * 128, :],
                        W2S(buf2)[:, :], f"wbo{buf2}")

                def prep2(o):
                    buf2 = o % 2
                    if o - 1 >= 0:
                        wb2(o - 1)
                    for h in range(2):
                        if o > 0:
                            P.wait("gpsimd", "v", G[("sc2v", h)])
                        G[("sc2ld", h)] = P.dma(
                            "gpsimd", SC2B[:, h * 2048:(h + 1) * 2048],
                            bass.AP(s2, o * F4 + h * 2048,
                                    [[0, 128], [1, 2048]]), "sc2l")
                    vms = []
                    for h in range(2):
                        P.wait("scalar", f"f2{h}", G[("f2ld", h)])
                        if h == 0:
                            P.wait("scalar", "t", G.get(("w2s", buf2), 0))
                            P.wait("scalar", f"wbo{buf2}", G.get(("wbo", buf2), 0))
                        cm = None
                        for b in range(4):
                            j = h * 4 + b
                            cm = P.op("scalar", lambda E, buf2=buf2, h=h, j=j:
                                      E.activation(
                                out=W2S(buf2)[:, (j - h * 4) * 512 + h * 2048:
                                              (j - h * 4) * 512 + h * 2048 + 512],
                                in_=ST2(j), func=AF.Identity,
                                bias=n128[:, 0:1], scale=1.0), "s")
                        G[("f2s", h)] = cm
                        P.wait("vector", "sc2l", G[("sc2ld", h)])
                        P.wait("vector", "s", cm)
                        vm = P.op("vector", lambda E, buf2=buf2, h=h: E.tensor_mul(
                            W2S(buf2)[:, h * 2048:(h + 1) * 2048],
                            W2S(buf2)[:, h * 2048:(h + 1) * 2048],
                            SC2B[:, h * 2048:(h + 1) * 2048]), "v")
                        G[("sc2v", h)] = vm
                        vms.append(vm)
                        if o + 1 < KC:
                            stage2(o + 1, h)
                    f2rdy[o] = vms

                stage2(0, 0)
                stage2(0, 1)
                prep2(0)
            for o in range(KC):
                buf = o % 2
                ls = "w2s0" if buf == 0 else "w2s1"
                w_ready2 = None
                if first and q == 0 and _old2:
                    P.wait("gpsimd", "t", G.get(("w2s", buf), 0))
                    P.wait("gpsimd", f"wbo{buf}", G.get(("wbo", buf), 0))
                    P.wait("gpsimd", "v", G.get(("sc2",), 0))
                    P.dma("gpsimd", W2S(buf)[:, :],
                          q2.ap()[o * 128:(o + 1) * 128, :], ls)
                    lsc = P.dma("gpsimd", SC2B[:, :],
                                bass.AP(s2, o * F4, [[0, 128], [1, F4]]), ls)
                    P.wait("vector", ls, lsc)
                    stt = P.op("vector", lambda E, buf=buf: E.scalar_tensor_tensor(
                        W2S(buf)[:, :], W2S(buf)[:, :], -128.0, SC2B[:, :],
                        ALU.add, ALU.mult), "v")
                    G[("sc2",)] = stt
                    P.wait("sync", "v", stt)
                    wbk = P.dma("sync", w2c.ap()[o * 128:(o + 1) * 128, :],
                                W2S(buf)[:, :], f"wbo{buf}")
                    G[("wbo", buf)] = wbk
                    w_ready = ("v", stt)
                elif first and q == 0:
                    if o + 1 < KC:
                        prep2(o + 1)
                    vms = f2rdy[o]
                    w_ready = ("v", vms[0])
                    w_ready2 = ("v", vms[1])
                else:
                    g = G.get(("w2s", buf))
                    if g is not None:
                        P.wait("sync", "t", g)
                    if first and q == 1 and o < 2:
                        for wb in range(2):
                            P.wait("sync", f"wbo{wb}", G.get(("wbo", wb), 0))
                    lw = P.dma("sync", W2S(buf)[:, :],
                               w2c.ap()[o * 128:(o + 1) * 128, :], ls)
                    w_ready = (ls, lw)
                pb = pB0 if buf == 0 else pB1
                pb_wait(buf)
                P.wait("tensor", w_ready[0], w_ready[1])
                P.wait("tensor", "s", a_ready_s)
                tmk = None
                for k in range(MC):
                    if k == 16 and w_ready2 is not None:
                        P.wait("tensor", w_ready2[0], w_ready2[1])
                    tmk = P.op("tensor", lambda E, pb=pb, buf=buf, k=k: E.matmul(
                        pb[:, 0:512], W2S(buf)[:, k * 128:(k + 1) * 128], Av(k),
                        start=(k == 0), stop=(k == MC - 1)), "t")
                G[("w2s", buf)] = tmk
                zg = G.get(("z4_v"))
                if zg is not None:
                    P.wait("scalar", "v", zg)
                P.wait("scalar", "t", tmk)
                zs = P.op("scalar", lambda E, pb=pb, o=o: E.activation(
                    out=z4t[:, :], in_=pb[:, 0:512], func=AF.Identity,
                    bias=sqb2[:, o:o + 1], scale=0.25), "s")
                pb_used(buf, s=zs)
                P.wait("vector", "t", pqb)
                P.wait("vector", "s", zs)
                P.op("vector", lambda E, o=o, t0=t0: E.tensor_mul(
                    Xv(o, t0, QT), Xv(o, t0, QT), pM[:, :]), "v")
                zv = P.op("vector", lambda E, o=o, t0=t0: E.tensor_add(
                    Xv(o, t0, QT), Xv(o, t0, QT), z4t[:, :]), "v")
                G[("z4_v")] = zv
                vmk = P.op("vector", lambda E, o=o, t0=t0: E.tensor_add(
                    Xv(o, t0, QT), Xv(o, t0, QT), pC[:, :]), "v")
                G[("pmc_rd_v")] = vmk
                if o == 1 and q + 1 < NQ and h_marks[q + 1] is None:
                    lnq(q + 1)
                if not first:
                    P.wait("gpsimd", "v", vmk)
                    P.dma("gpsimd", out_fm.ap()[o * 128:(o + 1) * 128, t0:t0 + QT],
                          Xv(o, t0, QT), "gd")
            if first and q == 0 and not _old2:
                wb2(KC - 1)
            if post_q is not None:
                post_q(q)

    y1_marks = [None] * NQ

    def lnq_a(tc):
        y1_marks[tc] = emit_ln_quarter(
            tc * QT, "attn", lambda k, tc=tc: Y1(k, tc * QT, QT))

    def emit_attn():
        ffn1_t = P.cnt["t"]
        for tc in range(NQ):
            if y1_marks[tc] is None:
                lnq_a(tc)

        P.wait("gpsimd", "t", ffn1_t)
        for wb in range(4):
            P.wait("gpsimd", f"wbq{wb}", None)
        for wb in range(2):
            P.wait("gpsimd", f"wbo{wb}", None)
        lk = P.dma("gpsimd", PART(0)[:, :].rearrange("p (k n) -> p k n", k=KC),
                   wqkv.ap()[:, 1024:2048].rearrange("(k p) n -> p k n", p=128), "p0")
        lv = P.dma("gpsimd", PART(1)[:, :].rearrange("p (k n) -> p k n", k=KC),
                   wqkv.ap()[:, 2048:3072].rearrange("(k p) n -> p k n", p=128), "p1")

        def qkv_mms(part_slot, tg, pbank, pmark):
            mvv, mss = y1_marks[tg // 4]
            P.wait("tensor", "v", mvv)
            P.wait("tensor", pmark[0], pmark[1])
            tmk = None
            for h in range(2):
                for k in range(KC):
                    tmk = P.op("tensor", lambda E, pbank=pbank, k=k, h=h, tg=tg,
                               ps=part_slot: E.matmul(
                        pbank[:, h * 512:(h + 1) * 512],
                        Y1(k, (tg // 4) * QT + (tg % 4) * 128, 128),
                        PART(ps)[:, k * 1024 + h * 512: k * 1024 + (h + 1) * 512],
                        start=(k == 0), stop=(k == KC - 1)), "t")
            return tmk

        def rope_elu(tg, pbank, scale_q):
            sc = P.op("scalar", lambda E, pb=pbank: E.activation(
                out=rtile, in_=pb[:, :], func=AF.Identity), "s")
            P.wait("vector", "s", sc)
            P.op("vector", lambda E, tg=tg: E.tensor_tensor(
                rcos.rearrange("p (g j) -> p g j", g=16),
                rtile.rearrange("p (g j) -> p g j", g=16),
                bass.AP(tabc, tg * 64, [[NTG * 64, 128], [0, 16], [1, 64]]),
                ALU.mult), "v")
            P.op("vector", lambda E, tg=tg: E.tensor_tensor(
                bass.AP(ovl, 2048, [[8192, 128], [64, 16], [1, 32]]),
                bass.AP(ovl, 32, [[8192, 128], [64, 16], [1, 32]]),
                bass.AP(tabs, tg * 64, [[NTG * 64, 128], [0, 16], [1, 32]]),
                ALU.mult), "v")
            P.op("vector", lambda E, tg=tg: E.tensor_tensor(
                bass.AP(ovl, 2048 + 32, [[8192, 128], [64, 16], [1, 32]]),
                bass.AP(ovl, 0, [[8192, 128], [64, 16], [1, 32]]),
                bass.AP(tabs, tg * 64 + 32, [[NTG * 64, 128], [0, 16], [1, 32]]),
                ALU.mult), "v")
            P.op("vector", lambda E: E.tensor_add(rtile, rcos, rsin), "v")
            s = QSCALE if scale_q else 1.0
            mv = P.op("vector", lambda E, s=s: E.tensor_scalar(
                rcos, rtile, s, 0.0, ALU.mult, ALU.min), "v")
            P.wait("scalar", "v", mv)
            ev = P.op("scalar", lambda E: E.activation(out=rsin, in_=rcos, func=AF.Exp), "s")
            P.wait("vector", "s", ev)
            if scale_q:
                P.op("vector", lambda E, s=s: E.tensor_scalar(
                    rcos, rtile, s, 0.0, ALU.mult, ALU.max), "v")
                fv = P.op("vector", lambda E: E.tensor_add(rtile, rcos, rsin), "v")
            else:
                fv = P.op("vector", lambda E: E.scalar_tensor_tensor(
                    rtile, rtile, 0.0, rsin, ALU.max, ALU.add), "v")
            return fv

        kv_mm = None
        kv_pend = None

        def flush_kv():
            nonlocal kv_mm, kv_pend
            if kv_pend is None:
                return
            tg0, kcp0, vv0 = kv_pend
            P.wait("tensor", "v", kcp0)
            P.wait("tensor", "s", vv0)
            for p in range(NP):
                pdst = pM if p < 4 else pC
                pcol = (p % 4) * 128
                P.op("tensor", lambda E, p=p, pdst=pdst, pcol=pcol, tg0=tg0: E.matmul(
                    pdst[:, pcol:pcol + 128], kft[0][:, p * 128:(p + 1) * 128],
                    vt[0][:, p * 128:(p + 1) * 128],
                    start=(tg0 == 0 and p % 4 == 0), stop=(tg0 == NTG - 1),
                    skip_group_check=True), "t")
                kv_mm = P.op("tensor", lambda E, p=p, tg0=tg0: E.matmul(
                    pSS[:, p:p + 1], kft[0][:, p * 128:(p + 1) * 128], ones_c[:, :],
                    start=(tg0 == 0 and p == 0), stop=(tg0 == NTG - 1),
                    skip_group_check=True), "t")
            G[("kft", 0)] = kv_mm
            kv_pend = None

        for tg in range(NTG):
            pb_wait(0)
            tmk = qkv_mms(0, tg, pB0, ("p0", lk))
            G[("k_done",)] = tmk
            pb_wait(1)
            tmv = qkv_mms(1, tg, pB1, ("p1", lv))
            flush_kv()
            P.wait("scalar", "t", tmk)
            P.wait("scalar", "v", G.get(("kcp",), 0))
            fv = rope_elu(tg, pB0, False)
            pb_used(0, s=P.cnt["s"])
            kg = G.get(("kft", 0))
            if kg is not None:
                P.wait("vector", "t", kg)
            kcp = P.op("vector", lambda E: E.tensor_copy(kft[0], rtile), "v")
            G[("kcp",)] = kcp
            P.wait("scalar", "t", tmv)
            P.wait("scalar", "t", G.get(("kft", 0), 0))
            vv = P.op("scalar", lambda E: E.activation(
                out=vt[0], in_=pB1[:, :], func=AF.Identity), "s")
            pb_used(1, s=vv)
            kv_pend = (tg, kcp, vv)
            if True:
                pass
        flush_kv()

        P.wait("vector", "t", None)
        P.op("vector", lambda E: E.tensor_copy(kvx[:, 0:512], pM[:, :]), "v")
        P.op("vector", lambda E: E.tensor_copy(kvx[:, 512:1024], pC[:, :]), "v")
        pk = P.op("vector", lambda E: E.tensor_copy(kvx[:, 1024:1032], pSS[:, 0:8]), "v")
        P.wait("gpsimd", "v", pk)
        P.dma("gpsimd", cc_in.ap(), kvx[:, :], "gd")
        P.wait("gpsimd", "gd", None)
        P.op("gpsimd", lambda E: E.collective_compute(
            "AllGather", ALU.bypass,
            replica_groups=[list(range(NCORES))],
            ins=[cc_in.ap().opt()], outs=[cc_out.ap().opt()]), "cc")

        P.wait("gpsimd", "t", G[("k_done",)])
        lq = P.dma("gpsimd", PART(0)[:, :].rearrange("p (k n) -> p k n", k=KC),
                   wqkv.ap()[:, 0:1024].rearrange("(k p) n -> p k n", p=128), "p0")

        tr_guard = None
        pend = None

        def flush_tr():
            nonlocal tr_guard, pend
            if pend is None:
                return
            tg0, fv0 = pend
            P.wait("tensor", "v", fv0)
            if tr_guard is not None:
                P.wait("tensor", "s", tr_guard)
            ttk = None
            for p in range(NP):
                ttk = P.op("tensor", lambda E, p=p: E.transpose(
                    pTR[:, p * 128:(p + 1) * 128], rtile[:, p * 128:(p + 1) * 128],
                    ident[:, :]), "t")
            P.wait("scalar", "t", ttk)
            tr_guard = P.op("scalar", lambda E, tg0=tg0: E.activation(
                out=bass.AP(arena, tg0 * 128, [[32768, 128], [TL, NP], [1, 128]]),
                in_=pTR[:, :].rearrange("p (g j) -> p g j", g=NP),
                func=AF.Identity), "s")
            pend = None

        for tg in range(NTG):
            pb = pB0 if tg % 2 == 0 else pB1
            pb_wait(tg % 2)
            tmk = qkv_mms(0, tg, pb, ("p0", lq))
            flush_tr()
            P.wait("scalar", "t", tmk)
            P.wait("scalar", "s", tr_guard or 0)
            fv = rope_elu(tg, pb, True)
            pb_used(tg % 2, s=P.cnt["s"])
            pend = (tg, fv)
        flush_tr()
        q_pass_t = P.cnt["t"]

        lk = P.dma("gpsimd", PART(1)[:, :].rearrange("p (k n) -> p k n", k=KC),
                   wqkv.ap()[:, 1024:2048].rearrange("(k p) n -> p k n", p=128), "p1")
        P.wait("gpsimd", "t", q_pass_t)
        lv = P.dma("gpsimd", PART(0)[:, :].rearrange("p (k n) -> p k n", k=KC),
                   wqkv.ap()[:, 2048:3072].rearrange("(k p) n -> p k n", p=128), "p0")

        P.wait("gpsimd", "cc", None)
        a0 = P.dma("gpsimd", kvx[:, :], cc_out.ap()[0:128, :], "gd")
        P.wait("gpsimd", "gd", None)
        marks = {}
        marks[1] = P.dma("gpsimd", kvt[1][:, :], cc_out.ap()[128:256, :], "cc3")
        marks[2] = P.dma("gpsimd", kvt[0][:, :], cc_out.ap()[256:384, :], "cc2")
        gsum = None
        for c in range(1, 8):
            sem = "cc3" if c % 2 == 1 else "cc2"
            P.wait("gpsimd", sem, marks[c])
            gsum = P.op("gpsimd", lambda E, c=c: E.tensor_add(
                kvx[:, :], kvx[:, :], kvt[c % 2][:, :]), "gp")
            if c + 2 < 8:
                nsem = "cc3" if (c + 2) % 2 == 1 else "cc2"
                marks[c + 2] = P.dma("gpsimd", kvt[c % 2][:, :],
                                     cc_out.ap()[(c + 2) * 128:(c + 3) * 128, :], nsem)

        P.wait("vector", "gp", gsum)
        P.op("vector", lambda E: E.memset(ksb[:, :], 0.0), "v")
        for p in range(NP):
            P.op("vector", lambda E, p=p: E.tensor_copy(
                ksb[0:64, p * 16 + 2 * p: p * 16 + 2 * p + 1],
                kvx[0:64, 1024 + p:1025 + p]), "v")
            P.op("vector", lambda E, p=p: E.tensor_copy(
                ksb[64:128, p * 16 + 2 * p + 1: p * 16 + 2 * p + 2],
                kvx[64:128, 1024 + p:1025 + p]), "v")
        kv_ready_v = P.cnt["v"]

        G[("qsc_v",)] = kv_ready_v
        den_s = None
        for tc in range(NQ):
            P.wait("tensor", "v", kv_ready_v)
            if den_s is not None:
                P.wait("tensor", "v", G[("dmax_v",)])
            dmk = None
            for p in range(NP):
                dmk = P.op("tensor", lambda E, p=p, tc=tc: E.matmul(
                    pSS[0:16, :], ksb[:, p * 16:(p + 1) * 16], QF(p, tc * QT, QT),
                    start=(p == 0), stop=(p == NP - 1)), "t")
            P.wait("vector", "t", dmk)
            P.wait("vector", "s", G.get(("dens",), 0))
            dvv = P.op("vector", lambda E: E.tensor_scalar_max(dmax, pSS[0:16, :], 1e-6), "v")
            G[("dmax_v",)] = dvv
            P.wait("scalar", "v", dvv)
            P.op("scalar", lambda E: E.activation(out=dmax, in_=dmax, func=AF.Ln), "s")
            den_s = P.op("scalar", lambda E: E.activation(
                out=denr, in_=dmax, func=AF.Exp, scale=-1.0), "s")
            G[("dens",)] = den_s
            for p in range(NP):
                i = (tc * NP + p) % 2
                P.wait("tensor", "s", den_s)
                P.wait("tensor", "v", G.get(("qsc_v",), 0))
                bmk = P.op("tensor", lambda E, p=p: E.matmul(
                    pM[:, :], sel[:, p * 128:(p + 1) * 128],
                    denr, start=True, stop=True), "t")
                P.wait("vector", "t", bmk)
                qv = P.op("vector", lambda E, p=p, tc=tc: E.tensor_mul(
                    QF(p, tc * QT, QT), QF(p, tc * QT, QT), pM[:, :]), "v")
                G[("qsc_v",)] = qv
                pb = pB0 if i == 0 else pB1
                pb_wait(i)
                P.wait("tensor", "v", qv)
                P.op("tensor", lambda E, p=p, tc=tc, pb=pb: E.matmul(
                    pb[0:64, 0:512], kvx[0:64, p * 128:p * 128 + 64],
                    bass.AP(arena, p * TL + tc * QT, [[32768, 64], [1, QT]]),
                    start=True, stop=True, tile_position=(0, 0)), "t")
                amk = P.op("tensor", lambda E, p=p, tc=tc, pb=pb: E.matmul(
                    pb[64:128, 0:512], kvx[64:128, p * 128 + 64:(p + 1) * 128],
                    bass.AP(arena, 64 * 32768 + p * TL + tc * QT, [[32768, 64], [1, QT]]),
                    start=True, stop=True, tile_position=(64, 64)), "t")
                P.wait("scalar", "t", amk)
                smk = P.op("scalar", lambda E, p=p, tc=tc, pb=pb: E.activation(
                    out=Y1(p, tc * QT, QT), in_=pb[:, 0:512], func=AF.Identity), "s")
                pb_used(i, s=smk)
        at_done_s = P.cnt["s"]

        wo_t = None
        for o in range(KC):
            buf = o % 2
            g = G.get(("wos", buf))
            if g is not None:
                P.wait("gpsimd", "t", g)
            if o < 2:
                P.wait("gpsimd", "t", q_pass_t)
            ls = "wos0" if buf == 0 else "wos1"
            lw = P.dma("gpsimd", WOS(buf)[:, :].rearrange("p (k j) -> p k j", k=KC),
                       wout.ap()[:, o * 128:(o + 1) * 128].rearrange("(k p) j -> p k j", p=128),
                       ls)
            for tc in range(NQ):
                i = (o * NQ + tc) % 2
                pb = pB0 if i == 0 else pB1
                pb_wait(i)
                P.wait("tensor", ls, lw)
                P.wait("tensor", "s", at_done_s)
                tmk = None
                for k in range(KC):
                    tmk = P.op("tensor", lambda E, o=o, tc=tc, k=k, pb=pb, buf=buf: E.matmul(
                        pb[:, 0:512], WOS(buf)[:, k * 128:(k + 1) * 128],
                        Y1(k, tc * QT, QT),
                        start=(k == 0), stop=(k == KC - 1)), "t")
                G[("wos", buf)] = tmk
                wo_t = tmk
                P.wait("vector", "t", tmk)
                vmk = P.op("vector", lambda E, o=o, tc=tc, pb=pb: E.tensor_add(
                    Xv(o, tc * QT, QT), Xv(o, tc * QT, QT), pb[:, 0:512]), "v")
                pb_used(i, v=vmk)

        G[("mc_rd_v")] = P.cnt["v"]
        G[("mc_rd_s")] = P.cnt["s"]
        G[("dst_t",)] = P.cnt["t"]
        G[("pmc_rd_v")] = P.cnt["v"]
        G[("pmc_rd_s")] = P.cnt["s"]
        G[("pss_rd_v")] = P.cnt["v"]
        for bb in range(4):
            G[("w1s", bb)] = wo_t
        G[("w2s", 0)] = wo_t
        G[("w2s", 1)] = wo_t
        G[("hq_gate",)] = wo_t

    import os as _os
    _phases = int(_os.environ.get("MK_PHASES", "3"))

    def dump_X():
        vnow = P.cnt["v"]
        P.wait("gpsimd", "v", vnow)
        for o in range(KC):
            P.dma("gpsimd", out_fm.ap()[o * 128:(o + 1) * 128, :], Xv(o, 0, TL), "gd")

    emit_ffn(first=True, post_q=lambda q: lnq_a(q))
    if _phases == 1:
        dump_X()
    if _phases >= 2:
        emit_attn()
        if _phases == 2:
            dump_X()
    if _phases >= 2:
        P.wait("vector", "t", None)
        P.wait("scalar", "t", None)
        emit_ffn(first=False)


    P.wait("gpsimd", "gd", None)
    P.wait("sync", "gd", P.cnt["gd"])
    for s in ("c0", "x0", "w1s0", "w1s1", "w1s2", "w1s3", "w2s0", "w2s1",
              "wbq0", "wbq1", "wbq2", "wbq3", "wbo0", "wbo1",
              "p0", "p1", "wos0", "wos1", "cc2",
              "qi0", "qi1", "qi2", "qi3", "f20", "f21", "sc2l"):
        P.wait("sync", s, None)

    with es:
        for s in list(P.cnt):
            P.handles[s] = es.enter_context(nc.semaphore(s))
        blk = es.enter_context(nc.Block())
        for eng in ENGS:
            stream = P.streams[eng]
            def body(E, stream=stream):
                for fn in stream:
                    fn(E)
            getattr(blk, eng)(body)
    return nc


def _rope_tables(core_id):
    pos = np.arange(core_id * TL, (core_id + 1) * TL, dtype=np.float64)
    inv = ROPE_BASE ** (-np.arange(0, 64, 2, dtype=np.float64) / 64.0)
    fr = pos[:, None] * inv[None, :]
    emb = np.concatenate([fr, fr], axis=1)
    cos = np.cos(emb).astype(np.float32)
    sin = np.sin(emb).astype(np.float32)
    ss = sin.copy()
    ss[:, 0:32] *= -1.0
    to_sb = lambda a: np.ascontiguousarray(
        a.reshape(NTG, 128, 64).transpose(1, 0, 2).reshape(128, NTG * 64)
    ).astype(ml_dtypes.bfloat16)
    return to_sb(cos), to_sb(ss)


def kernel(L0_attn_x, L0_attn_W_qkv, L0_attn_W_out, L0_ffn_ln_gamma,
           L0_ffn_ln_beta, L0_ffn_fc1_q, L0_ffn_fc1_scales, L0_ffn_fc1_bias,
           L0_ffn_fc2_q, L0_ffn_fc2_scales, L0_ffn_fc2_bias):
    x = np.ascontiguousarray(np.asarray(L0_attn_x, dtype=np.float32)[0].T)
    wqkv = np.ascontiguousarray(np.asarray(L0_attn_W_qkv, dtype=np.float32))
    wo = np.ascontiguousarray(np.asarray(L0_attn_W_out, dtype=np.float32))
    gamT = np.ascontiguousarray(np.asarray(L0_ffn_ln_gamma, np.float32).reshape(KC, 128).T)
    betT = np.ascontiguousarray(np.asarray(L0_ffn_ln_beta, np.float32).reshape(KC, 128).T)
    b1T = np.ascontiguousarray(np.asarray(L0_ffn_fc1_bias, np.float32).reshape(MC, 128).T)
    b2T = np.ascontiguousarray(np.asarray(L0_ffn_fc2_bias, np.float32).reshape(KC, 128).T)
    q1 = np.ascontiguousarray(np.asarray(L0_ffn_fc1_q, np.int32)
                              .reshape(KC, 128, MC, 128).transpose(2, 1, 0, 3)
                              .reshape(F4, D))
    s1 = np.ascontiguousarray(np.asarray(L0_ffn_fc1_scales, np.float32)
                              .reshape(KC, MC, 128).transpose(1, 0, 2).reshape(MC, D))
    q2 = np.ascontiguousarray(np.asarray(L0_ffn_fc2_q, np.int32)
                              .reshape(MC, 128, KC, 128).transpose(2, 1, 0, 3)
                              .reshape(D, F4))
    s2 = np.ascontiguousarray(np.asarray(L0_ffn_fc2_scales, np.float32)
                              .reshape(MC, KC, 128).transpose(1, 0, 2).reshape(KC, F4))
    iden = np.eye(128, dtype=ml_dtypes.bfloat16)
    selm = np.zeros((16, 1024), dtype=ml_dtypes.bfloat16)
    for p in range(NP):
        selm[2 * p, p * 128:p * 128 + 64] = 1
        selm[2 * p + 1, p * 128 + 64:(p + 1) * 128] = 1

    nc = build_program()
    in_maps = []
    for c in range(NCORES):
        tc, ts = _rope_tables(c)
        in_maps.append(dict(
            x_fm=np.ascontiguousarray(x[:, c * TL:(c + 1) * TL]),
            wqkv=wqkv, wout=wo, gamT=gamT, betT=betT, b1T=b1T, b2T=b2T,
            q1=q1, s1=s1, q2=q2, s2=s2, ropc=tc, rops=ts, identin=iden, selin=selm,
        ))
    res = run_bass_kernel_spmd(nc, in_maps, core_ids=list(range(NCORES)))
    full = np.concatenate([r["out_fm"] for r in res.results], axis=1)
    return np.ascontiguousarray(full.T)[None, :, :]
